# version 04
# baseline (speedup 1.0000x reference)
import numpy as np
import concourse.bass as bass
import concourse.mybir as mybir
from concourse.bass_utils import run_bass_kernel_spmd

F32 = mybir.dt.float32
BF16 = mybir.dt.bfloat16
AF = mybir.ActivationFunctionType
OP = mybir.AluOpType
AX = mybir.AxisListType

NCORES = 8


ENGINES = ("sync", "gpsimd", "tensor", "vector", "scalar")
_W = {"sync": 16, "gpsimd": 16, "tensor": 1, "vector": 1, "scalar": 1}


class Prog:

    def __init__(self, nc):
        self.nc = nc
        self.ops = []
        self.eng_ops = {e: [] for e in ENGINES}
        self._writers = {}
        self._readers = {}
        self._keys = {}
        self.inst_tags = {}
        self._gq_count = 0

    @staticmethod
    def _base(r):
        return r[0] if isinstance(r, tuple) else r

    def _alias(self, r):
        base = self._base(r)
        seen = self._keys.setdefault(base, set())
        seen.add(r)
        if r == base:
            return list(seen)
        out = [r]
        if base in seen:
            out.append(base)
        return out

    @staticmethod
    def _norm(r):
        return r[0] if isinstance(r, tuple) and len(r) == 1 else r

    def op(self, engine, fn, reads=(), writes=(), tag="", kind="compute"):
        assert engine in ENGINES, engine
        if engine == "sync":
            kind = "dma"
        reads = [self._norm(r) for r in reads]
        writes = [self._norm(r) for r in writes]
        i = len(self.ops)
        deps = set()
        for r in reads:
            for k in self._alias(r):
                w = self._writers.get(k)
                if w is not None:
                    deps.add(w)
        wr_aliases = []
        for r in writes:
            aks = self._alias(r)
            wr_aliases.append((r, aks))
            for k in aks:
                w = self._writers.get(k)
                if w is not None:
                    deps.add(w)
                for rd in self._readers.get(k, ()):
                    deps.add(rd)
        deps.discard(i)
        for r in reads:
            self._readers.setdefault(r, []).append(i)
        for r, aks in wr_aliases:
            if r == self._base(r):
                for k in list(self._keys.get(r, ())):
                    self._writers[k] = i
                    self._readers[k] = []
            self._writers[r] = i
            self._readers[r] = []
        gq = self._gq_count if (engine == "gpsimd" and kind == "dma") else -1
        wm = -1
        for d in deps:
            o = self.ops[d]
            wm = max(wm, o[5] if o[6] < 0 else o[6])
        self.ops.append([engine, fn, deps, tag, kind, wm, gq])
        if gq >= 0:
            self._gq_count += 1
        self.eng_ops[engine].append(i)
        return i

    def barrier(self, tag="barrier"):
        lasts = [self.eng_ops[f][-1] for f in ENGINES if self.eng_ops[f]]
        for e in ENGINES:
            i = len(self.ops)
            deps = set(lasts)
            deps.discard(i)
            wm = -1
            for d in deps:
                o = self.ops[d]
                wm = max(wm, o[5] if o[6] < 0 else o[6])
            self.ops.append([e, (lambda eng: eng.nop()), deps, tag, "compute", wm, -1])
            self.eng_ops[e].append(i)

    def fence(self, engine="sync", tag="final_fence"):
        i = len(self.ops)
        deps = set()
        for f in ENGINES:
            if self.eng_ops[f]:
                deps.add(self.eng_ops[f][-1])
        deps.discard(i)
        wm = -1
        for d in deps:
            o = self.ops[d]
            wm = max(wm, o[5] if o[6] < 0 else o[6])
        self.ops.append([engine, (lambda eng: eng.nop()), deps, tag, "compute", wm, -1])
        self.eng_ops[engine].append(i)
        return i

    def emit(self, dma_depth=8):
        nc = self.nc
        n = len(self.ops)
        sem_names = [f"se_{e}" for e in ENGINES if e != "sync"]
        sem_names += [f"sq_sync{j}" for j in range(dma_depth)]
        sem_names += ["gq_mono"]
        cms, sems = [], {}
        for nm in sem_names:
            cm = nc.semaphore(nm)
            sems[nm] = cm.__enter__()
            cms.append(cm)
        comp = [None] * n
        throttle = [None] * n
        etick = {e: 0 for e in ENGINES}
        scnt = 0
        for i, (e, _, deps, _, kind, wm, gq) in enumerate(self.ops):
            if e == "sync":
                k = scnt
                scnt += 1
                snm = f"sq_sync{k % dma_depth}"
                comp[i] = (snm, 16 * (k // dma_depth + 1))
                if k >= dma_depth:
                    throttle[i] = (snm, 16 * (k // dma_depth))
            elif gq >= 0:
                comp[i] = ("gq_mono", 16 * (gq + 1))
                if gq > 0:
                    throttle[i] = ("gq_mono", 16 * gq)
            else:
                etick[e] += 1
                comp[i] = (f"se_{e}", etick[e])
        blk_cm = nc.Block()
        block = blk_cm.__enter__()
        prog = self

        def body_for(e):
            def body(eng):
                seen = {nm: 0 for nm in sem_names}
                for i in prog.eng_ops[e]:
                    _, fn, deps, tag, kind, wm, gq = prog.ops[i]
                    need = {}
                    for d in deps:
                        snm, v = comp[d]
                        if v > need.get(snm, 0):
                            need[snm] = v
                    if throttle[i] is not None:
                        snm, v = throttle[i]
                        if v > need.get(snm, 0):
                            need[snm] = v
                    for snm in sem_names:
                        v = need.get(snm, 0)
                        if v > seen[snm]:
                            eng.wait_ge(sems[snm], v)
                            seen[snm] = v
                    try:
                        inst = fn(eng)
                    except Exception as ex:
                        raise RuntimeError(f"emit failed at op tag={tag} engine={e}: {ex}") from ex
                    assert inst is not None, f"op fn returned None: {e} {tag}"
                    try:
                        prog.inst_tags[inst.ins.name] = (tag, i)
                    except Exception:
                        pass
                    snm, v = comp[i]
                    inst.then_inc(sems[snm], 16 if (e == "sync" or gq >= 0) else 1)
            return body

        for e in ENGINES:
            getattr(block, e)(body_for(e))
        blk_cm.__exit__(None, None, None)
        for cm in cms:
            cm.__exit__(None, None, None)



def ap3(t, off, dims):
    return bass.AP(t.ap().tensor, off, [list(d) for d in dims])


class Ctx:

    def __init__(self, nc, D=1024, T=2048, B=2):
        self.nc = nc
        self.p = Prog(nc)
        self.D, self.T, self.B = D, T, B
        self.NT = D // 128
        self.NN = T // 512
        self.bufs = {}
        self._uid = 0

    def uid(self):
        self._uid += 1
        return self._uid

    def sb(self, name, shape, dtype):
        if getattr(self, "_scopes", None):
            g = self.nc.sbuf_tensor(name, shape, dtype)
            t = g.__enter__()
            self._scopes[-1].append(g)
        else:
            t = self.nc.alloc_sbuf_tensor(name, shape, dtype)
        self.bufs[name] = t
        return t

    def ps(self, name, shape, dtype=F32):
        if getattr(self, "_scopes", None):
            g = self.nc.psum_tensor(name, shape, dtype)
            t = g.__enter__()
            self._scopes[-1].append(g)
        else:
            t = self.nc.alloc_psum_tensor(name, shape, dtype)
        self.bufs[name] = t
        return t

    def push_scope(self, npsum=4):
        if not hasattr(self, "_scopes"):
            self._scopes = []
        self._scopes.append([])
        u = self.uid()
        self.psum_pool = [self.ps(f"pp_s{u}_{j}", [128, 512]) for j in range(npsum)]
        self._psum_i = 0

    def pop_scope(self):
        self.p.barrier()
        for g in reversed(self._scopes.pop()):
            g.__exit__(None, None, None)

    def dma(self, q, dst, src, reads, writes, tag="dma"):
        self.p.op(q, (lambda eng, d=dst, s=src: eng.dma_start(out=d, in_=s)),
                  reads=reads, writes=writes, tag=tag, kind="dma")

    def memset(self, dst, val, writes, tag="memset", engine="vector"):
        self.p.op(engine, (lambda eng, d=dst, v=val: eng.memset(d, v)),
                  writes=writes, tag=tag)

    def tt(self, out, in0, in1, op, reads, writes, tag="tt", engine="vector"):
        self.p.op(engine,
                  (lambda eng, o=out, a=in0, b=in1, c=op: eng.tensor_tensor(out=o, in0=a, in1=b, op=c)),
                  reads=reads, writes=writes, tag=tag)

    def ts(self, out, in0, s1, s2, op0, op1, reads, writes, tag="ts", engine="vector"):
        def fn(eng, o=out, a=in0, s1=s1, s2=s2, op0=op0, op1=op1):
            return eng.tensor_scalar(out=o, in0=a, scalar1=s1, scalar2=s2, op0=op0, op1=op1)
        self.p.op(engine, fn, reads=reads, writes=writes, tag=tag)

    def stt(self, out, in0, scal, in1, op0, op1, reads, writes, tag="stt", engine="vector"):
        def fn(eng, o=out, a=in0, s=scal, b=in1, op0=op0, op1=op1):
            return eng.scalar_tensor_tensor(out=o, in0=a, scalar=s, in1=b, op0=op0, op1=op1)
        self.p.op(engine, fn, reads=reads, writes=writes, tag=tag)

    def act(self, out, in_, func, reads, writes, bias=0.0, scale=1.0, tag="act"):
        def fn(eng, o=out, i=in_, f=func, b=bias, s=scale):
            return eng.activation(o, i, f, bias=b, scale=s)
        self.p.op("scalar", fn, reads=reads, writes=writes, tag=tag)

    def copy(self, out, in_, reads, writes, tag="copy", engine="vector"):
        if engine == "scalar":
            fn = lambda eng, o=out, i=in_: eng.copy(o, i)
        else:
            fn = lambda eng, o=out, i=in_: eng.tensor_copy(o, i)
        self.p.op(engine, fn, reads=reads, writes=writes, tag=tag)

    def scan(self, out, data0, data1, initial, op0, op1, reads, writes, tag="scan"):
        def fn(eng, o=out, d0=data0, d1=data1, ini=initial, op0=op0, op1=op1):
            return eng.tensor_tensor_scan(out=o, data0=d0, data1=d1, initial=ini, op0=op0, op1=op1)
        self.p.op("vector", fn, reads=reads, writes=writes, tag=tag)

    def recip(self, out, in_, reads, writes, tag="recip"):
        self.p.op("vector", (lambda eng, o=out, i=in_: eng.reciprocal(o, i)),
                  reads=reads, writes=writes, tag=tag)

    def mm(self, out, lhsT, rhs, reads, writes, start=True, stop=True, tag="mm"):
        def fn(eng, o=out, l=lhsT, r=rhs, st=start, sp=stop):
            return eng.matmul(o, l, r, start=st, stop=sp)
        self.p.op("tensor", fn, reads=reads, writes=writes, tag=tag)

    def mm_group(self, fn, reads, writes, tag="mmg"):
        self.p.op("tensor", fn, reads=reads, writes=writes, tag=tag)

    def transpose(self, out, in_, ident, reads, writes, tag="tr"):
        def fn(eng, o=out, i=in_, d=ident):
            return eng.transpose(o, i, d)
        self.p.op("tensor", fn, reads=reads, writes=writes, tag=tag)



class Plane:

    def __init__(self, t, ncols, npart=128):
        self.t = t
        self.name = t.ap().tensor.name if hasattr(t.ap().tensor, "name") else t.name
        self.ncols = ncols
        self.npart = npart

    def _mk(self, off, dims):
        return bass.AP(self.t.ap().tensor, off, [list(d) for d in dims])

    def sl(self, c0, n, p0=0, np_=None):
        np_ = self.npart if np_ is None else np_
        return self._mk(p0 * self.ncols + c0, [[self.ncols, np_], [1, n]])

    def v3(self, ntile, tlen, stride=None, off=0):
        stride = tlen if stride is None else stride
        return self._mk(off, [[self.ncols, self.npart], [stride, ntile], [1, tlen]])

    def bc3(self, ntile, tlen, off=0):
        return self._mk(off, [[self.ncols, self.npart], [0, ntile], [1, tlen]])

    def reg(self, *key):
        return (self.name,) + key if key else (self.name,)


def mkplane(c, name, ncols, dtype, npart=128):
    return Plane(c.sb(name, [npart, ncols], dtype), ncols, npart)


class Row:

    def __init__(self, pl, base, col0=0):
        self.pl, self.base, self.col0 = pl, base, col0

    def sl(self, c0, n, np_=1):
        return self.pl._mk(self.base * self.pl.ncols + self.col0 + c0,
                           [[self.pl.ncols, np_], [1, n]])

    def reg(self, *k):
        return (self.pl.name, "row", self.base, self.col0) + k



def setup_common(c: Ctx, npsum=4):
    c.ones_col = mkplane(c, "ones_col", 1, BF16)
    c.ones128 = mkplane(c, "ones128", 128, F32)
    c.memset(c.ones_col.sl(0, 1), 1.0, writes=[c.ones_col.reg()], tag="init")
    c.memset(c.ones128.sl(0, 128), 1.0, writes=[c.ones128.reg()], tag="init")
    c.psum_pool = [c.ps(f"pp{j}", [128, 512]) for j in range(npsum)]
    c._psum_i = 0


def need_zero_col(c):
    if not hasattr(c, "zero_col"):
        c.zero_col = mkplane(c, "zero_col", 1, F32)
        c.memset(c.zero_col.sl(0, 1), 0.0, writes=[c.zero_col.reg()], tag="init")
    return c.zero_col


def need_ident32(c):
    if not hasattr(c, "ident32"):
        c.ident32 = mkplane(c, "ident32", 128, F32)
        c.memset(c.ident32.sl(0, 128), 1.0, writes=[c.ident32.reg()], tag="init")
        def fn(eng):
            return eng.affine_select(c.ident32.sl(0, 128), c.ident32.sl(0, 128),
                                     [[-1, 128]], OP.is_equal, 0.0, channel_multiplier=1)
        c.p.op("gpsimd", fn, reads=[c.ident32.reg()], writes=[c.ident32.reg()], tag="ident")
    return c.ident32


def need_ident_bf(c):
    if not hasattr(c, "ident_bf"):
        c.ident_bf = mkplane(c, "ident_bf", 128, BF16)
        c.memset(c.ident_bf.sl(0, 128), 1.0, writes=[c.ident_bf.reg()], tag="init")
        def fn(eng):
            return eng.affine_select(c.ident_bf.sl(0, 128), c.ident_bf.sl(0, 128),
                                     [[-1, 128]], OP.is_equal, 0.0, channel_multiplier=1)
        c.p.op("gpsimd", fn, reads=[c.ident_bf.reg()], writes=[c.ident_bf.reg()], tag="identb")
    return c.ident_bf


def need_e8(c):
    if not hasattr(c, "E8"):
        c.E8 = mkplane(c, "E8", 8 * 128, F32, npart=8)
        c.memset(c.E8.sl(0, 8 * 128, np_=8), 1.0, writes=[c.E8.reg()], tag="init")
        def fn(eng):
            ap = c.E8._mk(0, [[8 * 128, 8], [128, 8], [1, 128]])
            return eng.affine_select(ap, ap, [[-1, 8], [0, 128]], OP.is_equal, 0.0,
                                     channel_multiplier=1)
        c.p.op("gpsimd", fn, reads=[c.E8.reg()], writes=[c.E8.reg()], tag="e8")
    return c.E8


def next_psum(c):
    j = c._psum_i % len(c.psum_pool)
    c._psum_i += 1
    t = c.psum_pool[j]
    return t, (t.ap().tensor.name,)


def load_vec(c, name, dram_name, q="sync"):
    pl = mkplane(c, name, c.NT, F32)
    src = c.dram[dram_name]
    c.dma(q, pl.sl(0, c.NT), src.ap()[:, :], reads=[(dram_name,)], writes=[pl.reg()],
          tag=f"ld_{name}")
    return pl


def precast_weights(c, names, staging_pl, half_cols):
    i = 0
    for name in names:
        src = c.dram[name]
        dst = c.dram["bf_" + name]
        tot = 1
        for d in src.ap().shape:
            tot *= d
        hc = half_cols
        while hc > 128 and (tot % hc != 0 or (tot // hc) % 128 != 0):
            hc //= 2
        assert tot % hc == 0 and (tot // hc) % 128 == 0, (name, tot, half_cols)
        R = tot // hc
        for r0 in range(0, R, 128):
            h = i % 2
            stg = staging_pl.sl(h * half_cols, hc)
            sv = bass.AP(src.ap().tensor, r0 * hc, [[hc, 128], [1, hc]])
            dv = bass.AP(dst.ap().tensor, r0 * hc, [[hc, 128], [1, hc]])
            c.dma("gpsimd", stg, sv, reads=[(name,)], writes=[(staging_pl.name, h)],
                  tag="pc_ld")
            c.dma("sync", dv, stg, reads=[(staging_pl.name, h)],
                  writes=[("bf_" + name,)], tag="pc_st")
            i += 1



def ln_cmajor(c: Ctx, src_pl, src3, dst_pl, dst3, sq_pl, meanb, rstdb, rows,
              gcol=None, bcol=None, gb_regs=(), eps=1e-5, tag="ln", tlen=None,
              sq_parts=1, ntile=None):
    NT = c.NT if ntile is None else ntile
    T = c.T if tlen is None else tlen
    D = c.D
    r0, r1 = rows[0], rows[1]
    NCH = T // 512
    sq_parts = min(sq_parts, NCH)
    part_nch = NCH // sq_parts
    sq3 = lambda: sq_pl.v3(NT, T // sq_parts)
    for part in range(sq_parts):
        lo = part * part_nch
        c.act(sq_pl.v3(NT, part_nch * 512), src3[:, :, lo * 512:(lo + part_nch) * 512],
              AF.Square, reads=[src_pl.reg()], writes=[sq_pl.reg()], tag=tag + "_sq")
        for nch in range(lo, lo + part_nch):
            for row, rhs3, rbase in ((r0, src3[:, :, nch * 512:(nch + 1) * 512], src_pl.reg()),
                                     (r1, sq_pl.v3(NT, part_nch * 512)[:, :, (nch - lo) * 512:(nch - lo + 1) * 512],
                                      sq_pl.reg())):
                pt, preg = next_psum(c)
                def fn(eng, pt=pt, rhs3=rhs3):
                    last = None
                    for kc in range(NT):
                        last = eng.matmul(pt.ap()[0:1, :], c.ones_col.sl(0, 1),
                                          rhs3[:, kc, :],
                                          start=(kc == 0), stop=(kc == NT - 1))
                    return last
                c.mm_group(fn, reads=[rbase, c.ones_col.reg()], writes=[preg], tag=tag + "_red")
                c.copy(row.sl(nch * 512, 512, np_=1), pt.ap()[0:1, :],
                       reads=[preg], writes=[row.reg()], tag=tag + "_rowev", engine="scalar")
    a0, a32 = rows[2], rows[3]
    c.ts(r0.sl(0, T, np_=1), r0.sl(0, T, np_=1), 1.0 / D, None, OP.mult, OP.bypass,
         reads=[r0.reg()], writes=[r0.reg()], tag=tag + "_mean")
    c.ts(r1.sl(0, T, np_=1), r1.sl(0, T, np_=1), 1.0 / D, None, OP.mult, OP.bypass,
         reads=[r1.reg()], writes=[r1.reg()], tag=tag + "_msq")
    c.tt(a0.sl(0, T, np_=1), r0.sl(0, T, np_=1), r0.sl(0, T, np_=1), OP.mult,
         reads=[r0.reg()], writes=[a0.reg()], tag=tag + "_m2")
    c.dma("sync", a32.sl(0, T, np_=1), a0.sl(0, T, np_=1),
          reads=[a0.reg()], writes=[a32.reg()], tag=tag + "_m2cp")
    c.tt(r1.sl(0, T, np_=1), r1.sl(0, T, np_=1), a32.sl(0, T, np_=1), OP.subtract,
         reads=[r1.reg(), a32.reg()], writes=[r1.reg()], tag=tag + "_var")
    c.ts(r1.sl(0, T, np_=1), r1.sl(0, T, np_=1), float(eps), None, OP.add, OP.bypass,
         reads=[r1.reg()], writes=[r1.reg()], tag=tag + "_vare")
    c.recip(a32.sl(0, T, np_=1), r1.sl(0, T, np_=1), reads=[r1.reg()], writes=[a32.reg()],
            tag=tag + "_rv")
    c.act(a32.sl(0, T, np_=1), a32.sl(0, T, np_=1), AF.Sqrt, reads=[a32.reg()],
          writes=[a32.reg()], tag=tag + "_rstd")
    for row, dst in ((r0, meanb), (a32, rstdb)):
        for nch in range(NCH):
            pt, preg = next_psum(c)
            def fn(eng, pt=pt, row=row, nch=nch):
                ones_v = c.ones128._mk(row.base * 128, [[128, 1], [1, 128]])
                return eng.matmul(pt.ap()[:, :], ones_v,
                                  row.sl(nch * 512, 512, np_=1), start=True, stop=True)
            c.mm_group(fn, reads=[row.reg(), c.ones128.reg()], writes=[preg], tag=tag + "_bc")
            c.copy(dst.sl(nch * 512, 512), pt.ap()[:, :], reads=[preg],
                   writes=[dst.reg()], tag=tag + "_bcev", engine="scalar")
    mb3 = meanb.bc3(NT, T)
    rb3 = rstdb.bc3(NT, T)
    c.tt(dst3, src3, mb3, OP.subtract, reads=[src_pl.reg(), meanb.reg()],
         writes=[dst_pl.reg()], tag=tag + "_sub")
    c.tt(dst3, dst3, rb3, OP.mult, reads=[dst_pl.reg(), rstdb.reg()],
         writes=[dst_pl.reg()], tag=tag + "_app")
    if gcol is not None:
        for kc in range(NT):
            c.ts(dst3[:, kc, :], dst3[:, kc, :], gcol(kc), bcol(kc),
                 OP.mult, OP.add, reads=[dst_pl.reg()] + list(gb_regs),
                 writes=[dst_pl.reg()], tag=tag + "_gb")



def build_l0_attn(c: Ctx, xin, xout, w):
    XI = xin.ap().tensor.name; XO = xout.ap().tensor.name
    NT, T, D, B = c.NT, c.T, c.D, c.B
    NH = 4 * NT
    nc = c.nc

    A = mkplane(c, "A", NT * T, BF16)
    Bz = mkplane(c, "Bz", NT * (T + 1), BF16)
    C = mkplane(c, "C", NT * T, BF16)
    Dp = mkplane(c, "Dp", NT * T, BF16)
    E = mkplane(c, "E", max(NT * T, 4 * NT * 512), BF16)
    SC = mkplane(c, "SC", 4 * T, F32)
    wk = [mkplane(c, "wkbuf0", NT * 128, BF16), mkplane(c, "wkbuf1", NT * 128, BF16)]
    wbig = [mkplane(c, "wbig0", NT * 128, BF16), mkplane(c, "wbig1", NT * 128, BF16)]
    vecs = mkplane(c, "vecs", 16 * c.NT, F32)

    VNAMES = ["ln1_g", "ln1_b", "ln2_g", "ln2_b", "tmk", "tmv", "tmr",
              "f_tmk", "f_tmr", "time_decay", "time_first"]
    vcol = {}
    for j, nm in enumerate(VNAMES):
        vcol[nm] = j * NT
        c.dma("sync", vecs.sl(j * NT, NT), c.dram["L0_attn_" + nm].ap()[:, :],
              reads=[("L0_attn_" + nm,)], writes=[vecs.reg()], tag="ldv_" + nm)
    vcol["lam"] = 11 * NT
    vcol["eu"] = 12 * NT
    c.act(vecs.sl(11 * NT, NT), vecs.sl(vcol["time_decay"], NT), AF.Exp,
          reads=[vecs.reg()], writes=[vecs.reg()], tag="lam1")
    c.act(vecs.sl(11 * NT, NT), vecs.sl(11 * NT, NT), AF.Exp, scale=-1.0,
          reads=[vecs.reg()], writes=[vecs.reg()], tag="lam2")
    c.act(vecs.sl(12 * NT, NT), vecs.sl(vcol["time_first"], NT), AF.Exp,
          reads=[vecs.reg()], writes=[vecs.reg()], tag="eu")

    def vc(nm, kc):
        return vecs.sl(vcol[nm] + kc, 1)

    rows = [Row(SC, 0, 3 * T), Row(SC, 32, 3 * T), Row(SC, 0, 2 * T), Row(SC, 32, 2 * T)]

    meanb = Plane(SC.t, 4 * T)
    rstdb = Plane(SC.t, 4 * T)

    class SCView:
        def __init__(self, c0):
            self.c0 = c0
        def sl(self, off, n, np_=128):
            return SC._mk(self.c0 + off, [[4 * T, np_], [1, n]])
        def bc3(self, ntile, tlen, off=0):
            return SC._mk(self.c0 + off, [[4 * T, 128], [0, ntile], [1, tlen]])
        def reg(self, *k):
            return ("SC",) + k
    meanb, rstdb = SCView(0), SCView(T)
    ek, vz = SCView(0), SCView(T)
    asc, bsc = SCView(2 * T), SCView(3 * T)
    x1c = SCView(0)
    x0c = SCView(2 * T)

    xin3 = [xin.ap()[b].rearrange("(n p) t -> p n t", p=128) for b in range(B)]
    xout3 = [xout.ap()[b].rearrange("(n p) t -> p n t", p=128) for b in range(B)]

    Bz_xn = Bz.v3(NT, T, stride=T + 1, off=1)
    Bz_xx = Bz.v3(NT, T, stride=T + 1, off=0)
    NCH = T // 512
    precast_weights(c, ["L0_attn_WkT", "L0_attn_WvT", "L0_attn_WrT", "L0_attn_WoT",
                        "L0_attn_fWrT", "L0_attn_fWkT", "L0_attn_fWvT"],
                    E, E.ncols // 2)

    def stream_wcols(dst_pl, src_name, i, nk, tag):
        src = c.dram["bf_" + src_name].ap().rearrange("(n p) m -> p n m", p=128)
        c.dma("sync", dst_pl._mk(0, [[dst_pl.ncols, 128], [128, nk], [1, 128]]),
              src[:, :, i * 128:(i + 1) * 128],
              reads=[("bf_" + src_name,)], writes=[dst_pl.reg()], tag=tag)

    def proj_mm(dst_pl, dst_off_fn, wname, rhs_pl, rhs3, act_fn, act_dtype_note, b, tagp,
                wbuf=None, nk=None, rhs_reg=None):
        nk = NT if nk is None else nk
        for i in range(NT):
            wbuf = wk[i % 2]
            stream_wcols(wbuf, wname, i, nk, tagp + "_w%d" % i)
            for nch in range(NCH):
                pt, preg = next_psum(c)
                def fn(eng, pt=pt, i=i, nch=nch, wbuf=wbuf):
                    last = None
                    for kc in range(nk):
                        last = eng.matmul(pt.ap()[:, :], wbuf.sl(kc * 128, 128),
                                          rhs3[:, kc, nch * 512:(nch + 1) * 512],
                                          start=(kc == 0), stop=(kc == nk - 1))
                    return last
                c.mm_group(fn, reads=[wbuf.reg(), rhs_reg or rhs_pl.reg()],
                           writes=[preg], tag=tagp + "_mm")
                act_fn(i, nch, pt, preg)

    for b in range(B):
        c.dma("gpsimd", A.v3(NT, T), xin3[b], reads=[(XI,)], writes=[A.reg()],
              tag="ld_x0")
        ln_cmajor(c, A, A.v3(NT, T), A, A.v3(NT, T), C, meanb, rstdb, rows, tag="ln0")
        c.memset(Bz._mk(0, [[Bz.ncols, 128], [T + 1, NT], [1, 1]]), 0.0,
                 writes=[Bz.reg()], tag="bz_zero")
        ln_cmajor(c, A, A.v3(NT, T), Bz, Bz_xn, C, meanb, rstdb, rows, tag="ln1",
                  gcol=lambda kc: vc("ln1_g", kc), bcol=lambda kc: vc("ln1_b", kc),
                  gb_regs=[vecs.reg()])

        c.tt(C.v3(NT, T), Bz_xn, Bz_xx, OP.subtract, reads=[Bz.reg()], writes=[C.reg()],
             tag="mix_d")
        for kc in range(NT):
            c.stt(Dp.v3(NT, T)[:, kc, :], C.v3(NT, T)[:, kc, :], vc("tmk", kc),
                  Bz_xx[:, kc, :], OP.mult, OP.add,
                  reads=[C.reg(), Bz.reg(), vecs.reg()], writes=[Dp.reg()], tag="mix_xk")
            c.stt(E.v3(NT, T)[:, kc, :], C.v3(NT, T)[:, kc, :], vc("tmv", kc),
                  Bz_xx[:, kc, :], OP.mult, OP.add,
                  reads=[C.reg(), Bz.reg(), vecs.reg()], writes=[E.reg()], tag="mix_xv")

        for i in range(NT):
            wki = wk[i % 2]
            stream_wcols(wki, "L0_attn_WkT", i, NT, "wk%d" % i)
            for nch in range(NCH):
                pt, preg = next_psum(c)
                def fn(eng, pt=pt, i=i, nch=nch, wki=wki):
                    last = None
                    for kc in range(NT):
                        last = eng.matmul(pt.ap()[:, :], wki.sl(kc * 128, 128),
                                          Dp.v3(NT, T)[:, kc, nch * 512:(nch + 1) * 512],
                                          start=(kc == 0), stop=(kc == NT - 1))
                    return last
                c.mm_group(fn, reads=[wki.reg(), Dp.reg()], writes=[preg], tag="kmm")
                c.act(ek.sl(nch * 512, 512), pt.ap()[:, :], AF.Exp,
                      reads=[preg], writes=[ek.reg()], tag="ek")
            stream_wcols(wbig[0], "L0_attn_WvT", i, NT, "wv%d" % i)
            for nch in range(NCH):
                pt, preg = next_psum(c)
                def fn(eng, pt=pt, i=i, nch=nch):
                    last = None
                    for kc in range(NT):
                        last = eng.matmul(pt.ap()[:, :], wbig[0].sl(kc * 128, 128),
                                          E.v3(NT, T)[:, kc, nch * 512:(nch + 1) * 512],
                                          start=(kc == 0), stop=(kc == NT - 1))
                    return last
                c.mm_group(fn, reads=[wbig[0].reg(), E.reg()], writes=[preg], tag="vmm")
                c.copy(vz.sl(nch * 512, 512), pt.ap()[:, :], reads=[preg],
                       writes=[vz.reg()], tag="vev", engine="scalar")
            c.tt(vz.sl(0, T), ek.sl(0, T), vz.sl(0, T), OP.mult,
                 reads=[ek.reg(), vz.reg()], writes=[vz.reg()], tag="z")
            lam_b = vecs._mk(vcol["lam"] + i, [[vecs.ncols, 128], [0, T]])
            c.scan(bsc.sl(0, T), lam_b, ek.sl(0, T), 0.0, OP.mult, OP.add,
                   reads=[vecs.reg(), ek.reg()], writes=[bsc.reg()], tag="bscan")
            c.scan(asc.sl(0, T), lam_b, vz.sl(0, T), 0.0, OP.mult, OP.add,
                   reads=[vecs.reg(), vz.reg()], writes=[asc.reg()], tag="ascan")
            c.stt(ek.sl(1, T - 1), ek.sl(1, T - 1), vc("eu", i), bsc.sl(0, T - 1),
                  OP.mult, OP.add, reads=[ek.reg(), bsc.reg(), vecs.reg()],
                  writes=[ek.reg()], tag="den")
            c.ts(ek.sl(0, 1), ek.sl(0, 1), vc("eu", i), None, OP.mult, OP.bypass,
                 reads=[ek.reg(), vecs.reg()], writes=[ek.reg()], tag="den0")
            c.recip(ek.sl(0, T), ek.sl(0, T), reads=[ek.reg()], writes=[ek.reg()],
                    tag="recip")
            c.stt(vz.sl(1, T - 1), vz.sl(1, T - 1), vc("eu", i), asc.sl(0, T - 1),
                  OP.mult, OP.add, reads=[vz.reg(), asc.reg(), vecs.reg()],
                  writes=[vz.reg()], tag="num")
            c.ts(vz.sl(0, 1), vz.sl(0, 1), vc("eu", i), None, OP.mult, OP.bypass,
                 reads=[vz.reg(), vecs.reg()], writes=[vz.reg()], tag="num0")
            c.tt(C.v3(NT, T)[:, i, :], vz.sl(0, T), ek.sl(0, T), OP.mult,
                 reads=[vz.reg(), ek.reg()], writes=[C.reg()], tag="y")

        c.tt(Dp.v3(NT, T), Bz_xn, Bz_xx, OP.subtract, reads=[Bz.reg()], writes=[Dp.reg()],
             tag="mix_d2")
        for kc in range(NT):
            c.stt(E.v3(NT, T)[:, kc, :], Dp.v3(NT, T)[:, kc, :], vc("tmr", kc),
                  Bz_xx[:, kc, :], OP.mult, OP.add,
                  reads=[Dp.reg(), Bz.reg(), vecs.reg()], writes=[E.reg()], tag="mix_xr")
        def sr_evac(i, nch, pt, preg):
            c.act(Dp.v3(NT, T)[:, i, nch * 512:(nch + 1) * 512], pt.ap()[:, :],
                  AF.Sigmoid, reads=[preg], writes=[Dp.reg()], tag="sr")
        proj_mm(Dp, None, "L0_attn_WrT", E, E.v3(NT, T), sr_evac, None, b, "rproj")
        c.tt(C.v3(NT, T), C.v3(NT, T), Dp.v3(NT, T), OP.mult,
             reads=[C.reg(), Dp.reg()], writes=[C.reg()], tag="sy")

        def wo_evac(i, nch, pt, preg):
            sl = A.v3(NT, T)[:, i, nch * 512:(nch + 1) * 512]
            c.tt(sl, pt.ap()[:, :], sl, OP.add, reads=[preg, A.reg()], writes=[A.reg()],
                 tag="xmid")
        proj_mm(A, None, "L0_attn_WoT", C, C.v3(NT, T), wo_evac, None, b, "oproj")

        c.memset(Bz._mk(0, [[Bz.ncols, 128], [T + 1, NT], [1, 1]]), 0.0,
                 writes=[Bz.reg()], tag="bz_zero2")
        ln_cmajor(c, A, A.v3(NT, T), Bz, Bz_xn, C, meanb, rstdb, rows, tag="ln2",
                  gcol=lambda kc: vc("ln2_g", kc), bcol=lambda kc: vc("ln2_b", kc),
                  gb_regs=[vecs.reg()])

        c.tt(C.v3(NT, T), Bz_xn, Bz_xx, OP.subtract, reads=[Bz.reg()], writes=[C.reg()],
             tag="mix_d3")
        for kc in range(NT):
            c.stt(Dp.v3(NT, T)[:, kc, :], C.v3(NT, T)[:, kc, :], vc("f_tmk", kc),
                  Bz_xx[:, kc, :], OP.mult, OP.add,
                  reads=[C.reg(), Bz.reg(), vecs.reg()], writes=[Dp.reg()], tag="mix_xk2")
            c.stt(E.v3(NT, T)[:, kc, :], C.v3(NT, T)[:, kc, :], vc("f_tmr", kc),
                  Bz_xx[:, kc, :], OP.mult, OP.add,
                  reads=[C.reg(), Bz.reg(), vecs.reg()], writes=[E.reg()], tag="mix_xr2")

        def sp_evac(i, nch, pt, preg):
            c.act(C.v3(NT, T)[:, i, nch * 512:(nch + 1) * 512], pt.ap()[:, :],
                  AF.Sigmoid, reads=[preg], writes=[C.reg()], tag="sp")
        proj_mm(C, None, "L0_attn_fWrT", E, E.v3(NT, T), sp_evac, None, b, "fr")

        fWvT3 = c.dram["bf_L0_attn_fWvT"].ap().rearrange("(n p) m -> p n m", p=128)
        E_kk = E._mk(0, [[E.ncols, 128], [512, NH], [1, 512]])
        for n in range(NCH):
            c.dma("sync", SC._mk(2 * T, [[4 * T, 128], [512, NT], [1, 512]]),
                  xin3[b][:, :, n * 512:(n + 1) * 512],
                  reads=[(XI,)], writes=[("SC",)], tag="ld_x0c")
            for fM in range(NH):
                wkf = wk[fM % 2]
                stream_wcols(wkf, "L0_attn_fWkT", fM, NT, "fwk")
                pt, preg = next_psum(c)
                def fn(eng, pt=pt, fM=fM, n=n, wkf=wkf):
                    last = None
                    for kc in range(NT):
                        last = eng.matmul(pt.ap()[:, :], wkf.sl(kc * 128, 128),
                                          Dp.v3(NT, T)[:, kc, n * 512:(n + 1) * 512],
                                          start=(kc == 0), stop=(kc == NT - 1))
                    return last
                c.mm_group(fn, reads=[wkf.reg(), Dp.reg()], writes=[preg], tag="kkmm")
                sl = E_kk[:, fM, :]
                c.ts(sl, pt.ap()[:, :], 0.0, None, OP.max, OP.bypass,
                     reads=[preg], writes=[E.reg()], tag="kkrelu")
                c.act(sl, sl, AF.Square, reads=[E.reg()], writes=[E.reg()], tag="kksq")
            for dM in range(NT):
                pt, preg = next_psum(c)
                for slab in range(4):
                    wb = wbig[slab % 2]
                    c.dma("sync",
                          wb._mk(0, [[wb.ncols, 128], [128, NT], [1, 128]]),
                          fWvT3[:, slab * NT:(slab + 1) * NT, dM * 128:(dM + 1) * 128],
                          reads=[("bf_L0_attn_fWvT",)], writes=[wb.reg()], tag="fwv_w")
                    def fn(eng, pt=pt, wb=wb, slab=slab, n=n):
                        last = None
                        for j in range(NT):
                            kc = slab * NT + j
                            last = eng.matmul(pt.ap()[:, :], wb.sl(j * 128, 128),
                                              E_kk[:, kc, :],
                                              start=(slab == 0 and j == 0),
                                              stop=(slab == 3 and j == NT - 1))
                        return last
                    c.mm_group(fn, reads=[wb.reg(), E.reg()], writes=[preg], tag="fvmm")
                t1 = x1c.sl(dM * 512, 512)
                c.tt(t1, pt.ap()[:, :], C.v3(NT, T)[:, dM, n * 512:(n + 1) * 512],
                     OP.mult, reads=[preg, C.reg()], writes=[ek.reg(), vz.reg()],
                     tag="rkv")
                c.tt(t1, t1, A.v3(NT, T)[:, dM, n * 512:(n + 1) * 512], OP.add,
                     reads=[ek.reg(), vz.reg(), A.reg()], writes=[ek.reg(), vz.reg()],
                     tag="addmid")
                c.tt(t1, t1, x0c.sl(dM * 512, 512), OP.add,
                     reads=[ek.reg(), vz.reg(), asc.reg(), bsc.reg()],
                     writes=[ek.reg(), vz.reg()], tag="addx0")
            c.dma("sync", xout3[b][:, :, n * 512:(n + 1) * 512],
                  SC._mk(0, [[4 * T, 128], [512, NT], [1, 512]]),
                  reads=[ek.reg(), vz.reg()], writes=[(XO,)], tag="st_x1")



def _decl(nc, c, name, shape, kind="ExternalInput", dtype=F32):
    h = nc.dram_tensor(name, list(shape), dtype, kind=kind)
    c.dram[name] = h
    return h


def build_prog_l0_attn(D=1024, T=2048, B=2):
    nc = bass.Bass("TRN2", target_bir_lowering=False, debug=False)
    c = Ctx(nc, D=D, T=T, B=B)
    c.dram = {}
    NT = c.NT
    HID = 4 * D
    xin = _decl(nc, c, "xin", [B, D, T])
    xout = _decl(nc, c, "xout", [B, D, T], kind="ExternalOutput")
    for nm in ("WkT", "WvT", "WrT", "WoT", "fWrT"):
        _decl(nc, c, "L0_attn_" + nm, [D, D])
        _decl(nc, c, "bf_L0_attn_" + nm, [D, D], kind="Internal", dtype=BF16)
    _decl(nc, c, "L0_attn_fWkT", [D, HID])
    _decl(nc, c, "bf_L0_attn_fWkT", [D, HID], kind="Internal", dtype=BF16)
    _decl(nc, c, "L0_attn_fWvT", [HID, D])
    _decl(nc, c, "bf_L0_attn_fWvT", [HID, D], kind="Internal", dtype=BF16)
    for nm in ("ln1_g", "ln1_b", "ln2_g", "ln2_b", "tmk", "tmv", "tmr",
               "f_tmk", "f_tmr", "time_decay", "time_first"):
        _decl(nc, c, "L0_attn_" + nm, [128, NT])
    setup_common(c)
    build_l0_attn(c, xin, xout, None)
    c.p.fence()
    c.p.emit()
    global _LAST_PROG
    _LAST_PROG = c.p
    return nc


def _vecprep(v, NT):
    v = np.asarray(v, dtype=np.float32).reshape(-1)
    return np.ascontiguousarray(v.reshape(NT, 128).T)


def host_l0_attn(I, D=1024):
    NT = D // 128
    m = {}
    for nm in ("Wk", "Wv", "Wr", "Wo"):
        m["L0_attn_" + nm + "T"] = np.ascontiguousarray(I["L0_attn_" + nm].T)
    m["L0_attn_fWrT"] = np.ascontiguousarray(I["L0_attn_fWr"].T)
    m["L0_attn_fWkT"] = np.ascontiguousarray(I["L0_attn_fWk"].T)
    m["L0_attn_fWvT"] = np.ascontiguousarray(I["L0_attn_fWv"].T)
    for nm in ("ln1_g", "ln1_b", "ln2_g", "ln2_b", "tmk", "tmv", "tmr",
               "f_tmk", "f_tmr", "time_decay", "time_first"):
        m["L0_attn_" + nm] = _vecprep(I["L0_attn_" + nm], NT)
    return m



def build_l1_ffn(c: Ctx, xin, xout, pfx="L1_ffn_"):
    XI = xin.ap().tensor.name; XO = xout.ap().tensor.name
    need_ident32(c)
    need_e8(c)
    NT, T, D, B = c.NT, c.T, c.D, c.B
    TOK = B * T
    NH = 4 * NT
    F = 4 * D
    NE = 8
    NCHU = TOK // 512

    A4 = mkplane(c, "m_A4", NT * TOK, BF16)
    SQP = min(4, TOK // 512)
    SQ4 = mkplane(c, "m_SQ4", NT * TOK // SQP, BF16)
    mb4 = mkplane(c, "m_mb4", TOK, BF16)
    rb4 = mkplane(c, "m_rb4", TOK, BF16)

    hpl = mkplane(c, "m_h", NH * 512, BF16)
    yac = mkplane(c, "m_yac", NT * 512, F32)
    x3c = mkplane(c, "m_x3c", NT * 512, F32)
    rows4 = [Row(yac, 0, 0), Row(yac, 32, 0), Row(x3c, 0, 0), Row(x3c, 32, 0)]
    w1b = [mkplane(c, f"m_w1b{j}", NT * 128, BF16) for j in range(4)]
    w2b = [mkplane(c, f"m_w2b{j}", 2 * NT * 128, BF16) for j in range(4)]
    wbc = mkplane(c, "m_wbc", 512, F32)
    redp = mkplane(c, "m_red", 512, F32, npart=16)
    wCp = mkplane(c, "m_wC", 512, F32, npart=8)
    wredp = mkplane(c, "m_wred", NT * 16, F32)
    wgp = mkplane(c, "m_wg", 16, F32, npart=8)
    wgnT = mkplane(c, "m_wgnT", 8, F32, npart=16)
    colsum = mkplane(c, "m_colsum", 1, F32, npart=16)
    gmisc = mkplane(c, "m_gmisc", 64, F32)
    b1c = mkplane(c, "m_b1c", NE * NH, F32)
    b2sb = mkplane(c, "m_b2sb", D, F32, npart=8)

    xin4 = xin.ap().rearrange("b (n p) t -> p n b t", p=128)
    xout4 = xout.ap().rearrange("b (n p) t -> p n b t", p=128)
    NPB = T // 512
    def xin_chunk(n):
        return xin4[:, :, n // NPB, (n % NPB) * 512:(n % NPB) * 512 + 512]
    def xout_chunk(n):
        return xout4[:, :, n // NPB, (n % NPB) * 512:(n % NPB) * 512 + 512]

    for b in range(B):
        c.dma("gpsimd", A4._mk(b * T, [[A4.ncols, 128], [TOK, NT], [1, T]]),
              xin4[:, :, b, :], reads=[(XI,)], writes=[A4.reg()], tag="m_ldx")
    c.dma("sync", wredp._mk(0, [[wredp.ncols, 128], [16, NT], [1, 16]]),
          c.dram[pfx + "w_red"].ap().rearrange("(n p) g -> p n g", p=128),
          reads=[(pfx + "w_red",)], writes=[wredp.reg()], tag="m_ldwred")
    c.dma("sync", wgp.sl(0, 16, np_=8), c.dram[pfx + "wg"].ap()[:, :],
          reads=[(pfx + "wg",)], writes=[wgp.reg()], tag="m_ldwg")
    c.dma("sync", b1c.sl(0, NE * NH), c.dram[pfx + "b1c"].ap()[:, :],
          reads=[(pfx + "b1c",)], writes=[b1c.reg()], tag="m_ldb1")
    c.dma("sync", b2sb.sl(0, D, np_=8), c.dram[pfx + "b2"].ap()[:, :],
          reads=[(pfx + "b2",)], writes=[b2sb.reg()], tag="m_ldb2")

    precast_weights(c, [pfx + "w1", pfx + "w2"], hpl, hpl.ncols // 2)
    ln_cmajor(c, A4, A4.v3(NT, TOK), A4, A4.v3(NT, TOK), SQ4, mb4, rb4, rows4,
              tag="m_ln", tlen=TOK, sq_parts=SQP)

    pt, preg = next_psum(c)
    def fn(eng, pt=pt):
        last = None
        for kc in range(NT):
            last = eng.matmul(pt.ap()[0:1, 0:16], c.ones128.sl(0, 1),
                              wredp._mk(0, [[wredp.ncols, 128], [16, NT], [1, 16]])[:, kc, :],
                              start=(kc == 0), stop=(kc == NT - 1))
        return last
    c.mm_group(fn, reads=[wredp.reg(), c.ones128.reg()], writes=[preg], tag="m_colsum_mm")
    c.copy(gmisc.sl(0, 16, np_=1), pt.ap()[0:1, 0:16], reads=[preg], writes=[gmisc.reg()],
           tag="m_colsum_ev", engine="scalar")
    cs_dram = c.dram["m_scr16"]
    c.dma("sync", cs_dram.ap()[:, :], gmisc.sl(0, 16, np_=1), reads=[gmisc.reg()],
          writes=[("m_scr16",)], tag="m_cs_out")
    def fn(eng):
        return eng.dma_start(out=colsum.sl(0, 1, np_=16),
                             in_=cs_dram.ap().rearrange("a g -> (g a)")[:, None],
                             allow_slow_non_contiguous=True)
    c.p.op("sync", fn, reads=[("m_scr16",)], writes=[colsum.reg()], tag="m_cs_in")
    c.act(gmisc.sl(32, 16, np_=8), wgp.sl(0, 16, np_=8), AF.Square,
          reads=[wgp.reg()], writes=[gmisc.reg()], tag="m_wgsq")
    c.p.op("vector", (lambda eng: eng.reduce_sum(gmisc.sl(16, 1, np_=8), gmisc.sl(32, 16, np_=8),
                                                 mybir.AxisListType.X)),
           reads=[gmisc.reg()], writes=[gmisc.reg()], tag="m_wgss2")
    c.act(gmisc.sl(17, 1, np_=8), gmisc.sl(16, 1, np_=8), AF.Sqrt,
          reads=[gmisc.reg()], writes=[gmisc.reg()], tag="m_wgn1")
    c.ts(gmisc.sl(17, 1, np_=8), gmisc.sl(17, 1, np_=8), 1e-4, None, OP.max, OP.bypass,
         reads=[gmisc.reg()], writes=[gmisc.reg()], tag="m_wgn2")
    c.recip(gmisc.sl(18, 1, np_=8), gmisc.sl(17, 1, np_=8), reads=[gmisc.reg()],
            writes=[gmisc.reg()], tag="m_wgn3")
    c.ts(wgp.sl(0, 16, np_=8), wgp.sl(0, 16, np_=8), gmisc.sl(18, 1, np_=8), None,
         OP.mult, OP.bypass, reads=[wgp.reg(), gmisc.reg()], writes=[wgp.reg()], tag="m_wgn4")
    wgn_dram = c.dram["m_scr8x16"]
    c.dma("sync", wgn_dram.ap()[:, :], wgp.sl(0, 16, np_=8), reads=[wgp.reg()],
          writes=[("m_scr8x16",)], tag="m_wgn_out")
    def fn(eng):
        return eng.dma_start(out=wgnT.sl(0, 8, np_=16),
                             in_=wgn_dram.ap().rearrange("e g -> g e"),
                             allow_slow_non_contiguous=True)
    c.p.op("sync", fn, reads=[("m_scr8x16",)], writes=[wgnT.reg()], tag="m_wgn_in")

    for n in range(NCHU):
        c.dma("sync", x3c.v3(NT, 512), xin_chunk(n),
              reads=[(XI,)], writes=[x3c.reg()], tag="m_ldx3c")
        pt, preg = next_psum(c)
        def fn(eng, pt=pt, n=n):
            last = None
            for kc in range(NT):
                last = eng.matmul(pt.ap()[0:16, :],
                                  wredp._mk(0, [[wredp.ncols, 128], [16, NT], [1, 16]])[:, kc, :],
                                  x3c.v3(NT, 512)[:, kc, :],
                                  start=(kc == 0), stop=(kc == NT - 1))
            return last
        c.mm_group(fn, reads=[wredp.reg(), x3c.reg()], writes=[preg], tag="m_redmm")
        m16 = mb4._mk(n * 512, [[mb4.ncols, 16], [1, 512]])
        r16 = rb4._mk(n * 512, [[rb4.ncols, 16], [1, 512]])
        c.ts(redp.sl(0, 512, np_=16), m16, colsum.sl(0, 1, np_=16), None, OP.mult,
             OP.bypass, reads=[mb4.reg(), colsum.reg()], writes=[redp.reg()], tag="m_red1")
        c.tt(redp.sl(0, 512, np_=16), pt.ap()[0:16, :], redp.sl(0, 512, np_=16),
             OP.subtract, reads=[preg, redp.reg()], writes=[redp.reg()], tag="m_red2")
        c.tt(redp.sl(0, 512, np_=16), redp.sl(0, 512, np_=16), r16, OP.mult,
             reads=[redp.reg(), rb4.reg()], writes=[redp.reg()], tag="m_red3")
        for j in range(4):
            pt2, preg2 = next_psum(c)
            def fn(eng, pt2=pt2, j=j):
                return eng.matmul(pt2.ap()[:, 0:8], redp.sl(j * 128, 128, np_=16),
                                  wgnT.sl(0, 8, np_=16), start=True, stop=True)
            c.mm_group(fn, reads=[redp.reg(), wgnT.reg()], writes=[preg2], tag="m_lgmm")
            l8 = gmisc.sl(24, 8)
            mx = gmisc.sl(33, 1)
            sm = gmisc.sl(34, 1)
            c.copy(l8, pt2.ap()[:, 0:8], reads=[preg2], writes=[gmisc.reg()],
                   tag="m_l8", engine="vector")
            c.p.op("vector", (lambda eng, l8=l8, mx=mx: eng.reduce_max(mx, l8, mybir.AxisListType.X)),
                   reads=[gmisc.reg()], writes=[gmisc.reg()], tag="m_gmax")
            c.ts(l8, l8, mx, None, OP.subtract, OP.bypass,
                 reads=[gmisc.reg()], writes=[gmisc.reg()], tag="m_gsub")
            el8 = gmisc.sl(40, 8)
            c.act(el8, l8, AF.Exp, reads=[gmisc.reg()], writes=[gmisc.reg()], tag="m_gexp")
            c.p.op("vector", (lambda eng, el8=el8, sm=sm: eng.reduce_sum(sm, el8, mybir.AxisListType.X)),
                   reads=[gmisc.reg()], writes=[gmisc.reg()], tag="m_gsum")
            c.recip(sm, sm, reads=[gmisc.reg()], writes=[gmisc.reg()], tag="m_gscore")
            c.ts(l8, l8, 0.0, None, OP.is_equal, OP.bypass,
                 reads=[gmisc.reg()], writes=[gmisc.reg()], tag="m_gmask")
            c.ts(l8, l8, sm, None, OP.mult, OP.bypass,
                 reads=[gmisc.reg()], writes=[gmisc.reg()], tag="m_gsel")
            pt3, preg3 = next_psum(c)
            def fn(eng, pt3=pt3, l8=l8):
                return eng.transpose(pt3.ap()[0:8, 0:128], l8, c.ident32.sl(0, 128))
            c.p.op("tensor", fn, reads=[gmisc.reg(), ("ident32",)], writes=[preg3],
                   tag="m_gtr")
            c.copy(wCp._mk(j * 128, [[wCp.ncols, 8], [1, 128]]), pt3.ap()[0:8, 0:128],
                   reads=[preg3], writes=[wCp.reg()], tag="m_gtev", engine="vector")

        for e in range(NE):
            pt, preg = next_psum(c)
            def fn(eng, pt=pt, e=e):
                return eng.matmul(pt.ap()[:, :], c.E8._mk(e * 128, [[8 * 128, 8], [1, 128]]),
                                  wCp._mk(0, [[wCp.ncols, 8], [1, 512]]),
                                  start=True, stop=True)
            c.mm_group(fn, reads=[wCp.reg(), c.E8.reg()], writes=[preg], tag="m_bc")
            c.copy(wbc.sl(0, 512), pt.ap()[:, :], reads=[preg], writes=[wbc.reg()],
                   tag="m_bcev", engine="scalar")
            w1d = c.dram["bf_" + pfx + "w1"].ap()[e].rearrange("(k p) m -> p k m", p=128)
            for fM in range(NH):
                wb = w1b[fM % 4]
                c.dma("sync", wb._mk(0, [[wb.ncols, 128], [128, NT], [1, 128]]),
                      w1d[:, :, fM * 128:(fM + 1) * 128],
                      reads=[("bf_" + pfx + "w1",)], writes=[wb.reg()], tag="m_w1w")
                pt, preg = next_psum(c)
                def fn(eng, pt=pt, wb=wb, n=n):
                    last = None
                    for kc in range(NT):
                        last = eng.matmul(pt.ap()[:, :], wb.sl(kc * 128, 128),
                                          A4.v3(NT, TOK)[:, kc, n * 512:(n + 1) * 512],
                                          start=(kc == 0), stop=(kc == NT - 1))
                    return last
                c.mm_group(fn, reads=[wb.reg(), A4.reg()], writes=[preg], tag="m_w1mm")
                c.act(hpl.v3(NH, 512)[:, fM, :], pt.ap()[:, :], AF.Gelu,
                      bias=b1c.sl(e * NH + fM, 1), reads=[preg, b1c.reg()],
                      writes=[hpl.reg()], tag="m_gelu")
                c.tt(hpl.v3(NH, 512)[:, fM, :], hpl.v3(NH, 512)[:, fM, :], wbc.sl(0, 512),
                     OP.mult, reads=[hpl.reg(), wbc.reg()], writes=[hpl.reg()], tag="m_hsc")
            w2d = c.dram["bf_" + pfx + "w2"].ap()[e].rearrange("(k p) m -> p k m", p=128)
            for dM in range(NT):
                pt, preg = next_psum(c)
                for slab in range(2):
                    wb = w2b[(dM * 2 + slab) % 4]
                    c.dma("sync", wb._mk(0, [[wb.ncols, 128], [128, 2 * NT], [1, 128]]),
                          w2d[:, slab * 2 * NT:(slab + 1) * 2 * NT, dM * 128:(dM + 1) * 128],
                          reads=[("bf_" + pfx + "w2",)], writes=[wb.reg()], tag="m_w2w")
                    def fn(eng, pt=pt, wb=wb, slab=slab):
                        last = None
                        for j2 in range(2 * NT):
                            kc = slab * 2 * NT + j2
                            last = eng.matmul(pt.ap()[:, :], wb.sl(j2 * 128, 128),
                                              hpl.v3(NH, 512)[:, kc, :],
                                              start=(slab == 0 and j2 == 0),
                                              stop=(slab == 1 and j2 == 2 * NT - 1))
                        return last
                    c.mm_group(fn, reads=[wb.reg(), hpl.reg()], writes=[preg], tag="m_w2mm")
                if e == 0:
                    c.copy(yac.sl(dM * 512, 512), pt.ap()[:, :], reads=[preg],
                           writes=[yac.reg()], tag="m_yc0", engine="vector")
                else:
                    c.tt(yac.sl(dM * 512, 512), pt.ap()[:, :], yac.sl(dM * 512, 512),
                         OP.add, reads=[preg, yac.reg()], writes=[yac.reg()], tag="m_yacc")
        for dM in range(NT):
            pt, preg = next_psum(c)
            def fn(eng, pt=pt, dM=dM):
                return eng.matmul(pt.ap()[:, :], b2sb._mk(dM * 128, [[b2sb.ncols, 8], [1, 128]]),
                                  wCp._mk(0, [[wCp.ncols, 8], [1, 512]]), start=True, stop=True)
            c.mm_group(fn, reads=[b2sb.reg(), wCp.reg()], writes=[preg], tag="m_b2mm")
            c.tt(yac.sl(dM * 512, 512), pt.ap()[:, :], yac.sl(dM * 512, 512), OP.add,
                 reads=[preg, yac.reg()], writes=[yac.reg()], tag="m_b2acc")
        c.tt(x3c.v3(NT, 512), yac.v3(NT, 512), x3c.v3(NT, 512), OP.add,
             reads=[yac.reg(), x3c.reg()], writes=[x3c.reg()], tag="m_x4")
        c.dma("sync", xout_chunk(n), x3c.v3(NT, 512),
              reads=[x3c.reg()], writes=[(XO,)], tag="m_stx4")


def build_prog_l1_ffn(D=1024, T=2048, B=2):
    nc = bass.Bass("TRN2", target_bir_lowering=False, debug=False)
    c = Ctx(nc, D=D, T=T, B=B)
    c.dram = {}
    F = 4 * D
    xin = _decl(nc, c, "xin", [B, D, T])
    xout = _decl(nc, c, "xout", [B, D, T], kind="ExternalOutput")
    _decl(nc, c, "L1_ffn_w_red", [D, 16])
    _decl(nc, c, "L1_ffn_wg", [8, 16])
    _decl(nc, c, "L1_ffn_w1", [8, D, F])
    _decl(nc, c, "bf_L1_ffn_w1", [8, D, F], kind="Internal", dtype=BF16)
    _decl(nc, c, "L1_ffn_b1c", [128, 8 * (F // 128)])
    _decl(nc, c, "L1_ffn_w2", [8, F, D])
    _decl(nc, c, "bf_L1_ffn_w2", [8, F, D], kind="Internal", dtype=BF16)
    _decl(nc, c, "L1_ffn_b2", [8, D])
    _decl(nc, c, "m_scr16", [1, 16], kind="Internal")
    _decl(nc, c, "m_scr8x16", [8, 16], kind="Internal")
    setup_common(c, npsum=6)
    build_l1_ffn(c, xin, xout)
    c.p.fence()
    c.p.emit()
    global _LAST_PROG
    _LAST_PROG = c.p
    return nc


def host_l1_ffn(I, D=1024):
    F = 4 * D
    m = {
        "L1_ffn_w_red": I["L1_ffn_w_red"],
        "L1_ffn_wg": I["L1_ffn_wg"],
        "L1_ffn_w1": I["L1_ffn_w1"],
        "L1_ffn_w2": I["L1_ffn_w2"],
        "L1_ffn_b2": I["L1_ffn_b2"],
    }
    b1 = np.asarray(I["L1_ffn_b1"])
    cols = np.concatenate([b1[e].reshape(F // 128, 128).T for e in range(8)], axis=1)
    m["L1_ffn_b1c"] = np.ascontiguousarray(cols)
    return m



def build_l1_attn(c: Ctx, xin, xout, pfx="L1_attn_"):
    XI = xin.ap().tensor.name; XO = xout.ap().tensor.name
    need_zero_col(c)
    NT, T, D, B = c.NT, c.T, c.D, c.B
    R = 256
    NR = R // 128

    A5 = mkplane(c, "a_A5", NT * T, BF16)
    OP5 = mkplane(c, "a_OP5", NT * T, BF16)
    RK = mkplane(c, "a_RK", 6 * T, F32)
    AB = mkplane(c, "a_ABr", NR * T, BF16)
    AT = mkplane(c, "a_ATr", NR * T, BF16)
    rc = mkplane(c, "a_rc", T, F32)
    rows5_t = mkplane(c, "a_rows5", T, F32)
    xc5 = mkplane(c, "a_xc5", NT * 512, F32)
    wst = [mkplane(c, "a_wst0", NT * 128, BF16), mkplane(c, "a_wst1", NT * 128, BF16)]
    WQV = {}
    for nm in ("qvb", "qvt", "kwb", "kwt", "xt"):
        WQV[nm] = mkplane(c, "a_W" + nm, NT * R, BF16)
    UBT = mkplane(c, "a_UBT", NR * D, BF16)
    UTT = mkplane(c, "a_UTT", NR * D, BF16)
    av = mkplane(c, "a_av", 32, F32)

    rows5 = [Row(rows5_t, 0), Row(rows5_t, 32), Row(RK, 0, 2 * T), Row(RK, 32, 2 * T)]

    class SCV:
        def __init__(self, c0):
            self.c0 = c0
        def sl(self, off, n, np_=128):
            return RK._mk(self.c0 + off, [[RK.ncols, np_], [1, n]])
        def bc3(self, ntile, tlen, off=0):
            return RK._mk(self.c0 + off, [[RK.ncols, 128], [0, ntile], [1, tlen]])
        def reg(self, *k):
            return ("a_RK",) + k
    meanb5, rstdb5 = SCV(0), SCV(T)
    cB = [SCV(0), SCV(T)]
    cT = [SCV(2 * T), SCV(3 * T)]
    zX = [SCV(4 * T), SCV(5 * T)]

    xin4 = xin.ap().rearrange("b (n p) t -> p n b t", p=128)
    xout4 = xout.ap().rearrange("b (n p) t -> p n b t", p=128)

    c.dma("sync", av.sl(0, 1, np_=1), c.dram[pfx + "alpha"].ap()[:, :],
          reads=[(pfx + "alpha",)], writes=[av.reg()], tag="a_lda")
    pt, preg = next_psum(c)
    def fn(eng, pt=pt):
        return eng.matmul(pt.ap()[:, 0:1], c.ones128._mk(0, [[128, 1], [1, 128]]),
                          av.sl(0, 1, np_=1), start=True, stop=True)
    c.mm_group(fn, reads=[av.reg(), c.ones128.reg()], writes=[preg], tag="a_bca")
    c.copy(av.sl(1, 1), pt.ap()[:, 0:1], reads=[preg], writes=[av.reg()],
           tag="a_bcaev", engine="vector")
    c.dma("sync", av.sl(2, NT), c.dram[pfx + "bias_b"].ap()[:, :],
          reads=[(pfx + "bias_b",)], writes=[av.reg()], tag="a_ldbb")
    c.dma("sync", av.sl(2 + NT, NT), c.dram[pfx + "bias_t"].ap()[:, :],
          reads=[(pfx + "bias_t",)], writes=[av.reg()], tag="a_ldbt")
    c.stt(av.sl(2 + NT, NT), av.sl(2 + NT, NT), av.sl(1, 1), av.sl(2, NT),
          OP.mult, OP.add, reads=[av.reg()], writes=[av.reg()], tag="a_combo")
    def fn(eng):
        return eng.iota(rc.sl(0, T), [[1, T]], base=1, channel_multiplier=0,
                        allow_small_or_imprecise_dtypes=True)
    c.p.op("gpsimd", fn, writes=[rc.reg()], tag="a_iota")
    c.recip(rc.sl(0, T), rc.sl(0, T), reads=[rc.reg()], writes=[rc.reg()], tag="a_rc")
    for wname, vname, dst in (("Wq", "V_b", "qvb"), ("Wq", "V_t", "qvt"),
                              ("Wk", "W_b", "kwb"), ("Wk", "W_t", "kwt")):
        wd = c.dram[pfx + wname].ap().rearrange("(k p) m -> p k m", p=128)
        vd = c.dram[pfx + vname].ap().rearrange("(k p) r -> p k r", p=128)
        if not hasattr(c, "_a_vv"):
            c._a_vv = mkplane(c, "a_vv", NT * R, BF16)
        vv = c._a_vv
        c.dma("gpsimd", vv._mk(0, [[vv.ncols, 128], [R, NT], [1, R]]), vd,
              reads=[(pfx + vname,)], writes=[vv.reg()], tag="a_ldv")
        for iM in range(NT):
            wb = wst[iM % 2]
            c.dma("gpsimd", wb._mk(0, [[wb.ncols, 128], [128, NT], [1, 128]]),
                  wd[:, :, iM * 128:(iM + 1) * 128],
                  reads=[(pfx + wname,)], writes=[wb.reg()], tag="a_ldw")
            for nr in range(1):
                pt, preg = next_psum(c)
                def fn(eng, pt=pt, wb=wb, vv=vv):
                    last = None
                    for kc in range(NT):
                        last = eng.matmul(pt.ap()[:, 0:R], wb.sl(kc * 128, 128),
                                          vv._mk(0, [[vv.ncols, 128], [R, NT], [1, R]])[:, kc, :],
                                          start=(kc == 0), stop=(kc == NT - 1))
                    return last
                c.mm_group(fn, reads=[wb.reg(), vv.reg()], writes=[preg], tag="a_cwmm")
                c.copy(WQV[dst]._mk(iM * R, [[WQV[dst].ncols, 128], [1, R]]),
                       pt.ap()[:, 0:R], reads=[preg], writes=[WQV[dst].reg()],
                       tag="a_cwev", engine="scalar")
    c.dma("gpsimd", WQV["xt"]._mk(0, [[WQV["xt"].ncols, 128], [R, NT], [1, R]]),
          c.dram[pfx + "X_t"].ap().rearrange("(k p) r -> p k r", p=128),
          reads=[(pfx + "X_t",)], writes=[WQV["xt"].reg()], tag="a_ldxt")
    for nm, dst in (("U_bT", UBT), ("U_tT", UTT)):
        c.dma("gpsimd", dst._mk(0, [[dst.ncols, 128], [D, NR], [1, D]]),
              c.dram[pfx + nm].ap().rearrange("(k p) m -> p k m", p=128),
              reads=[(pfx + nm,)], writes=[dst.reg()], tag="a_ldu")

    NCH = T // 512
    for b in range(B):
        c.dma("gpsimd", A5.v3(NT, T), xin4[:, :, b, :], reads=[(XI,)],
              writes=[A5.reg()], tag="a_ldx")
        ln_cmajor(c, A5, A5.v3(NT, T), A5, A5.v3(NT, T), OP5, meanb5, rstdb5, rows5,
                  tag="a_ln")
        for dst_pl, wkey, outname in ((AB, "qvb", "ab"), (AT, "qvt", "at")):
            for rt in range(NR):
                for nch in range(NCH):
                    pt, preg = next_psum(c)
                    def fn(eng, pt=pt, wkey=wkey, rt=rt, nch=nch):
                        last = None
                        for kc in range(NT):
                            lhsT = WQV[wkey]._mk(kc * R + rt * 128,
                                                 [[WQV[wkey].ncols, 128], [1, 128]])
                            last = eng.matmul(pt.ap()[:, :], lhsT,
                                              A5.v3(NT, T)[:, kc, nch * 512:(nch + 1) * 512],
                                              start=(kc == 0), stop=(kc == NT - 1))
                        return last
                    c.mm_group(fn, reads=[WQV[wkey].reg(), A5.reg()], writes=[preg],
                               tag="a_prmm")
                    c.copy(dst_pl._mk(rt * T + nch * 512, [[dst_pl.ncols, 128], [1, 512]]),
                           pt.ap()[:, :], reads=[preg], writes=[dst_pl.reg()],
                           tag="a_prev", engine="scalar")
        for wkey, dsts in (("kwb", cB), ("kwt", cT), ("xt", zX)):
            for rt in range(NR):
                for nch in range(NCH):
                    pt, preg = next_psum(c)
                    def fn(eng, pt=pt, wkey=wkey, rt=rt, nch=nch):
                        last = None
                        for kc in range(NT):
                            lhsT = WQV[wkey]._mk(kc * R + rt * 128,
                                                 [[WQV[wkey].ncols, 128], [1, 128]])
                            last = eng.matmul(pt.ap()[:, :], lhsT,
                                              A5.v3(NT, T)[:, kc, nch * 512:(nch + 1) * 512],
                                              start=(kc == 0), stop=(kc == NT - 1))
                        return last
                    c.mm_group(fn, reads=[WQV[wkey].reg(), A5.reg()], writes=[preg],
                               tag="a_csmm")
                    c.copy(dsts[rt].sl(nch * 512, 512), pt.ap()[:, :],
                           reads=[preg], writes=[dsts[rt].reg()], tag="a_csev",
                           engine="scalar")
        zc = c.zero_col
        for dsts in (cB, cT, zX):
            for rt in range(NR):
                zb = zc._mk(0, [[1, 128], [0, T]])
                c.scan(dsts[rt].sl(0, T), dsts[rt].sl(0, T), zb, 0.0, OP.add, OP.add,
                       reads=[dsts[rt].reg(), ("zero_col",)], writes=[dsts[rt].reg()],
                       tag="a_scan")
        for rt in range(NR):
            c.tt(zX[rt].sl(0, T), zX[rt].sl(0, T), rc.sl(0, T), OP.mult,
                 reads=[zX[rt].reg(), rc.reg()], writes=[zX[rt].reg()], tag="a_zt")
        for rt in range(NR):
            ab_v = AB._mk(rt * T, [[AB.ncols, 128], [1, T]])
            at_v = AT._mk(rt * T, [[AT.ncols, 128], [1, T]])
            c.tt(ab_v, ab_v, cB[rt].sl(0, T), OP.mult,
                 reads=[AB.reg(), cB[rt].reg()], writes=[AB.reg()], tag="a_pb")
            c.tt(at_v, at_v, zX[rt].sl(0, T), OP.mult,
                 reads=[AT.reg(), zX[rt].reg()], writes=[AT.reg()], tag="a_pt1")
            c.tt(at_v, at_v, cT[rt].sl(0, T), OP.mult,
                 reads=[AT.reg(), cT[rt].reg()], writes=[AT.reg()], tag="a_pt2")
            c.ts(at_v, at_v, av.sl(1, 1), None, OP.mult, OP.bypass,
                 reads=[AT.reg(), av.reg()], writes=[AT.reg()], tag="a_pt3")
        for iM in range(NT):
            for nch in range(NCH):
                pt, preg = next_psum(c)
                def fn(eng, pt=pt, iM=iM, nch=nch):
                    last = None
                    for rt in range(NR):
                        lhsT = UBT._mk(rt * D + iM * 128, [[UBT.ncols, 128], [1, 128]])
                        last = eng.matmul(pt.ap()[:, :], lhsT,
                                          AB._mk(rt * T + nch * 512, [[AB.ncols, 128], [1, 512]]),
                                          start=(rt == 0), stop=False)
                    for rt in range(NR):
                        lhsT = UTT._mk(rt * D + iM * 128, [[UTT.ncols, 128], [1, 128]])
                        last = eng.matmul(pt.ap()[:, :], lhsT,
                                          AT._mk(rt * T + nch * 512, [[AT.ncols, 128], [1, 512]]),
                                          start=False, stop=(rt == NR - 1))
                    return last
                c.mm_group(fn, reads=[UBT.reg(), UTT.reg(), AB.reg(), AT.reg()],
                           writes=[preg], tag="a_gmm")
                sl = OP5.v3(NT, T)[:, iM, nch * 512:(nch + 1) * 512]
                c.tt(sl, pt.ap()[:, :], rc.sl(nch * 512, 512), OP.mult,
                     reads=[preg, rc.reg()], writes=[OP5.reg()], tag="a_gev1")
                c.ts(sl, sl, av.sl(2 + NT + iM, 1), None, OP.add, OP.bypass,
                     reads=[OP5.reg(), av.reg()], writes=[OP5.reg()], tag="a_gev2")
        WoT3 = c.dram[pfx + "WoT"].ap().rearrange("(k p) m -> p k m", p=128)
        for iM in range(NT):
            wb = wst[iM % 2]
            c.dma("gpsimd", wb._mk(0, [[wb.ncols, 128], [128, NT], [1, 128]]),
                  WoT3[:, :, iM * 128:(iM + 1) * 128],
                  reads=[(pfx + "WoT",)], writes=[wb.reg()], tag="a_ldwo")
            for nch in range(NCH):
                pt, preg = next_psum(c)
                def fn(eng, pt=pt, wb=wb, nch=nch):
                    last = None
                    for kc in range(NT):
                        last = eng.matmul(pt.ap()[:, :], wb.sl(kc * 128, 128),
                                          OP5.v3(NT, T)[:, kc, nch * 512:(nch + 1) * 512],
                                          start=(kc == 0), stop=(kc == NT - 1))
                    return last
                c.mm_group(fn, reads=[wb.reg(), OP5.reg()], writes=[preg], tag="a_omm")
                c.dma("sync", xc5.v3(1, 512, off=iM * 512),
                      xin4[:, iM, b, nch * 512:(nch + 1) * 512],
                      reads=[(XI,)], writes=[xc5.reg(iM)], tag="a_ldres")
                c.tt(xc5.v3(1, 512, off=iM * 512), pt.ap()[:, :],
                     xc5.v3(1, 512, off=iM * 512), OP.add,
                     reads=[preg, xc5.reg(iM)], writes=[xc5.reg(iM)], tag="a_res")
                c.dma("sync", xout4[:, iM, b, nch * 512:(nch + 1) * 512],
                      xc5.v3(1, 512, off=iM * 512),
                      reads=[xc5.reg(iM)], writes=[(XO,)], tag="a_st")


def build_prog_l1_attn(D=1024, T=2048, B=2):
    nc = bass.Bass("TRN2", target_bir_lowering=False, debug=False)
    c = Ctx(nc, D=D, T=T, B=B)
    c.dram = {}
    R = 256
    xin = _decl(nc, c, "xin", [B, D, T])
    xout = _decl(nc, c, "xout", [B, D, T], kind="ExternalOutput")
    for nm in ("Wq", "Wk"):
        _decl(nc, c, "L1_attn_" + nm, [D, D])
    _decl(nc, c, "L1_attn_WoT", [D, D])
    for nm in ("V_b", "W_b", "V_t", "W_t", "X_t"):
        _decl(nc, c, "L1_attn_" + nm, [D, R])
    _decl(nc, c, "L1_attn_U_bT", [R, D])
    _decl(nc, c, "L1_attn_U_tT", [R, D])
    _decl(nc, c, "L1_attn_bias_b", [128, D // 128])
    _decl(nc, c, "L1_attn_bias_t", [128, D // 128])
    _decl(nc, c, "L1_attn_alpha", [1, 1])
    setup_common(c)
    build_l1_attn(c, xin, xout)
    c.p.fence()
    c.p.emit()
    global _LAST_PROG
    _LAST_PROG = c.p
    return nc


def host_l1_attn(I, D=1024):
    NT = D // 128
    m = {}
    for nm in ("Wq", "Wk", "V_b", "W_b", "V_t", "W_t", "X_t"):
        m["L1_attn_" + nm] = I["L1_attn_" + nm]
    m["L1_attn_WoT"] = np.ascontiguousarray(I["L1_attn_Wo"].T)
    m["L1_attn_U_bT"] = np.ascontiguousarray(I["L1_attn_U_b"].T)
    m["L1_attn_U_tT"] = np.ascontiguousarray(I["L1_attn_U_t"].T)
    m["L1_attn_bias_b"] = _vecprep(I["L1_attn_bias_b"], NT)
    m["L1_attn_bias_t"] = _vecprep(I["L1_attn_bias_t"], NT)
    m["L1_attn_alpha"] = np.asarray(I["L1_attn_alpha"], np.float32).reshape(1, 1)
    return m



def build_l0_ffn_a(c: Ctx, xin, e_out, s_out, xn_out):
    XI = xin.ap().tensor.name; EO = e_out.ap().tensor.name; SO = s_out.ap().tensor.name; XNO = xn_out.ap().tensor.name
    NT, T, D, B = c.NT, c.T, c.D, c.B
    NP = c.NP
    NPC = NP // 512

    A2 = mkplane(c, "f_A2", NT * T, BF16)
    SQ2 = mkplane(c, "f_SQ2", NT * T, BF16)
    mb2 = mkplane(c, "f_mb2", T, F32)
    rb2 = mkplane(c, "f_rb2", T, F32)
    rows2_t = mkplane(c, "f_rows2", T, F32)
    phip = mkplane(c, "f_phi", NT * NP, BF16)
    est = mkplane(c, "f_est", NP, BF16)
    sa = mkplane(c, "f_sa", NP, BF16)
    sb2 = mkplane(c, "f_sb", NP, F32)

    rows2 = [Row(rows2_t, 0), Row(rows2_t, 32), Row(sb2, 0, 0), Row(sb2, 32, 0)]

    xin4 = xin.ap().rearrange("b (n p) t -> p n b t", p=128)
    c.dma("gpsimd", phip._mk(0, [[phip.ncols, 128], [NP, NT], [1, NP]]),
          c.dram["L0_ffn_phi"].ap().rearrange("(n p) q -> p n q", p=128),
          reads=[("L0_ffn_phi",)], writes=[phip.reg()], tag="f_ldphi")

    MT = T // 128
    for b in range(B):
        c.dma("gpsimd", A2.v3(NT, T), xin4[:, :, b, :], reads=[(XI,)],
              writes=[A2.reg()], tag="f_ldx")
        ln_cmajor(c, A2, A2.v3(NT, T), A2, A2.v3(NT, T), SQ2, mb2, rb2, rows2,
                  tag="f_ln")
        c.dma("sync", xn_out.ap()[b].rearrange("(n p) t -> p n t", p=128),
              A2.v3(NT, T), reads=[A2.reg()], writes=[(XNO,)], tag="f_stxn")
        for mt in range(MT):
            for npc in range(NPC):
                pt, preg = next_psum(c)
                def fn(eng, pt=pt, mt=mt, npc=npc):
                    last = None
                    for kc in range(NT):
                        lhsT = A2._mk(kc * T + mt * 128, [[A2.ncols, 128], [1, 128]])
                        last = eng.matmul(pt.ap()[:, :], lhsT,
                                          phip._mk(kc * NP + npc * 512, [[phip.ncols, 128], [1, 512]]),
                                          start=(kc == 0), stop=(kc == NT - 1))
                    return last
                c.mm_group(fn, reads=[A2.reg(), phip.reg()], writes=[preg], tag="f_lgmm")
                c.act(est.sl(npc * 512, 512), pt.ap()[:, :], AF.Exp,
                      reads=[preg], writes=[est.reg()], tag="f_exp")
            c.dma("sync", e_out.ap()[b, mt * 128:(mt + 1) * 128, :], est.sl(0, NP),
                  reads=[est.reg()], writes=[(EO,)], tag="f_ste")
    for mt in range(MT):
        c.dma("sync", est.sl(0, NP), e_out.ap()[0, mt * 128:(mt + 1) * 128, :],
              reads=[(EO,)], writes=[est.reg()], tag="f_lde0")
        c.dma("sync", sa.sl(0, NP), e_out.ap()[1, mt * 128:(mt + 1) * 128, :],
              reads=[(EO,)], writes=[sa.reg()], tag="f_lde1")
        c.tt(sb2.sl(0, NP), est.sl(0, NP), sa.sl(0, NP), OP.add,
             reads=[est.reg(), sa.reg()], writes=[sb2.reg()], tag="f_sadd")
        c.dma("sync", s_out.ap()[mt * 128:(mt + 1) * 128, :], sb2.sl(0, NP),
              reads=[sb2.reg()], writes=[(SO,)], tag="f_sts")


def build_l0_ffn_b(c: Ctx, xin, e_in, xn_in, s_all, xout, n_parts=8):
    XI = xin.ap().tensor.name; XO = xout.ap().tensor.name; EI = e_in.ap().tensor.name; XNI = xn_in.ap().tensor.name; SA = s_all.ap().tensor.name
    need_ident_bf(c)
    """Phase B: S = sum(s_all); D = E/S; slots; y; out; x2 = x1 + out."""
    NT, T, D, B = c.NT, c.T, c.D, c.B
    NP = c.NP
    NN = NP // 32
    NPC = NP // 512
    MT = T // 128

    xn2 = mkplane(c, "g_xn", NT * T, BF16)
    xnT = mkplane(c, "g_xnT", MT * D, BF16)
    slT = mkplane(c, "g_slT", NT * NP, BF16)
    ysb = mkplane(c, "g_y", (NP // 128) * D, BF16)
    rS = [mkplane(c, "g_rS0", NP, F32), mkplane(c, "g_rS1", NP, F32)]
    rcz = mkplane(c, "g_rcz", 256, F32)
    zr = mkplane(c, "g_zr", 256, F32, npart=1)
    wnb = [mkplane(c, f"g_wn{j}", NT * 512, BF16) for j in range(4)]
    bbc = [mkplane(c, "g_bb0", D, BF16), mkplane(c, "g_bb1", D, BF16)]
    ecb = [mkplane(c, "g_ec0", 256, BF16), mkplane(c, "g_ec1", 256, BF16)]
    etb = [mkplane(c, f"g_et{j}", 512, BF16) for j in range(4)]
    ebig = [mkplane(c, "g_eb0", NP, BF16), mkplane(c, "g_eb1", NP, BF16)]
    xch = mkplane(c, "g_xch", NT * 256, F32)
    otmp = mkplane(c, "g_otmp", 256, F32)
    pout = c.ps("g_pout", [128, 2048])
    ptb = c.ps("g_ptb", [128, 128], BF16)

    xin4 = xin.ap().rearrange("b (n p) t -> p n b t", p=128)
    xout4 = xout.ap().rearrange("b (n p) t -> p n b t", p=128)
    sred = c.dram["g_sred"]
    d_dram = c.dram["g_D"]
    precast_weights(c, ["L0_ffn_W"], slT, slT.ncols // 2)

    for mt in range(MT):
        c.dma("sync", rS[0].sl(0, NP), s_all.ap()[0, mt * 128:(mt + 1) * 128, :],
              reads=[(SA,)], writes=[rS[0].reg()], tag="g_lds")
        for i in range(1, n_parts):
            c.dma("sync", rS[1].sl(0, NP), s_all.ap()[i, mt * 128:(mt + 1) * 128, :],
                  reads=[(SA,)], writes=[rS[1].reg()], tag="g_lds2")
            c.tt(rS[0].sl(0, NP), rS[0].sl(0, NP), rS[1].sl(0, NP), OP.add,
                 reads=[rS[0].reg(), rS[1].reg()], writes=[rS[0].reg()], tag="g_sacc")
        c.recip(rS[0].sl(0, NP), rS[0].sl(0, NP), reads=[rS[0].reg()],
                writes=[rS[0].reg()], tag="g_srec")
        for b in range(B):
            eb = ebig[b % 2]
            c.dma("sync", eb.sl(0, NP), e_in.ap()[b, mt * 128:(mt + 1) * 128, :],
                  reads=[(EI,)], writes=[eb.reg()], tag="g_dld")
            c.tt(eb.sl(0, NP), eb.sl(0, NP), rS[0].sl(0, NP), OP.mult,
                 reads=[eb.reg(), rS[0].reg()], writes=[eb.reg()], tag="g_dmul")
            c.dma("sync", d_dram.ap()[b, mt * 128:(mt + 1) * 128, :], eb.sl(0, NP),
                  reads=[eb.reg()], writes=[("g_D",)], tag="g_dst")

    for b in range(B):
        c.dma("sync", xn2.v3(NT, T), xn_in.ap()[b].rearrange("(n p) t -> p n t", p=128),
              reads=[(XNI,)], writes=[xn2.reg()], tag="g_ldxn")
        for mt in range(MT):
            for kc in range(NT):
                ptv = ptb.ap()[:, :]
                src = xn2._mk(kc * T + mt * 128, [[xn2.ncols, 128], [1, 128]])
                def fn(eng, ptv=ptv, src=src):
                    return eng.transpose(ptv, src, c.ident_bf.sl(0, 128))
                c.p.op("tensor", fn, reads=[xn2.reg(), c.ident_bf.reg()],
                       writes=[("g_ptb",)], tag="g_xtr")
                c.copy(xnT._mk(mt * D + kc * 128, [[xnT.ncols, 128], [1, 128]]), ptv,
                       reads=[("g_ptb",)], writes=[xnT.reg()], tag="g_xtrev",
                       engine="vector")
        e3 = d_dram.ap()[b].rearrange("(q p) r -> p q r", p=128)
        for grp in range(2):
            for npc in range(NPC):
                for mt in range(MT):
                    eb = etb[mt % 4]
                    c.dma("sync", eb.sl(0, 512), e3[:, mt, npc * 512:(npc + 1) * 512],
                          reads=[("g_D",)], writes=[eb.reg()], tag="g_ldec2")
                    def fn(eng, eb=eb, grp=grp, mt=mt):
                        last = None
                        for dj in range(NT // 2):
                            dM = grp * (NT // 2) + dj
                            lhsT = xnT._mk(mt * D + dM * 128, [[xnT.ncols, 128], [1, 128]])
                            last = eng.matmul(pout.ap()[:, dj * 512:(dj + 1) * 512],
                                              lhsT, eb.sl(0, 512),
                                              start=(mt == 0), stop=(mt == MT - 1))
                        return last
                    c.mm_group(fn, reads=[xnT.reg(), eb.reg()], writes=[("g_pout",)],
                               tag="g_slmm")
                for dj in range(NT // 2):
                    dM = grp * (NT // 2) + dj
                    c.copy(slT._mk(dM * NP + npc * 512, [[slT.ncols, 128], [1, 512]]),
                           pout.ap()[:, dj * 512:(dj + 1) * 512],
                           reads=[("g_pout",)], writes=[slT.reg()], tag="g_slev",
                           engine="scalar")
        W3 = c.dram["bf_L0_ffn_W"].ap().rearrange("n (k p) m -> p n k m", p=128)
        for n in range(NN):
            bb = bbc[n % 2]
            j, base = n // 4, (n % 4) * 32
            c.dma("gpsimd", bb._mk(base * bb.ncols, [[bb.ncols, 32], [1, D]]),
                  c.dram["L0_ffn_bbc"].ap()[n],
                  reads=[("L0_ffn_bbc",)], writes=[bb.reg()], tag="g_ldb")
            DW = min(512, D)
            for dc in range(D // DW):
                wb = wnb[(n * (D // DW) + dc) % 4]
                c.dma("sync", wb._mk(0, [[wb.ncols, 128], [DW, NT], [1, DW]]),
                      W3[:, n, :, dc * DW:(dc + 1) * DW],
                      reads=[("bf_L0_ffn_W",)], writes=[wb.reg()], tag="g_ldW")
                pt2, preg2 = next_psum(c)
                def fn(eng, pt2=pt2, n=n, wb=wb, dc=dc, DW=DW):
                    last = None
                    for kc in range(NT):
                        lhsT = slT._mk(kc * NP + n * 32, [[slT.ncols, 128], [1, 32]])
                        last = eng.matmul(pt2.ap()[0:32, 0:DW], lhsT,
                                          wb._mk(kc * DW, [[wb.ncols, 128], [1, DW]]),
                                          start=(kc == 0), stop=(kc == NT - 1))
                    return last
                c.mm_group(fn, reads=[slT.reg(), wb.reg()], writes=[preg2], tag="g_ymm")
                dst = ysb._mk(base * ysb.ncols + j * D + dc * DW, [[ysb.ncols, 32], [1, DW]])
                c.tt(dst, pt2.ap()[0:32, 0:DW],
                     bb._mk(base * bb.ncols + dc * DW, [[bb.ncols, 32], [1, DW]]),
                     OP.add, reads=[preg2, bb.reg()], writes=[ysb.reg()], tag="g_yev")
        prow = c.psum_pool[0]
        prreg = (prow.ap().tensor.name,)
        for mc in range(T // 256):
            for npc in range(NP // 128):
                eb = ecb[npc % 2]
                def fn(eng, eb=eb, b=b, mc=mc, npc=npc):
                    return eng.dma_start(out=eb.sl(0, 256),
                                         in_=e_in.ap()[b, mc * 256:(mc + 1) * 256,
                                                       npc * 128:(npc + 1) * 128],
                                         transpose=True)
                c.p.op("sync", fn, reads=[(EI,)], writes=[eb.reg()], tag="g_ldec",
                       kind="dma")
                def fn(eng, eb=eb, npc=npc):
                    last = None
                    for dM in range(NT):
                        st = (npc == 0) and (dM % 2 == 0)
                        sp = (npc == NP // 128 - 1) and (dM % 2 == 1)
                        last = eng.matmul(pout.ap()[:, dM * 256:(dM + 1) * 256], lhsT_f(npc, dM),
                                          eb.sl(0, 256), start=st, stop=sp)
                    return last
                def lhsT_f(npc, dM):
                    return ysb._mk(npc * D + dM * 128, [[ysb.ncols, 128], [1, 128]])
                c.mm_group(fn, reads=[ysb.reg(), eb.reg()], writes=[("g_pout",)],
                           tag="g_omm")
                def fn(eng, eb=eb, npc=npc, prow=prow):
                    return eng.matmul(prow.ap()[0:1, 0:256], c.ones_col.sl(0, 1),
                                      eb.sl(0, 256), start=(npc == 0),
                                      stop=(npc == NP // 128 - 1))
                c.mm_group(fn, reads=[c.ones_col.reg(), eb.reg()], writes=[prreg],
                           tag="g_zmm")
            c.copy(zr.sl(0, 256, np_=1), prow.ap()[0:1, 0:256], reads=[prreg],
                   writes=[zr.reg()], tag="g_zev", engine="scalar")
            c.recip(zr.sl(0, 256, np_=1), zr.sl(0, 256, np_=1), reads=[zr.reg()],
                    writes=[zr.reg()], tag="g_zrec")
            pt3 = c.psum_pool[1]
            preg3 = (pt3.ap().tensor.name,)
            def fn(eng, pt3=pt3):
                return eng.matmul(pt3.ap()[:, 0:256], c.ones128._mk(0, [[128, 1], [1, 128]]),
                                  zr.sl(0, 256, np_=1), start=True, stop=True)
            c.mm_group(fn, reads=[zr.reg(), c.ones128.reg()], writes=[preg3], tag="g_zbc")
            c.copy(rcz.sl(0, 256), pt3.ap()[:, 0:256], reads=[preg3],
                   writes=[rcz.reg()], tag="g_zbcev", engine="scalar")
            c.dma("sync", xch.v3(NT, 256),
                  xin4[:, :, b, mc * 256:(mc + 1) * 256],
                  reads=[(XI,)], writes=[xch.reg()], tag="g_ldx1")
            for dM in range(NT):
                c.tt(otmp.sl(0, 256), pout.ap()[:, dM * 256:(dM + 1) * 256],
                     rcz.sl(0, 256), OP.mult,
                     reads=[("g_pout",), rcz.reg()], writes=[otmp.reg()], tag="g_oev1")
                sl = xch.v3(NT, 256)[:, dM, :]
                c.tt(sl, otmp.sl(0, 256), sl, OP.add,
                     reads=[otmp.reg(), xch.reg()], writes=[xch.reg()], tag="g_oev2")
            c.dma("sync", xout4[:, :, b, mc * 256:(mc + 1) * 256], xch.v3(NT, 256),
                  reads=[xch.reg()], writes=[(XO,)], tag="g_stx2")


def build_prog_l0_ffn_a(D=1024, T=2048, B=2, NP=2048):
    nc = bass.Bass("TRN2", target_bir_lowering=False, debug=False)
    c = Ctx(nc, D=D, T=T, B=B)
    c.dram = {}
    c.NP = NP
    xin = _decl(nc, c, "xin", [B, D, T])
    _decl(nc, c, "L0_ffn_phi", [D, NP])
    e_out = _decl(nc, c, "e_out", [B, T, NP], kind="ExternalOutput", dtype=BF16)
    s_out = _decl(nc, c, "s_out", [T, NP], kind="ExternalOutput")
    xn_out = _decl(nc, c, "xn_out", [B, D, T], kind="ExternalOutput", dtype=BF16)
    setup_common(c)
    build_l0_ffn_a(c, xin, e_out, s_out, xn_out)
    c.p.fence()
    c.p.emit()
    global _LAST_PROG
    _LAST_PROG = c.p
    return nc


def build_prog_l0_ffn_b(D=1024, T=2048, B=2, NP=2048, n_parts=8):
    nc = bass.Bass("TRN2", target_bir_lowering=False, debug=False)
    c = Ctx(nc, D=D, T=T, B=B)
    c.dram = {}
    c.NP = NP
    NN = NP // 32
    xin = _decl(nc, c, "xin", [B, D, T])
    e_in = _decl(nc, c, "e_in", [B, T, NP], dtype=BF16)
    xn_in = _decl(nc, c, "xn_in", [B, D, T], dtype=BF16)
    s_all = _decl(nc, c, "s_all", [n_parts, T, NP])
    xout = _decl(nc, c, "xout", [B, D, T], kind="ExternalOutput")
    _decl(nc, c, "L0_ffn_W", [NN, D, D])
    _decl(nc, c, "bf_L0_ffn_W", [NN, D, D], kind="Internal", dtype=BF16)
    _decl(nc, c, "L0_ffn_bbc", [NN, 32, D])
    _decl(nc, c, "g_sred", [T, NP], kind="Internal")
    _decl(nc, c, "g_D", [B, T, NP], kind="Internal", dtype=BF16)
    setup_common(c, npsum=3)
    build_l0_ffn_b(c, xin, e_in, xn_in, s_all, xout, n_parts=n_parts)
    c.p.fence()
    c.p.emit()
    global _LAST_PROG
    _LAST_PROG = c.p
    return nc


def host_l0_ffn(I, D=1024):
    phi = np.asarray(I["L0_ffn_phi"])
    NN = phi.shape[1]
    m = {
        "L0_ffn_phi": np.ascontiguousarray(phi.reshape(D, -1)),
        "L0_ffn_W": I["L0_ffn_W"],
        "L0_ffn_bbc": np.ascontiguousarray(
            np.broadcast_to(np.asarray(I["L0_ffn_b"])[:, None, :], (NN, 32, D))),
    }
    return m



_PROGS = {}


def _get_prog(name, builder):
    if name not in _PROGS:
        _PROGS[name] = builder()
    return _PROGS[name]


def kernel5(**I):
    D, T, NP = 1024, 2048, 2048
    BT, BPC = 16, 2
    x = np.asarray(I["L0_attn_x"], np.float32)

    xs = [np.ascontiguousarray(x[c * BPC:(c + 1) * BPC].transpose(0, 2, 1))
          for c in range(NCORES)]

    nc1 = _get_prog("l0a", lambda: build_prog_l0_attn(D=D, T=T, B=BPC))
    w1 = host_l0_attn(I, D=D)
    maps = [dict(w1, xin=xs[c]) for c in range(NCORES)]
    r1 = run_bass_kernel_spmd(nc1, maps, core_ids=list(range(NCORES)))
    x1 = [r["xout"] for r in r1.results]

    nc2 = _get_prog("l0fa", lambda: build_prog_l0_ffn_a(D=D, T=T, B=BPC, NP=NP))
    wf = host_l0_ffn(I, D=D)
    maps = [{"xin": x1[c], "L0_ffn_phi": wf["L0_ffn_phi"]} for c in range(NCORES)]
    r2 = run_bass_kernel_spmd(nc2, maps, core_ids=list(range(NCORES)))
    S_all = np.stack([r["s_out"] for r in r2.results])

    nc3 = _get_prog("l0fb", lambda: build_prog_l0_ffn_b(D=D, T=T, B=BPC, NP=NP,
                                                        n_parts=NCORES))
    maps = [{"xin": x1[c], "e_in": r2.results[c]["e_out"],
             "xn_in": r2.results[c]["xn_out"], "s_all": S_all,
             "L0_ffn_W": wf["L0_ffn_W"], "L0_ffn_bbc": wf["L0_ffn_bbc"]}
            for c in range(NCORES)]
    r3 = run_bass_kernel_spmd(nc3, maps, core_ids=list(range(NCORES)))
    x2 = [r["xout"] for r in r3.results]

    nc4 = _get_prog("l1a", lambda: build_prog_l1_attn(D=D, T=T, B=BPC))
    w4 = host_l1_attn(I, D=D)
    maps = [dict(w4, xin=x2[c]) for c in range(NCORES)]
    r4 = run_bass_kernel_spmd(nc4, maps, core_ids=list(range(NCORES)))
    x3 = [r["xout"] for r in r4.results]

    nc5 = _get_prog("l1f", lambda: build_prog_l1_ffn(D=D, T=T, B=BPC))
    w5 = host_l1_ffn(I, D=D)
    maps = [dict(w5, xin=x3[c]) for c in range(NCORES)]
    r5 = run_bass_kernel_spmd(nc5, maps, core_ids=list(range(NCORES)))

    out = np.concatenate([r["xout"].transpose(0, 2, 1) for r in r5.results], axis=0)
    return np.ascontiguousarray(out.astype(np.float32))



def build_prog_launchA(D=1024, T=2048, B=2, NP=2048):
    nc = bass.Bass("TRN2", target_bir_lowering=False, debug=False)
    c = Ctx(nc, D=D, T=T, B=B)
    c.dram = {}
    c.NP = NP
    NT = c.NT
    HID = 4 * D
    xin = _decl(nc, c, "xin", [B, D, T])
    x1 = _decl(nc, c, "x1", [B, D, T], kind="ExternalOutput")
    for nm in ("WkT", "WvT", "WrT", "WoT", "fWrT"):
        _decl(nc, c, "L0_attn_" + nm, [D, D])
        _decl(nc, c, "bf_L0_attn_" + nm, [D, D], kind="Internal", dtype=BF16)
    _decl(nc, c, "L0_attn_fWkT", [D, HID])
    _decl(nc, c, "bf_L0_attn_fWkT", [D, HID], kind="Internal", dtype=BF16)
    _decl(nc, c, "L0_attn_fWvT", [HID, D])
    _decl(nc, c, "bf_L0_attn_fWvT", [HID, D], kind="Internal", dtype=BF16)
    for nm in ("ln1_g", "ln1_b", "ln2_g", "ln2_b", "tmk", "tmv", "tmr",
               "f_tmk", "f_tmr", "time_decay", "time_first"):
        _decl(nc, c, "L0_attn_" + nm, [128, NT])
    _decl(nc, c, "L0_ffn_phi", [D, NP])
    e_out = _decl(nc, c, "e_out", [B, T, NP], kind="ExternalOutput", dtype=BF16)
    s_out = _decl(nc, c, "s_out", [T, NP], kind="ExternalOutput")
    xn_out = _decl(nc, c, "xn_out", [B, D, T], kind="ExternalOutput", dtype=BF16)
    setup_common(c, npsum=0)
    c.push_scope(4)
    build_l0_attn(c, xin, x1, None)
    c.pop_scope()
    c.push_scope(4)
    build_l0_ffn_a(c, x1, e_out, s_out, xn_out)
    c.pop_scope()
    c.p.fence()
    c.p.emit()
    global _LAST_PROG
    _LAST_PROG = c.p
    return nc


def build_prog_launchB(D=1024, T=2048, B=2, NP=2048, n_parts=8):
    nc = bass.Bass("TRN2", target_bir_lowering=False, debug=False)
    c = Ctx(nc, D=D, T=T, B=B)
    c.dram = {}
    c.NP = NP
    NN = NP // 32
    R = 256
    F = 4 * D
    x1 = _decl(nc, c, "x1", [B, D, T])
    e_in = _decl(nc, c, "e_in", [B, T, NP], dtype=BF16)
    xn_in = _decl(nc, c, "xn_in", [B, D, T], dtype=BF16)
    s_all = _decl(nc, c, "s_all", [n_parts, T, NP])
    x2 = _decl(nc, c, "x2", [B, D, T], kind="Internal")
    x3 = _decl(nc, c, "x3", [B, D, T], kind="Internal")
    xout = _decl(nc, c, "xout", [B, D, T], kind="ExternalOutput")
    _decl(nc, c, "L0_ffn_W", [NN, D, D])
    _decl(nc, c, "bf_L0_ffn_W", [NN, D, D], kind="Internal", dtype=BF16)
    _decl(nc, c, "L0_ffn_bbc", [NN, 32, D])
    _decl(nc, c, "g_sred", [T, NP], kind="Internal")
    _decl(nc, c, "g_D", [B, T, NP], kind="Internal", dtype=BF16)
    for nm in ("Wq", "Wk"):
        _decl(nc, c, "L1_attn_" + nm, [D, D])
    _decl(nc, c, "L1_attn_WoT", [D, D])
    for nm in ("V_b", "W_b", "V_t", "W_t", "X_t"):
        _decl(nc, c, "L1_attn_" + nm, [D, R])
    _decl(nc, c, "L1_attn_U_bT", [R, D])
    _decl(nc, c, "L1_attn_U_tT", [R, D])
    _decl(nc, c, "L1_attn_bias_b", [128, D // 128])
    _decl(nc, c, "L1_attn_bias_t", [128, D // 128])
    _decl(nc, c, "L1_attn_alpha", [1, 1])
    _decl(nc, c, "L1_ffn_w_red", [D, 16])
    _decl(nc, c, "L1_ffn_wg", [8, 16])
    _decl(nc, c, "L1_ffn_w1", [8, D, F])
    _decl(nc, c, "bf_L1_ffn_w1", [8, D, F], kind="Internal", dtype=BF16)
    _decl(nc, c, "L1_ffn_b1c", [128, 8 * (F // 128)])
    _decl(nc, c, "L1_ffn_w2", [8, F, D])
    _decl(nc, c, "bf_L1_ffn_w2", [8, F, D], kind="Internal", dtype=BF16)
    _decl(nc, c, "L1_ffn_b2", [8, D])
    _decl(nc, c, "m_scr16", [1, 16], kind="Internal")
    _decl(nc, c, "m_scr8x16", [8, 16], kind="Internal")
    setup_common(c, npsum=0)
    c.push_scope(3)
    build_l0_ffn_b(c, x1, e_in, xn_in, s_all, x2, n_parts=n_parts)
    c.pop_scope()
    c.push_scope(4)
    build_l1_attn(c, x2, x3)
    c.pop_scope()
    c.push_scope(6)
    build_l1_ffn(c, x3, xout)
    c.pop_scope()
    c.p.fence()
    c.p.emit()
    global _LAST_PROG
    _LAST_PROG = c.p
    return nc


def kernel(**I):
    D, T, NP = 1024, 2048, 2048
    BPC = 2
    x = np.asarray(I["L0_attn_x"], np.float32)
    xs = [np.ascontiguousarray(x[c * BPC:(c + 1) * BPC].transpose(0, 2, 1))
          for c in range(NCORES)]
    wa = host_l0_attn(I, D=D)
    wf = host_l0_ffn(I, D=D)
    w4 = host_l1_attn(I, D=D)
    w5 = host_l1_ffn(I, D=D)

    ncA = _get_prog("A", lambda: build_prog_launchA(D=D, T=T, B=BPC, NP=NP))
    maps = [dict(wa, xin=xs[c], L0_ffn_phi=wf["L0_ffn_phi"]) for c in range(NCORES)]
    rA = run_bass_kernel_spmd(ncA, maps, core_ids=list(range(NCORES)))
    S_all = np.stack([r["s_out"] for r in rA.results])

    ncB = _get_prog("B", lambda: build_prog_launchB(D=D, T=T, B=BPC, NP=NP,
                                                    n_parts=NCORES))
    maps = []
    for c in range(NCORES):
        m = dict(w4)
        m.update(w5)
        m.update(L0_ffn_W=wf["L0_ffn_W"], L0_ffn_bbc=wf["L0_ffn_bbc"],
                 x1=rA.results[c]["x1"], e_in=rA.results[c]["e_out"],
                 xn_in=rA.results[c]["xn_out"], s_all=S_all)
        maps.append(m)
    rB = run_bass_kernel_spmd(ncB, maps, core_ids=list(range(NCORES)))
    out = np.concatenate([r["xout"].transpose(0, 2, 1) for r in rB.results], axis=0)
    return np.ascontiguousarray(out.astype(np.float32))


# version 06 (final)
# speedup vs baseline: 1.0378x; 1.0378x over previous
import numpy as np
import concourse.bass as bass
import concourse.mybir as mybir
from concourse.bass_utils import run_bass_kernel_spmd

F32 = mybir.dt.float32
BF16 = mybir.dt.bfloat16
AF = mybir.ActivationFunctionType
OP = mybir.AluOpType
AX = mybir.AxisListType

NCORES = 8


ENGINES = ("sync", "gpsimd", "tensor", "vector", "scalar")
_W = {"sync": 16, "gpsimd": 16, "tensor": 1, "vector": 1, "scalar": 1}


class Prog:

    def __init__(self, nc):
        self.nc = nc
        self.ops = []
        self.eng_ops = {e: [] for e in ENGINES}
        self._writers = {}
        self._readers = {}
        self._keys = {}
        self.inst_tags = {}
        self._gq_count = 0

    @staticmethod
    def _base(r):
        return r[0] if isinstance(r, tuple) else r

    def _alias(self, r):
        base = self._base(r)
        seen = self._keys.setdefault(base, set())
        seen.add(r)
        if r == base:
            return list(seen)
        out = [r]
        if base in seen:
            out.append(base)
        return out

    @staticmethod
    def _norm(r):
        return r[0] if isinstance(r, tuple) and len(r) == 1 else r

    def op(self, engine, fn, reads=(), writes=(), tag="", kind="compute"):
        assert engine in ENGINES, engine
        if engine == "sync":
            kind = "dma"
        reads = [self._norm(r) for r in reads]
        writes = [self._norm(r) for r in writes]
        i = len(self.ops)
        deps = set()
        for r in reads:
            for k in self._alias(r):
                w = self._writers.get(k)
                if w is not None:
                    deps.add(w)
        wr_aliases = []
        for r in writes:
            aks = self._alias(r)
            wr_aliases.append((r, aks))
            for k in aks:
                w = self._writers.get(k)
                if w is not None:
                    deps.add(w)
                for rd in self._readers.get(k, ()):
                    deps.add(rd)
        deps.discard(i)
        for r in reads:
            self._readers.setdefault(r, []).append(i)
        for r, aks in wr_aliases:
            if r == self._base(r):
                for k in list(self._keys.get(r, ())):
                    self._writers[k] = i
                    self._readers[k] = []
            self._writers[r] = i
            self._readers[r] = []
        gq = self._gq_count if (engine == "gpsimd" and kind == "dma") else -1
        wm = -1
        for d in deps:
            o = self.ops[d]
            wm = max(wm, o[5] if o[6] < 0 else o[6])
        self.ops.append([engine, fn, deps, tag, kind, wm, gq])
        if gq >= 0:
            self._gq_count += 1
        self.eng_ops[engine].append(i)
        return i

    def barrier(self, tag="barrier"):
        lasts = [self.eng_ops[f][-1] for f in ENGINES if self.eng_ops[f]]
        for e in ENGINES:
            i = len(self.ops)
            deps = set(lasts)
            deps.discard(i)
            wm = -1
            for d in deps:
                o = self.ops[d]
                wm = max(wm, o[5] if o[6] < 0 else o[6])
            self.ops.append([e, (lambda eng: eng.nop()), deps, tag, "compute", wm, -1])
            self.eng_ops[e].append(i)

    def fence(self, engine="sync", tag="final_fence"):
        i = len(self.ops)
        deps = set()
        for f in ENGINES:
            if self.eng_ops[f]:
                deps.add(self.eng_ops[f][-1])
        deps.discard(i)
        wm = -1
        for d in deps:
            o = self.ops[d]
            wm = max(wm, o[5] if o[6] < 0 else o[6])
        self.ops.append([engine, (lambda eng: eng.nop()), deps, tag, "compute", wm, -1])
        self.eng_ops[engine].append(i)
        return i

    def emit(self, dma_depth=8):
        nc = self.nc
        n = len(self.ops)
        sem_names = [f"se_{e}" for e in ENGINES if e != "sync"]
        sem_names += [f"sq_sync{j}" for j in range(dma_depth)]
        sem_names += ["gq_mono"]
        cms, sems = [], {}
        for nm in sem_names:
            cm = nc.semaphore(nm)
            sems[nm] = cm.__enter__()
            cms.append(cm)
        comp = [None] * n
        throttle = [None] * n
        etick = {e: 0 for e in ENGINES}
        scnt = 0
        for i, (e, _, deps, _, kind, wm, gq) in enumerate(self.ops):
            if e == "sync":
                k = scnt
                scnt += 1
                snm = f"sq_sync{k % dma_depth}"
                comp[i] = (snm, 16 * (k // dma_depth + 1))
                if k >= dma_depth:
                    throttle[i] = (snm, 16 * (k // dma_depth))
            elif gq >= 0:
                comp[i] = ("gq_mono", 16 * (gq + 1))
                if gq > 0:
                    throttle[i] = ("gq_mono", 16 * gq)
            else:
                etick[e] += 1
                comp[i] = (f"se_{e}", etick[e])
        blk_cm = nc.Block()
        block = blk_cm.__enter__()
        prog = self

        def body_for(e):
            def body(eng):
                seen = {nm: 0 for nm in sem_names}
                for i in prog.eng_ops[e]:
                    _, fn, deps, tag, kind, wm, gq = prog.ops[i]
                    need = {}
                    for d in deps:
                        snm, v = comp[d]
                        if v > need.get(snm, 0):
                            need[snm] = v
                    if throttle[i] is not None:
                        snm, v = throttle[i]
                        if v > need.get(snm, 0):
                            need[snm] = v
                    for snm in sem_names:
                        v = need.get(snm, 0)
                        if v > seen[snm]:
                            eng.wait_ge(sems[snm], v)
                            seen[snm] = v
                    try:
                        inst = fn(eng)
                    except Exception as ex:
                        raise RuntimeError(f"emit failed at op tag={tag} engine={e}: {ex}") from ex
                    assert inst is not None, f"op fn returned None: {e} {tag}"
                    try:
                        prog.inst_tags[inst.ins.name] = (tag, i)
                    except Exception:
                        pass
                    snm, v = comp[i]
                    inst.then_inc(sems[snm], 16 if (e == "sync" or gq >= 0) else 1)
            return body

        for e in ENGINES:
            getattr(block, e)(body_for(e))
        blk_cm.__exit__(None, None, None)
        for cm in cms:
            cm.__exit__(None, None, None)



def ap3(t, off, dims):
    return bass.AP(t.ap().tensor, off, [list(d) for d in dims])


class Ctx:

    def __init__(self, nc, D=1024, T=2048, B=2):
        self.nc = nc
        self.p = Prog(nc)
        self.D, self.T, self.B = D, T, B
        self.NT = D // 128
        self.NN = T // 512
        self.bufs = {}
        self._uid = 0

    def uid(self):
        self._uid += 1
        return self._uid

    def sb(self, name, shape, dtype):
        if getattr(self, "_scopes", None):
            g = self.nc.sbuf_tensor(name, shape, dtype)
            t = g.__enter__()
            self._scopes[-1].append(g)
        else:
            t = self.nc.alloc_sbuf_tensor(name, shape, dtype)
        self.bufs[name] = t
        return t

    def ps(self, name, shape, dtype=F32):
        if getattr(self, "_scopes", None):
            g = self.nc.psum_tensor(name, shape, dtype)
            t = g.__enter__()
            self._scopes[-1].append(g)
        else:
            t = self.nc.alloc_psum_tensor(name, shape, dtype)
        self.bufs[name] = t
        return t

    def push_scope(self, npsum=4):
        if not hasattr(self, "_scopes"):
            self._scopes = []
        self._scopes.append([])
        u = self.uid()
        self.psum_pool = [self.ps(f"pp_s{u}_{j}", [128, 512]) for j in range(npsum)]
        self._psum_i = 0

    def pop_scope(self):
        self.p.barrier()
        for g in reversed(self._scopes.pop()):
            g.__exit__(None, None, None)

    def dma(self, q, dst, src, reads, writes, tag="dma"):
        self.p.op(q, (lambda eng, d=dst, s=src: eng.dma_start(out=d, in_=s)),
                  reads=reads, writes=writes, tag=tag, kind="dma")

    def memset(self, dst, val, writes, tag="memset", engine="vector"):
        self.p.op(engine, (lambda eng, d=dst, v=val: eng.memset(d, v)),
                  writes=writes, tag=tag)

    def tt(self, out, in0, in1, op, reads, writes, tag="tt", engine="vector"):
        self.p.op(engine,
                  (lambda eng, o=out, a=in0, b=in1, c=op: eng.tensor_tensor(out=o, in0=a, in1=b, op=c)),
                  reads=reads, writes=writes, tag=tag)

    def ts(self, out, in0, s1, s2, op0, op1, reads, writes, tag="ts", engine="vector"):
        def fn(eng, o=out, a=in0, s1=s1, s2=s2, op0=op0, op1=op1):
            return eng.tensor_scalar(out=o, in0=a, scalar1=s1, scalar2=s2, op0=op0, op1=op1)
        self.p.op(engine, fn, reads=reads, writes=writes, tag=tag)

    def stt(self, out, in0, scal, in1, op0, op1, reads, writes, tag="stt", engine="vector"):
        def fn(eng, o=out, a=in0, s=scal, b=in1, op0=op0, op1=op1):
            return eng.scalar_tensor_tensor(out=o, in0=a, scalar=s, in1=b, op0=op0, op1=op1)
        self.p.op(engine, fn, reads=reads, writes=writes, tag=tag)

    def act(self, out, in_, func, reads, writes, bias=0.0, scale=1.0, tag="act"):
        def fn(eng, o=out, i=in_, f=func, b=bias, s=scale):
            return eng.activation(o, i, f, bias=b, scale=s)
        self.p.op("scalar", fn, reads=reads, writes=writes, tag=tag)

    def copy(self, out, in_, reads, writes, tag="copy", engine="vector"):
        if engine == "scalar":
            fn = lambda eng, o=out, i=in_: eng.copy(o, i)
        else:
            fn = lambda eng, o=out, i=in_: eng.tensor_copy(o, i)
        self.p.op(engine, fn, reads=reads, writes=writes, tag=tag)

    def scan(self, out, data0, data1, initial, op0, op1, reads, writes, tag="scan"):
        def fn(eng, o=out, d0=data0, d1=data1, ini=initial, op0=op0, op1=op1):
            return eng.tensor_tensor_scan(out=o, data0=d0, data1=d1, initial=ini, op0=op0, op1=op1)
        self.p.op("vector", fn, reads=reads, writes=writes, tag=tag)

    def recip(self, out, in_, reads, writes, tag="recip"):
        self.p.op("vector", (lambda eng, o=out, i=in_: eng.reciprocal(o, i)),
                  reads=reads, writes=writes, tag=tag)

    def recip_pos(self, out, in_, reads, writes, tag="rcp"):
        self.act(out, in_, AF.Ln, reads=reads, writes=writes, tag=tag + "_ln")
        self.act(out, out, AF.Exp, scale=-1.0, reads=list(writes), writes=writes,
                 tag=tag + "_ex")

    def mm(self, out, lhsT, rhs, reads, writes, start=True, stop=True, tag="mm"):
        def fn(eng, o=out, l=lhsT, r=rhs, st=start, sp=stop):
            return eng.matmul(o, l, r, start=st, stop=sp)
        self.p.op("tensor", fn, reads=reads, writes=writes, tag=tag)

    def mm_group(self, fn, reads, writes, tag="mmg"):
        self.p.op("tensor", fn, reads=reads, writes=writes, tag=tag)

    def transpose(self, out, in_, ident, reads, writes, tag="tr"):
        def fn(eng, o=out, i=in_, d=ident):
            return eng.transpose(o, i, d)
        self.p.op("tensor", fn, reads=reads, writes=writes, tag=tag)



class Plane:

    def __init__(self, t, ncols, npart=128):
        self.t = t
        self.name = t.ap().tensor.name if hasattr(t.ap().tensor, "name") else t.name
        self.ncols = ncols
        self.npart = npart

    def _mk(self, off, dims):
        return bass.AP(self.t.ap().tensor, off, [list(d) for d in dims])

    def sl(self, c0, n, p0=0, np_=None):
        np_ = self.npart if np_ is None else np_
        return self._mk(p0 * self.ncols + c0, [[self.ncols, np_], [1, n]])

    def v3(self, ntile, tlen, stride=None, off=0):
        stride = tlen if stride is None else stride
        return self._mk(off, [[self.ncols, self.npart], [stride, ntile], [1, tlen]])

    def bc3(self, ntile, tlen, off=0):
        return self._mk(off, [[self.ncols, self.npart], [0, ntile], [1, tlen]])

    def reg(self, *key):
        return (self.name,) + key if key else (self.name,)


def mkplane(c, name, ncols, dtype, npart=128):
    return Plane(c.sb(name, [npart, ncols], dtype), ncols, npart)


class Row:

    def __init__(self, pl, base, col0=0):
        self.pl, self.base, self.col0 = pl, base, col0

    def sl(self, c0, n, np_=1):
        return self.pl._mk(self.base * self.pl.ncols + self.col0 + c0,
                           [[self.pl.ncols, np_], [1, n]])

    def reg(self, *k):
        return (self.pl.name, "row", self.base, self.col0) + k



def setup_common(c: Ctx, npsum=4):
    c.ones_col = mkplane(c, "ones_col", 1, BF16)
    c.ones128 = mkplane(c, "ones128", 128, F32)
    c.memset(c.ones_col.sl(0, 1), 1.0, writes=[c.ones_col.reg()], tag="init")
    c.memset(c.ones128.sl(0, 128), 1.0, writes=[c.ones128.reg()], tag="init")
    c.psum_pool = [c.ps(f"pp{j}", [128, 512]) for j in range(npsum)]
    c._psum_i = 0


def need_zero_col(c):
    if not hasattr(c, "zero_col"):
        c.zero_col = mkplane(c, "zero_col", 1, F32)
        c.memset(c.zero_col.sl(0, 1), 0.0, writes=[c.zero_col.reg()], tag="init")
    return c.zero_col


def need_ident32(c):
    if not hasattr(c, "ident32"):
        c.ident32 = mkplane(c, "ident32", 128, F32)
        c.memset(c.ident32.sl(0, 128), 1.0, writes=[c.ident32.reg()], tag="init")
        def fn(eng):
            return eng.affine_select(c.ident32.sl(0, 128), c.ident32.sl(0, 128),
                                     [[-1, 128]], OP.is_equal, 0.0, channel_multiplier=1)
        c.p.op("gpsimd", fn, reads=[c.ident32.reg()], writes=[c.ident32.reg()], tag="ident")
    return c.ident32


def need_ident_bf(c):
    if not hasattr(c, "ident_bf"):
        c.ident_bf = mkplane(c, "ident_bf", 128, BF16)
        c.memset(c.ident_bf.sl(0, 128), 1.0, writes=[c.ident_bf.reg()], tag="init")
        def fn(eng):
            return eng.affine_select(c.ident_bf.sl(0, 128), c.ident_bf.sl(0, 128),
                                     [[-1, 128]], OP.is_equal, 0.0, channel_multiplier=1)
        c.p.op("gpsimd", fn, reads=[c.ident_bf.reg()], writes=[c.ident_bf.reg()], tag="identb")
    return c.ident_bf


def need_e8(c):
    if not hasattr(c, "E8"):
        c.E8 = mkplane(c, "E8", 8 * 128, F32, npart=8)
        c.memset(c.E8.sl(0, 8 * 128, np_=8), 1.0, writes=[c.E8.reg()], tag="init")
        def fn(eng):
            ap = c.E8._mk(0, [[8 * 128, 8], [128, 8], [1, 128]])
            return eng.affine_select(ap, ap, [[-1, 8], [0, 128]], OP.is_equal, 0.0,
                                     channel_multiplier=1)
        c.p.op("gpsimd", fn, reads=[c.E8.reg()], writes=[c.E8.reg()], tag="e8")
    return c.E8


def next_psum(c):
    j = c._psum_i % len(c.psum_pool)
    c._psum_i += 1
    t = c.psum_pool[j]
    return t, (t.ap().tensor.name,)


def load_vec(c, name, dram_name, q="sync"):
    pl = mkplane(c, name, c.NT, F32)
    src = c.dram[dram_name]
    c.dma(q, pl.sl(0, c.NT), src.ap()[:, :], reads=[(dram_name,)], writes=[pl.reg()],
          tag=f"ld_{name}")
    return pl


def precast_weights(c, names, staging_pl, half_cols):
    i = 0
    for name in names:
        src = c.dram[name]
        dst = c.dram["bf_" + name]
        tot = 1
        for d in src.ap().shape:
            tot *= d
        hc = half_cols
        while hc > 128 and (tot % hc != 0 or (tot // hc) % 128 != 0):
            hc //= 2
        assert tot % hc == 0 and (tot // hc) % 128 == 0, (name, tot, half_cols)
        R = tot // hc
        for r0 in range(0, R, 128):
            h = i % 2
            stg = staging_pl.sl(h * half_cols, hc)
            sv = bass.AP(src.ap().tensor, r0 * hc, [[hc, 128], [1, hc]])
            dv = bass.AP(dst.ap().tensor, r0 * hc, [[hc, 128], [1, hc]])
            c.dma("gpsimd", stg, sv, reads=[(name,)], writes=[(staging_pl.name, h)],
                  tag="pc_ld")
            c.dma("sync", dv, stg, reads=[(staging_pl.name, h)],
                  writes=[("bf_" + name,)], tag="pc_st")
            i += 1



def ln_cmajor(c: Ctx, src_pl, src3, dst_pl, dst3, sq_pl, meanb, rstdb, rows,
              gcol=None, bcol=None, gb_regs=(), eps=1e-5, tag="ln", tlen=None,
              sq_parts=1, ntile=None):
    NT = c.NT if ntile is None else ntile
    T = c.T if tlen is None else tlen
    D = c.D
    r0, r1 = rows[0], rows[1]
    NCH = T // 512
    sq_parts = min(sq_parts, NCH)
    part_nch = NCH // sq_parts
    sq3 = lambda: sq_pl.v3(NT, T // sq_parts)
    for part in range(sq_parts):
        lo = part * part_nch
        c.act(sq_pl.v3(NT, part_nch * 512), src3[:, :, lo * 512:(lo + part_nch) * 512],
              AF.Square, reads=[src_pl.reg()], writes=[sq_pl.reg()], tag=tag + "_sq")
        for nch in range(lo, lo + part_nch):
            for row, rhs3, rbase in ((r0, src3[:, :, nch * 512:(nch + 1) * 512], src_pl.reg()),
                                     (r1, sq_pl.v3(NT, part_nch * 512)[:, :, (nch - lo) * 512:(nch - lo + 1) * 512],
                                      sq_pl.reg())):
                pt, preg = next_psum(c)
                def fn(eng, pt=pt, rhs3=rhs3):
                    last = None
                    for kc in range(NT):
                        last = eng.matmul(pt.ap()[0:1, :], c.ones_col.sl(0, 1),
                                          rhs3[:, kc, :],
                                          start=(kc == 0), stop=(kc == NT - 1))
                    return last
                c.mm_group(fn, reads=[rbase, c.ones_col.reg()], writes=[preg], tag=tag + "_red")
                c.copy(row.sl(nch * 512, 512, np_=1), pt.ap()[0:1, :],
                       reads=[preg], writes=[row.reg()], tag=tag + "_rowev", engine="scalar")
    a0, a32 = rows[2], rows[3]
    c.ts(r0.sl(0, T, np_=1), r0.sl(0, T, np_=1), 1.0 / D, None, OP.mult, OP.bypass,
         reads=[r0.reg()], writes=[r0.reg()], tag=tag + "_mean")
    c.ts(r1.sl(0, T, np_=1), r1.sl(0, T, np_=1), 1.0 / D, None, OP.mult, OP.bypass,
         reads=[r1.reg()], writes=[r1.reg()], tag=tag + "_msq")
    c.tt(a0.sl(0, T, np_=1), r0.sl(0, T, np_=1), r0.sl(0, T, np_=1), OP.mult,
         reads=[r0.reg()], writes=[a0.reg()], tag=tag + "_m2")
    c.dma("sync", a32.sl(0, T, np_=1), a0.sl(0, T, np_=1),
          reads=[a0.reg()], writes=[a32.reg()], tag=tag + "_m2cp")
    c.tt(r1.sl(0, T, np_=1), r1.sl(0, T, np_=1), a32.sl(0, T, np_=1), OP.subtract,
         reads=[r1.reg(), a32.reg()], writes=[r1.reg()], tag=tag + "_var")
    c.ts(r1.sl(0, T, np_=1), r1.sl(0, T, np_=1), float(eps), None, OP.add, OP.bypass,
         reads=[r1.reg()], writes=[r1.reg()], tag=tag + "_vare")
    c.act(a32.sl(0, T, np_=1), r1.sl(0, T, np_=1), AF.Ln, reads=[r1.reg()],
          writes=[a32.reg()], tag=tag + "_rv")
    c.act(a32.sl(0, T, np_=1), a32.sl(0, T, np_=1), AF.Exp, scale=-0.5,
          reads=[a32.reg()], writes=[a32.reg()], tag=tag + "_rstd")
    for row, dst in ((r0, meanb), (a32, rstdb)):
        for nch in range(NCH):
            pt, preg = next_psum(c)
            def fn(eng, pt=pt, row=row, nch=nch):
                ones_v = c.ones128._mk(row.base * 128, [[128, 1], [1, 128]])
                return eng.matmul(pt.ap()[:, :], ones_v,
                                  row.sl(nch * 512, 512, np_=1), start=True, stop=True)
            c.mm_group(fn, reads=[row.reg(), c.ones128.reg()], writes=[preg], tag=tag + "_bc")
            c.copy(dst.sl(nch * 512, 512), pt.ap()[:, :], reads=[preg],
                   writes=[dst.reg()], tag=tag + "_bcev", engine="scalar")
    mb3 = meanb.bc3(NT, T)
    rb3 = rstdb.bc3(NT, T)
    c.tt(dst3, src3, mb3, OP.subtract, reads=[src_pl.reg(), meanb.reg()],
         writes=[dst_pl.reg()], tag=tag + "_sub")
    c.tt(dst3, dst3, rb3, OP.mult, reads=[dst_pl.reg(), rstdb.reg()],
         writes=[dst_pl.reg()], tag=tag + "_app")
    if gcol is not None:
        for kc in range(NT):
            c.ts(dst3[:, kc, :], dst3[:, kc, :], gcol(kc), bcol(kc),
                 OP.mult, OP.add, reads=[dst_pl.reg()] + list(gb_regs),
                 writes=[dst_pl.reg()], tag=tag + "_gb")



def build_l0_attn(c: Ctx, xin, xout, w):
    XI = xin.ap().tensor.name; XO = xout.ap().tensor.name
    NT, T, D, B = c.NT, c.T, c.D, c.B
    NH = 4 * NT
    nc = c.nc

    A = mkplane(c, "A", NT * T, BF16)
    Bz = mkplane(c, "Bz", NT * (T + 1), BF16)
    C = mkplane(c, "C", NT * T, BF16)
    Dp = mkplane(c, "Dp", NT * T, BF16)
    E = mkplane(c, "E", max(NT * T, 4 * NT * 512), BF16)
    SC = mkplane(c, "SC", 4 * T, F32)
    wk = [mkplane(c, "wkbuf0", NT * 128, BF16), mkplane(c, "wkbuf1", NT * 128, BF16)]
    wbig = [mkplane(c, "wbig0", NT * 128, BF16), mkplane(c, "wbig1", NT * 128, BF16)]
    vecs = mkplane(c, "vecs", 16 * c.NT, F32)

    VNAMES = ["ln1_g", "ln1_b", "ln2_g", "ln2_b", "tmk", "tmv", "tmr",
              "f_tmk", "f_tmr", "time_decay", "time_first"]
    vcol = {}
    for j, nm in enumerate(VNAMES):
        vcol[nm] = j * NT
        c.dma("sync", vecs.sl(j * NT, NT), c.dram["L0_attn_" + nm].ap()[:, :],
              reads=[("L0_attn_" + nm,)], writes=[vecs.reg()], tag="ldv_" + nm)
    vcol["lam"] = 11 * NT
    vcol["eu"] = 12 * NT
    c.act(vecs.sl(11 * NT, NT), vecs.sl(vcol["time_decay"], NT), AF.Exp,
          reads=[vecs.reg()], writes=[vecs.reg()], tag="lam1")
    c.act(vecs.sl(11 * NT, NT), vecs.sl(11 * NT, NT), AF.Exp, scale=-1.0,
          reads=[vecs.reg()], writes=[vecs.reg()], tag="lam2")
    c.act(vecs.sl(12 * NT, NT), vecs.sl(vcol["time_first"], NT), AF.Exp,
          reads=[vecs.reg()], writes=[vecs.reg()], tag="eu")

    def vc(nm, kc):
        return vecs.sl(vcol[nm] + kc, 1)

    rows = [Row(SC, 0, 3 * T), Row(SC, 32, 3 * T), Row(SC, 0, 2 * T), Row(SC, 32, 2 * T)]

    meanb = Plane(SC.t, 4 * T)
    rstdb = Plane(SC.t, 4 * T)

    class SCView:
        def __init__(self, c0):
            self.c0 = c0
        def sl(self, off, n, np_=128):
            return SC._mk(self.c0 + off, [[4 * T, np_], [1, n]])
        def bc3(self, ntile, tlen, off=0):
            return SC._mk(self.c0 + off, [[4 * T, 128], [0, ntile], [1, tlen]])
        def reg(self, *k):
            return ("SC",) + k
    meanb, rstdb = SCView(0), SCView(T)
    ek, vz = SCView(0), SCView(T)
    asc, bsc = SCView(2 * T), SCView(3 * T)
    x1c = SCView(0)
    x0c = SCView(2 * T)

    xin3 = [xin.ap()[b].rearrange("(n p) t -> p n t", p=128) for b in range(B)]
    xout3 = [xout.ap()[b].rearrange("(n p) t -> p n t", p=128) for b in range(B)]

    Bz_xn = Bz.v3(NT, T, stride=T + 1, off=1)
    Bz_xx = Bz.v3(NT, T, stride=T + 1, off=0)
    NCH = T // 512
    precast_weights(c, ["L0_attn_WkT", "L0_attn_WvT", "L0_attn_WrT", "L0_attn_WoT",
                        "L0_attn_fWrT", "L0_attn_fWkT", "L0_attn_fWvT"],
                    E, E.ncols // 2)

    def stream_wcols(dst_pl, src_name, i, nk, tag):
        src = c.dram["bf_" + src_name].ap().rearrange("(n p) m -> p n m", p=128)
        c.dma("sync", dst_pl._mk(0, [[dst_pl.ncols, 128], [128, nk], [1, 128]]),
              src[:, :, i * 128:(i + 1) * 128],
              reads=[("bf_" + src_name,)], writes=[dst_pl.reg()], tag=tag)

    def proj_mm(dst_pl, dst_off_fn, wname, rhs_pl, rhs3, act_fn, act_dtype_note, b, tagp,
                wbuf=None, nk=None, rhs_reg=None):
        nk = NT if nk is None else nk
        for i in range(NT):
            wbuf = wk[i % 2]
            stream_wcols(wbuf, wname, i, nk, tagp + "_w%d" % i)
            for nch in range(NCH):
                pt, preg = next_psum(c)
                def fn(eng, pt=pt, i=i, nch=nch, wbuf=wbuf):
                    last = None
                    for kc in range(nk):
                        last = eng.matmul(pt.ap()[:, :], wbuf.sl(kc * 128, 128),
                                          rhs3[:, kc, nch * 512:(nch + 1) * 512],
                                          start=(kc == 0), stop=(kc == nk - 1))
                    return last
                c.mm_group(fn, reads=[wbuf.reg(), rhs_reg or rhs_pl.reg()],
                           writes=[preg], tag=tagp + "_mm")
                act_fn(i, nch, pt, preg)

    for b in range(B):
        c.dma("gpsimd", A.v3(NT, T), xin3[b], reads=[(XI,)], writes=[A.reg()],
              tag="ld_x0")
        ln_cmajor(c, A, A.v3(NT, T), A, A.v3(NT, T), C, meanb, rstdb, rows, tag="ln0")
        c.memset(Bz._mk(0, [[Bz.ncols, 128], [T + 1, NT], [1, 1]]), 0.0,
                 writes=[Bz.reg()], tag="bz_zero")
        ln_cmajor(c, A, A.v3(NT, T), Bz, Bz_xn, C, meanb, rstdb, rows, tag="ln1",
                  gcol=lambda kc: vc("ln1_g", kc), bcol=lambda kc: vc("ln1_b", kc),
                  gb_regs=[vecs.reg()])

        c.tt(C.v3(NT, T), Bz_xn, Bz_xx, OP.subtract, reads=[Bz.reg()], writes=[C.reg()],
             tag="mix_d")
        for kc in range(NT):
            c.stt(Dp.v3(NT, T)[:, kc, :], C.v3(NT, T)[:, kc, :], vc("tmk", kc),
                  Bz_xx[:, kc, :], OP.mult, OP.add,
                  reads=[C.reg(), Bz.reg(), vecs.reg()], writes=[Dp.reg()], tag="mix_xk")
            c.stt(E.v3(NT, T)[:, kc, :], C.v3(NT, T)[:, kc, :], vc("tmv", kc),
                  Bz_xx[:, kc, :], OP.mult, OP.add,
                  reads=[C.reg(), Bz.reg(), vecs.reg()], writes=[E.reg()], tag="mix_xv")

        for i in range(NT):
            wki = wk[i % 2]
            stream_wcols(wki, "L0_attn_WkT", i, NT, "wk%d" % i)
            for nch in range(NCH):
                pt, preg = next_psum(c)
                def fn(eng, pt=pt, i=i, nch=nch, wki=wki):
                    last = None
                    for kc in range(NT):
                        last = eng.matmul(pt.ap()[:, :], wki.sl(kc * 128, 128),
                                          Dp.v3(NT, T)[:, kc, nch * 512:(nch + 1) * 512],
                                          start=(kc == 0), stop=(kc == NT - 1))
                    return last
                c.mm_group(fn, reads=[wki.reg(), Dp.reg()], writes=[preg], tag="kmm")
                c.act(ek.sl(nch * 512, 512), pt.ap()[:, :], AF.Exp,
                      reads=[preg], writes=[ek.reg()], tag="ek")
            stream_wcols(wbig[0], "L0_attn_WvT", i, NT, "wv%d" % i)
            for nch in range(NCH):
                pt, preg = next_psum(c)
                def fn(eng, pt=pt, i=i, nch=nch):
                    last = None
                    for kc in range(NT):
                        last = eng.matmul(pt.ap()[:, :], wbig[0].sl(kc * 128, 128),
                                          E.v3(NT, T)[:, kc, nch * 512:(nch + 1) * 512],
                                          start=(kc == 0), stop=(kc == NT - 1))
                    return last
                c.mm_group(fn, reads=[wbig[0].reg(), E.reg()], writes=[preg], tag="vmm")
                c.copy(vz.sl(nch * 512, 512), pt.ap()[:, :], reads=[preg],
                       writes=[vz.reg()], tag="vev", engine="scalar")
            c.tt(vz.sl(0, T), ek.sl(0, T), vz.sl(0, T), OP.mult,
                 reads=[ek.reg(), vz.reg()], writes=[vz.reg()], tag="z")
            lam_b = vecs._mk(vcol["lam"] + i, [[vecs.ncols, 128], [0, T]])
            c.scan(bsc.sl(0, T), lam_b, ek.sl(0, T), 0.0, OP.mult, OP.add,
                   reads=[vecs.reg(), ek.reg()], writes=[bsc.reg()], tag="bscan")
            c.scan(asc.sl(0, T), lam_b, vz.sl(0, T), 0.0, OP.mult, OP.add,
                   reads=[vecs.reg(), vz.reg()], writes=[asc.reg()], tag="ascan")
            c.stt(ek.sl(1, T - 1), ek.sl(1, T - 1), vc("eu", i), bsc.sl(0, T - 1),
                  OP.mult, OP.add, reads=[ek.reg(), bsc.reg(), vecs.reg()],
                  writes=[ek.reg()], tag="den")
            c.ts(ek.sl(0, 1), ek.sl(0, 1), vc("eu", i), None, OP.mult, OP.bypass,
                 reads=[ek.reg(), vecs.reg()], writes=[ek.reg()], tag="den0")
            c.recip_pos(ek.sl(0, T), ek.sl(0, T), reads=[ek.reg()], writes=[ek.reg()],
                        tag="wkvrcp")
            c.stt(vz.sl(1, T - 1), vz.sl(1, T - 1), vc("eu", i), asc.sl(0, T - 1),
                  OP.mult, OP.add, reads=[vz.reg(), asc.reg(), vecs.reg()],
                  writes=[vz.reg()], tag="num")
            c.ts(vz.sl(0, 1), vz.sl(0, 1), vc("eu", i), None, OP.mult, OP.bypass,
                 reads=[vz.reg(), vecs.reg()], writes=[vz.reg()], tag="num0")
            c.tt(C.v3(NT, T)[:, i, :], vz.sl(0, T), ek.sl(0, T), OP.mult,
                 reads=[vz.reg(), ek.reg()], writes=[C.reg()], tag="y")

        c.tt(Dp.v3(NT, T), Bz_xn, Bz_xx, OP.subtract, reads=[Bz.reg()], writes=[Dp.reg()],
             tag="mix_d2")
        for kc in range(NT):
            c.stt(E.v3(NT, T)[:, kc, :], Dp.v3(NT, T)[:, kc, :], vc("tmr", kc),
                  Bz_xx[:, kc, :], OP.mult, OP.add,
                  reads=[Dp.reg(), Bz.reg(), vecs.reg()], writes=[E.reg()], tag="mix_xr")
        def sr_evac(i, nch, pt, preg):
            c.act(Dp.v3(NT, T)[:, i, nch * 512:(nch + 1) * 512], pt.ap()[:, :],
                  AF.Sigmoid, reads=[preg], writes=[Dp.reg()], tag="sr")
        proj_mm(Dp, None, "L0_attn_WrT", E, E.v3(NT, T), sr_evac, None, b, "rproj")
        c.tt(C.v3(NT, T), C.v3(NT, T), Dp.v3(NT, T), OP.mult,
             reads=[C.reg(), Dp.reg()], writes=[C.reg()], tag="sy")

        def wo_evac(i, nch, pt, preg):
            sl = A.v3(NT, T)[:, i, nch * 512:(nch + 1) * 512]
            c.tt(sl, pt.ap()[:, :], sl, OP.add, reads=[preg, A.reg()], writes=[A.reg()],
                 tag="xmid")
        proj_mm(A, None, "L0_attn_WoT", C, C.v3(NT, T), wo_evac, None, b, "oproj")

        c.memset(Bz._mk(0, [[Bz.ncols, 128], [T + 1, NT], [1, 1]]), 0.0,
                 writes=[Bz.reg()], tag="bz_zero2")
        ln_cmajor(c, A, A.v3(NT, T), Bz, Bz_xn, C, meanb, rstdb, rows, tag="ln2",
                  gcol=lambda kc: vc("ln2_g", kc), bcol=lambda kc: vc("ln2_b", kc),
                  gb_regs=[vecs.reg()])

        c.tt(C.v3(NT, T), Bz_xn, Bz_xx, OP.subtract, reads=[Bz.reg()], writes=[C.reg()],
             tag="mix_d3")
        for kc in range(NT):
            c.stt(Dp.v3(NT, T)[:, kc, :], C.v3(NT, T)[:, kc, :], vc("f_tmk", kc),
                  Bz_xx[:, kc, :], OP.mult, OP.add,
                  reads=[C.reg(), Bz.reg(), vecs.reg()], writes=[Dp.reg()], tag="mix_xk2")
            c.stt(E.v3(NT, T)[:, kc, :], C.v3(NT, T)[:, kc, :], vc("f_tmr", kc),
                  Bz_xx[:, kc, :], OP.mult, OP.add,
                  reads=[C.reg(), Bz.reg(), vecs.reg()], writes=[E.reg()], tag="mix_xr2")

        def sp_evac(i, nch, pt, preg):
            c.act(C.v3(NT, T)[:, i, nch * 512:(nch + 1) * 512], pt.ap()[:, :],
                  AF.Sigmoid, reads=[preg], writes=[C.reg()], tag="sp")
        proj_mm(C, None, "L0_attn_fWrT", E, E.v3(NT, T), sp_evac, None, b, "fr")

        fWvT3 = c.dram["bf_L0_attn_fWvT"].ap().rearrange("(n p) m -> p n m", p=128)
        E_kk = E._mk(0, [[E.ncols, 128], [512, NH], [1, 512]])
        for n in range(NCH):
            c.dma("sync", SC._mk(2 * T, [[4 * T, 128], [512, NT], [1, 512]]),
                  xin3[b][:, :, n * 512:(n + 1) * 512],
                  reads=[(XI,)], writes=[("SC",)], tag="ld_x0c")
            for fM in range(NH):
                wkf = wk[fM % 2]
                stream_wcols(wkf, "L0_attn_fWkT", fM, NT, "fwk")
                pt, preg = next_psum(c)
                def fn(eng, pt=pt, fM=fM, n=n, wkf=wkf):
                    last = None
                    for kc in range(NT):
                        last = eng.matmul(pt.ap()[:, :], wkf.sl(kc * 128, 128),
                                          Dp.v3(NT, T)[:, kc, n * 512:(n + 1) * 512],
                                          start=(kc == 0), stop=(kc == NT - 1))
                    return last
                c.mm_group(fn, reads=[wkf.reg(), Dp.reg()], writes=[preg], tag="kkmm")
                sl = E_kk[:, fM, :]
                c.ts(sl, pt.ap()[:, :], 0.0, None, OP.max, OP.bypass,
                     reads=[preg], writes=[E.reg()], tag="kkrelu")
                c.act(sl, sl, AF.Square, reads=[E.reg()], writes=[E.reg()], tag="kksq")
            for dM in range(NT):
                pt, preg = next_psum(c)
                for slab in range(4):
                    wb = wbig[slab % 2]
                    c.dma("sync",
                          wb._mk(0, [[wb.ncols, 128], [128, NT], [1, 128]]),
                          fWvT3[:, slab * NT:(slab + 1) * NT, dM * 128:(dM + 1) * 128],
                          reads=[("bf_L0_attn_fWvT",)], writes=[wb.reg()], tag="fwv_w")
                    def fn(eng, pt=pt, wb=wb, slab=slab, n=n):
                        last = None
                        for j in range(NT):
                            kc = slab * NT + j
                            last = eng.matmul(pt.ap()[:, :], wb.sl(j * 128, 128),
                                              E_kk[:, kc, :],
                                              start=(slab == 0 and j == 0),
                                              stop=(slab == 3 and j == NT - 1))
                        return last
                    c.mm_group(fn, reads=[wb.reg(), E.reg()], writes=[preg], tag="fvmm")
                t1 = x1c.sl(dM * 512, 512)
                c.tt(t1, pt.ap()[:, :], C.v3(NT, T)[:, dM, n * 512:(n + 1) * 512],
                     OP.mult, reads=[preg, C.reg()], writes=[ek.reg(), vz.reg()],
                     tag="rkv")
                c.tt(t1, t1, A.v3(NT, T)[:, dM, n * 512:(n + 1) * 512], OP.add,
                     reads=[ek.reg(), vz.reg(), A.reg()], writes=[ek.reg(), vz.reg()],
                     tag="addmid")
                c.tt(t1, t1, x0c.sl(dM * 512, 512), OP.add,
                     reads=[ek.reg(), vz.reg(), asc.reg(), bsc.reg()],
                     writes=[ek.reg(), vz.reg()], tag="addx0")
            c.dma("sync", xout3[b][:, :, n * 512:(n + 1) * 512],
                  SC._mk(0, [[4 * T, 128], [512, NT], [1, 512]]),
                  reads=[ek.reg(), vz.reg()], writes=[(XO,)], tag="st_x1")



def _decl(nc, c, name, shape, kind="ExternalInput", dtype=F32):
    h = nc.dram_tensor(name, list(shape), dtype, kind=kind)
    c.dram[name] = h
    return h


def build_prog_l0_attn(D=1024, T=2048, B=2):
    nc = bass.Bass("TRN2", target_bir_lowering=False, debug=False)
    c = Ctx(nc, D=D, T=T, B=B)
    c.dram = {}
    NT = c.NT
    HID = 4 * D
    xin = _decl(nc, c, "xin", [B, D, T])
    xout = _decl(nc, c, "xout", [B, D, T], kind="ExternalOutput")
    for nm in ("WkT", "WvT", "WrT", "WoT", "fWrT"):
        _decl(nc, c, "L0_attn_" + nm, [D, D])
        _decl(nc, c, "bf_L0_attn_" + nm, [D, D], kind="Internal", dtype=BF16)
    _decl(nc, c, "L0_attn_fWkT", [D, HID])
    _decl(nc, c, "bf_L0_attn_fWkT", [D, HID], kind="Internal", dtype=BF16)
    _decl(nc, c, "L0_attn_fWvT", [HID, D])
    _decl(nc, c, "bf_L0_attn_fWvT", [HID, D], kind="Internal", dtype=BF16)
    for nm in ("ln1_g", "ln1_b", "ln2_g", "ln2_b", "tmk", "tmv", "tmr",
               "f_tmk", "f_tmr", "time_decay", "time_first"):
        _decl(nc, c, "L0_attn_" + nm, [128, NT])
    setup_common(c)
    build_l0_attn(c, xin, xout, None)
    c.p.fence()
    c.p.emit()
    global _LAST_PROG
    _LAST_PROG = c.p
    return nc


def _vecprep(v, NT):
    v = np.asarray(v, dtype=np.float32).reshape(-1)
    return np.ascontiguousarray(v.reshape(NT, 128).T)


def host_l0_attn(I, D=1024):
    NT = D // 128
    m = {}
    for nm in ("Wk", "Wv", "Wr", "Wo"):
        m["L0_attn_" + nm + "T"] = np.ascontiguousarray(I["L0_attn_" + nm].T)
    m["L0_attn_fWrT"] = np.ascontiguousarray(I["L0_attn_fWr"].T)
    m["L0_attn_fWkT"] = np.ascontiguousarray(I["L0_attn_fWk"].T)
    m["L0_attn_fWvT"] = np.ascontiguousarray(I["L0_attn_fWv"].T)
    for nm in ("ln1_g", "ln1_b", "ln2_g", "ln2_b", "tmk", "tmv", "tmr",
               "f_tmk", "f_tmr", "time_decay", "time_first"):
        m["L0_attn_" + nm] = _vecprep(I["L0_attn_" + nm], NT)
    return m



def build_l1_ffn(c: Ctx, xin, xout, pfx="L1_ffn_"):
    XI = xin.ap().tensor.name; XO = xout.ap().tensor.name
    need_ident32(c)
    need_e8(c)
    NT, T, D, B = c.NT, c.T, c.D, c.B
    TOK = B * T
    NH = 4 * NT
    F = 4 * D
    NE = 8
    NCHU = TOK // 512

    A4 = mkplane(c, "m_A4", NT * TOK, BF16)
    SQP = min(4, TOK // 512)
    SQ4 = mkplane(c, "m_SQ4", NT * TOK // SQP, BF16)
    mb4 = mkplane(c, "m_mb4", TOK, BF16)
    rb4 = mkplane(c, "m_rb4", TOK, BF16)

    hpl = mkplane(c, "m_h", NH * 512, BF16)
    yac = mkplane(c, "m_yac", NT * 512, F32)
    x3c = mkplane(c, "m_x3c", NT * 512, F32)
    rows4 = [Row(yac, 0, 0), Row(yac, 32, 0), Row(x3c, 0, 0), Row(x3c, 32, 0)]
    w1b = [mkplane(c, f"m_w1b{j}", NT * 128, BF16) for j in range(4)]
    w2b = [mkplane(c, f"m_w2b{j}", 2 * NT * 128, BF16) for j in range(4)]
    wbc = mkplane(c, "m_wbc", 512, F32)
    redp = mkplane(c, "m_red", 512, F32, npart=16)
    wCp = mkplane(c, "m_wC", 512, F32, npart=8)
    wredp = mkplane(c, "m_wred", NT * 16, F32)
    wgp = mkplane(c, "m_wg", 16, F32, npart=8)
    wgnT = mkplane(c, "m_wgnT", 8, F32, npart=16)
    colsum = mkplane(c, "m_colsum", 1, F32, npart=16)
    gmisc = mkplane(c, "m_gmisc", 64, F32)
    b1c = mkplane(c, "m_b1c", NE * NH, F32)
    b2sb = mkplane(c, "m_b2sb", D, F32, npart=8)

    xin4 = xin.ap().rearrange("b (n p) t -> p n b t", p=128)
    xout4 = xout.ap().rearrange("b (n p) t -> p n b t", p=128)
    NPB = T // 512
    def xin_chunk(n):
        return xin4[:, :, n // NPB, (n % NPB) * 512:(n % NPB) * 512 + 512]
    def xout_chunk(n):
        return xout4[:, :, n // NPB, (n % NPB) * 512:(n % NPB) * 512 + 512]

    for b in range(B):
        c.dma("gpsimd", A4._mk(b * T, [[A4.ncols, 128], [TOK, NT], [1, T]]),
              xin4[:, :, b, :], reads=[(XI,)], writes=[A4.reg()], tag="m_ldx")
    c.dma("sync", wredp._mk(0, [[wredp.ncols, 128], [16, NT], [1, 16]]),
          c.dram[pfx + "w_red"].ap().rearrange("(n p) g -> p n g", p=128),
          reads=[(pfx + "w_red",)], writes=[wredp.reg()], tag="m_ldwred")
    c.dma("sync", wgp.sl(0, 16, np_=8), c.dram[pfx + "wg"].ap()[:, :],
          reads=[(pfx + "wg",)], writes=[wgp.reg()], tag="m_ldwg")
    c.dma("sync", b1c.sl(0, NE * NH), c.dram[pfx + "b1c"].ap()[:, :],
          reads=[(pfx + "b1c",)], writes=[b1c.reg()], tag="m_ldb1")
    c.dma("sync", b2sb.sl(0, D, np_=8), c.dram[pfx + "b2"].ap()[:, :],
          reads=[(pfx + "b2",)], writes=[b2sb.reg()], tag="m_ldb2")

    precast_weights(c, [pfx + "w1", pfx + "w2"], hpl, hpl.ncols // 2)
    ln_cmajor(c, A4, A4.v3(NT, TOK), A4, A4.v3(NT, TOK), SQ4, mb4, rb4, rows4,
              tag="m_ln", tlen=TOK, sq_parts=SQP)

    pt, preg = next_psum(c)
    def fn(eng, pt=pt):
        last = None
        for kc in range(NT):
            last = eng.matmul(pt.ap()[0:1, 0:16], c.ones128.sl(0, 1),
                              wredp._mk(0, [[wredp.ncols, 128], [16, NT], [1, 16]])[:, kc, :],
                              start=(kc == 0), stop=(kc == NT - 1))
        return last
    c.mm_group(fn, reads=[wredp.reg(), c.ones128.reg()], writes=[preg], tag="m_colsum_mm")
    c.copy(gmisc.sl(0, 16, np_=1), pt.ap()[0:1, 0:16], reads=[preg], writes=[gmisc.reg()],
           tag="m_colsum_ev", engine="scalar")
    cs_dram = c.dram["m_scr16"]
    c.dma("sync", cs_dram.ap()[:, :], gmisc.sl(0, 16, np_=1), reads=[gmisc.reg()],
          writes=[("m_scr16",)], tag="m_cs_out")
    def fn(eng):
        return eng.dma_start(out=colsum.sl(0, 1, np_=16),
                             in_=cs_dram.ap().rearrange("a g -> (g a)")[:, None],
                             allow_slow_non_contiguous=True)
    c.p.op("sync", fn, reads=[("m_scr16",)], writes=[colsum.reg()], tag="m_cs_in")
    c.act(gmisc.sl(32, 16, np_=8), wgp.sl(0, 16, np_=8), AF.Square,
          reads=[wgp.reg()], writes=[gmisc.reg()], tag="m_wgsq")
    c.p.op("vector", (lambda eng: eng.reduce_sum(gmisc.sl(16, 1, np_=8), gmisc.sl(32, 16, np_=8),
                                                 mybir.AxisListType.X)),
           reads=[gmisc.reg()], writes=[gmisc.reg()], tag="m_wgss2")
    c.act(gmisc.sl(17, 1, np_=8), gmisc.sl(16, 1, np_=8), AF.Sqrt,
          reads=[gmisc.reg()], writes=[gmisc.reg()], tag="m_wgn1")
    c.ts(gmisc.sl(17, 1, np_=8), gmisc.sl(17, 1, np_=8), 1e-4, None, OP.max, OP.bypass,
         reads=[gmisc.reg()], writes=[gmisc.reg()], tag="m_wgn2")
    c.recip(gmisc.sl(18, 1, np_=8), gmisc.sl(17, 1, np_=8), reads=[gmisc.reg()],
            writes=[gmisc.reg()], tag="m_wgn3")
    c.ts(wgp.sl(0, 16, np_=8), wgp.sl(0, 16, np_=8), gmisc.sl(18, 1, np_=8), None,
         OP.mult, OP.bypass, reads=[wgp.reg(), gmisc.reg()], writes=[wgp.reg()], tag="m_wgn4")
    wgn_dram = c.dram["m_scr8x16"]
    c.dma("sync", wgn_dram.ap()[:, :], wgp.sl(0, 16, np_=8), reads=[wgp.reg()],
          writes=[("m_scr8x16",)], tag="m_wgn_out")
    def fn(eng):
        return eng.dma_start(out=wgnT.sl(0, 8, np_=16),
                             in_=wgn_dram.ap().rearrange("e g -> g e"),
                             allow_slow_non_contiguous=True)
    c.p.op("sync", fn, reads=[("m_scr8x16",)], writes=[wgnT.reg()], tag="m_wgn_in")

    for n in range(NCHU):
        c.dma("sync", x3c.v3(NT, 512), xin_chunk(n),
              reads=[(XI,)], writes=[x3c.reg()], tag="m_ldx3c")
        pt, preg = next_psum(c)
        def fn(eng, pt=pt, n=n):
            last = None
            for kc in range(NT):
                last = eng.matmul(pt.ap()[0:16, :],
                                  wredp._mk(0, [[wredp.ncols, 128], [16, NT], [1, 16]])[:, kc, :],
                                  x3c.v3(NT, 512)[:, kc, :],
                                  start=(kc == 0), stop=(kc == NT - 1))
            return last
        c.mm_group(fn, reads=[wredp.reg(), x3c.reg()], writes=[preg], tag="m_redmm")
        m16 = mb4._mk(n * 512, [[mb4.ncols, 16], [1, 512]])
        r16 = rb4._mk(n * 512, [[rb4.ncols, 16], [1, 512]])
        c.ts(redp.sl(0, 512, np_=16), m16, colsum.sl(0, 1, np_=16), None, OP.mult,
             OP.bypass, reads=[mb4.reg(), colsum.reg()], writes=[redp.reg()], tag="m_red1")
        c.tt(redp.sl(0, 512, np_=16), pt.ap()[0:16, :], redp.sl(0, 512, np_=16),
             OP.subtract, reads=[preg, redp.reg()], writes=[redp.reg()], tag="m_red2")
        c.tt(redp.sl(0, 512, np_=16), redp.sl(0, 512, np_=16), r16, OP.mult,
             reads=[redp.reg(), rb4.reg()], writes=[redp.reg()], tag="m_red3")
        for j in range(4):
            pt2, preg2 = next_psum(c)
            def fn(eng, pt2=pt2, j=j):
                return eng.matmul(pt2.ap()[:, 0:8], redp.sl(j * 128, 128, np_=16),
                                  wgnT.sl(0, 8, np_=16), start=True, stop=True)
            c.mm_group(fn, reads=[redp.reg(), wgnT.reg()], writes=[preg2], tag="m_lgmm")
            l8 = gmisc.sl(24, 8)
            mx = gmisc.sl(33, 1)
            sm = gmisc.sl(34, 1)
            c.copy(l8, pt2.ap()[:, 0:8], reads=[preg2], writes=[gmisc.reg()],
                   tag="m_l8", engine="vector")
            c.p.op("vector", (lambda eng, l8=l8, mx=mx: eng.reduce_max(mx, l8, mybir.AxisListType.X)),
                   reads=[gmisc.reg()], writes=[gmisc.reg()], tag="m_gmax")
            c.ts(l8, l8, mx, None, OP.subtract, OP.bypass,
                 reads=[gmisc.reg()], writes=[gmisc.reg()], tag="m_gsub")
            el8 = gmisc.sl(40, 8)
            c.act(el8, l8, AF.Exp, reads=[gmisc.reg()], writes=[gmisc.reg()], tag="m_gexp")
            c.p.op("vector", (lambda eng, el8=el8, sm=sm: eng.reduce_sum(sm, el8, mybir.AxisListType.X)),
                   reads=[gmisc.reg()], writes=[gmisc.reg()], tag="m_gsum")
            c.recip(sm, sm, reads=[gmisc.reg()], writes=[gmisc.reg()], tag="m_gscore")
            c.ts(l8, l8, 0.0, None, OP.is_equal, OP.bypass,
                 reads=[gmisc.reg()], writes=[gmisc.reg()], tag="m_gmask")
            c.ts(l8, l8, sm, None, OP.mult, OP.bypass,
                 reads=[gmisc.reg()], writes=[gmisc.reg()], tag="m_gsel")
            pt3, preg3 = next_psum(c)
            def fn(eng, pt3=pt3, l8=l8):
                return eng.transpose(pt3.ap()[0:8, 0:128], l8, c.ident32.sl(0, 128))
            c.p.op("tensor", fn, reads=[gmisc.reg(), ("ident32",)], writes=[preg3],
                   tag="m_gtr")
            c.copy(wCp._mk(j * 128, [[wCp.ncols, 8], [1, 128]]), pt3.ap()[0:8, 0:128],
                   reads=[preg3], writes=[wCp.reg()], tag="m_gtev", engine="vector")

        for e in range(NE):
            pt, preg = next_psum(c)
            def fn(eng, pt=pt, e=e):
                return eng.matmul(pt.ap()[:, :], c.E8._mk(e * 128, [[8 * 128, 8], [1, 128]]),
                                  wCp._mk(0, [[wCp.ncols, 8], [1, 512]]),
                                  start=True, stop=True)
            c.mm_group(fn, reads=[wCp.reg(), c.E8.reg()], writes=[preg], tag="m_bc")
            c.copy(wbc.sl(0, 512), pt.ap()[:, :], reads=[preg], writes=[wbc.reg()],
                   tag="m_bcev", engine="scalar")
            w1d = c.dram["bf_" + pfx + "w1"].ap()[e].rearrange("(k p) m -> p k m", p=128)
            for fM in range(NH):
                wb = w1b[fM % 4]
                c.dma("sync", wb._mk(0, [[wb.ncols, 128], [128, NT], [1, 128]]),
                      w1d[:, :, fM * 128:(fM + 1) * 128],
                      reads=[("bf_" + pfx + "w1",)], writes=[wb.reg()], tag="m_w1w")
                pt, preg = next_psum(c)
                def fn(eng, pt=pt, wb=wb, n=n):
                    last = None
                    for kc in range(NT):
                        last = eng.matmul(pt.ap()[:, :], wb.sl(kc * 128, 128),
                                          A4.v3(NT, TOK)[:, kc, n * 512:(n + 1) * 512],
                                          start=(kc == 0), stop=(kc == NT - 1))
                    return last
                c.mm_group(fn, reads=[wb.reg(), A4.reg()], writes=[preg], tag="m_w1mm")
                c.act(hpl.v3(NH, 512)[:, fM, :], pt.ap()[:, :], AF.Gelu,
                      bias=b1c.sl(e * NH + fM, 1), reads=[preg, b1c.reg()],
                      writes=[hpl.reg()], tag="m_gelu")
                c.tt(hpl.v3(NH, 512)[:, fM, :], hpl.v3(NH, 512)[:, fM, :], wbc.sl(0, 512),
                     OP.mult, reads=[hpl.reg(), wbc.reg()], writes=[hpl.reg()], tag="m_hsc")
            w2d = c.dram["bf_" + pfx + "w2"].ap()[e].rearrange("(k p) m -> p k m", p=128)
            for dM in range(NT):
                pt, preg = next_psum(c)
                for slab in range(2):
                    wb = w2b[(dM * 2 + slab) % 4]
                    c.dma("sync", wb._mk(0, [[wb.ncols, 128], [128, 2 * NT], [1, 128]]),
                          w2d[:, slab * 2 * NT:(slab + 1) * 2 * NT, dM * 128:(dM + 1) * 128],
                          reads=[("bf_" + pfx + "w2",)], writes=[wb.reg()], tag="m_w2w")
                    def fn(eng, pt=pt, wb=wb, slab=slab):
                        last = None
                        for j2 in range(2 * NT):
                            kc = slab * 2 * NT + j2
                            last = eng.matmul(pt.ap()[:, :], wb.sl(j2 * 128, 128),
                                              hpl.v3(NH, 512)[:, kc, :],
                                              start=(slab == 0 and j2 == 0),
                                              stop=(slab == 1 and j2 == 2 * NT - 1))
                        return last
                    c.mm_group(fn, reads=[wb.reg(), hpl.reg()], writes=[preg], tag="m_w2mm")
                if e == 0:
                    c.copy(yac.sl(dM * 512, 512), pt.ap()[:, :], reads=[preg],
                           writes=[yac.reg()], tag="m_yc0", engine="vector")
                else:
                    c.tt(yac.sl(dM * 512, 512), pt.ap()[:, :], yac.sl(dM * 512, 512),
                         OP.add, reads=[preg, yac.reg()], writes=[yac.reg()], tag="m_yacc")
        for dM in range(NT):
            pt, preg = next_psum(c)
            def fn(eng, pt=pt, dM=dM):
                return eng.matmul(pt.ap()[:, :], b2sb._mk(dM * 128, [[b2sb.ncols, 8], [1, 128]]),
                                  wCp._mk(0, [[wCp.ncols, 8], [1, 512]]), start=True, stop=True)
            c.mm_group(fn, reads=[b2sb.reg(), wCp.reg()], writes=[preg], tag="m_b2mm")
            c.tt(yac.sl(dM * 512, 512), pt.ap()[:, :], yac.sl(dM * 512, 512), OP.add,
                 reads=[preg, yac.reg()], writes=[yac.reg()], tag="m_b2acc")
        c.tt(x3c.v3(NT, 512), yac.v3(NT, 512), x3c.v3(NT, 512), OP.add,
             reads=[yac.reg(), x3c.reg()], writes=[x3c.reg()], tag="m_x4")
        c.dma("sync", xout_chunk(n), x3c.v3(NT, 512),
              reads=[x3c.reg()], writes=[(XO,)], tag="m_stx4")


def build_prog_l1_ffn(D=1024, T=2048, B=2):
    nc = bass.Bass("TRN2", target_bir_lowering=False, debug=False)
    c = Ctx(nc, D=D, T=T, B=B)
    c.dram = {}
    F = 4 * D
    xin = _decl(nc, c, "xin", [B, D, T])
    xout = _decl(nc, c, "xout", [B, D, T], kind="ExternalOutput")
    _decl(nc, c, "L1_ffn_w_red", [D, 16])
    _decl(nc, c, "L1_ffn_wg", [8, 16])
    _decl(nc, c, "L1_ffn_w1", [8, D, F])
    _decl(nc, c, "bf_L1_ffn_w1", [8, D, F], kind="Internal", dtype=BF16)
    _decl(nc, c, "L1_ffn_b1c", [128, 8 * (F // 128)])
    _decl(nc, c, "L1_ffn_w2", [8, F, D])
    _decl(nc, c, "bf_L1_ffn_w2", [8, F, D], kind="Internal", dtype=BF16)
    _decl(nc, c, "L1_ffn_b2", [8, D])
    _decl(nc, c, "m_scr16", [1, 16], kind="Internal")
    _decl(nc, c, "m_scr8x16", [8, 16], kind="Internal")
    setup_common(c, npsum=6)
    build_l1_ffn(c, xin, xout)
    c.p.fence()
    c.p.emit()
    global _LAST_PROG
    _LAST_PROG = c.p
    return nc


def host_l1_ffn(I, D=1024):
    F = 4 * D
    m = {
        "L1_ffn_w_red": I["L1_ffn_w_red"],
        "L1_ffn_wg": I["L1_ffn_wg"],
        "L1_ffn_w1": I["L1_ffn_w1"],
        "L1_ffn_w2": I["L1_ffn_w2"],
        "L1_ffn_b2": I["L1_ffn_b2"],
    }
    b1 = np.asarray(I["L1_ffn_b1"])
    cols = np.concatenate([b1[e].reshape(F // 128, 128).T for e in range(8)], axis=1)
    m["L1_ffn_b1c"] = np.ascontiguousarray(cols)
    return m



def build_l1_attn(c: Ctx, xin, xout, pfx="L1_attn_"):
    XI = xin.ap().tensor.name; XO = xout.ap().tensor.name
    need_zero_col(c)
    NT, T, D, B = c.NT, c.T, c.D, c.B
    R = 256
    NR = R // 128

    A5 = mkplane(c, "a_A5", NT * T, BF16)
    OP5 = mkplane(c, "a_OP5", NT * T, BF16)
    RK = mkplane(c, "a_RK", 6 * T, F32)
    AB = mkplane(c, "a_ABr", NR * T, BF16)
    AT = mkplane(c, "a_ATr", NR * T, BF16)
    rc = mkplane(c, "a_rc", T, F32)
    rows5_t = mkplane(c, "a_rows5", T, F32)
    xc5 = mkplane(c, "a_xc5", NT * 512, F32)
    wst = [mkplane(c, "a_wst0", NT * 128, BF16), mkplane(c, "a_wst1", NT * 128, BF16)]
    WQV = {}
    for nm in ("qvb", "qvt", "kwb", "kwt", "xt"):
        WQV[nm] = mkplane(c, "a_W" + nm, NT * R, BF16)
    UBT = mkplane(c, "a_UBT", NR * D, BF16)
    UTT = mkplane(c, "a_UTT", NR * D, BF16)
    av = mkplane(c, "a_av", 32, F32)

    rows5 = [Row(rows5_t, 0), Row(rows5_t, 32), Row(RK, 0, 2 * T), Row(RK, 32, 2 * T)]

    class SCV:
        def __init__(self, c0):
            self.c0 = c0
        def sl(self, off, n, np_=128):
            return RK._mk(self.c0 + off, [[RK.ncols, np_], [1, n]])
        def bc3(self, ntile, tlen, off=0):
            return RK._mk(self.c0 + off, [[RK.ncols, 128], [0, ntile], [1, tlen]])
        def reg(self, *k):
            return ("a_RK",) + k
    meanb5, rstdb5 = SCV(0), SCV(T)
    cB = [SCV(0), SCV(T)]
    cT = [SCV(2 * T), SCV(3 * T)]
    zX = [SCV(4 * T), SCV(5 * T)]

    xin4 = xin.ap().rearrange("b (n p) t -> p n b t", p=128)
    xout4 = xout.ap().rearrange("b (n p) t -> p n b t", p=128)

    c.dma("sync", av.sl(0, 1, np_=1), c.dram[pfx + "alpha"].ap()[:, :],
          reads=[(pfx + "alpha",)], writes=[av.reg()], tag="a_lda")
    pt, preg = next_psum(c)
    def fn(eng, pt=pt):
        return eng.matmul(pt.ap()[:, 0:1], c.ones128._mk(0, [[128, 1], [1, 128]]),
                          av.sl(0, 1, np_=1), start=True, stop=True)
    c.mm_group(fn, reads=[av.reg(), c.ones128.reg()], writes=[preg], tag="a_bca")
    c.copy(av.sl(1, 1), pt.ap()[:, 0:1], reads=[preg], writes=[av.reg()],
           tag="a_bcaev", engine="vector")
    c.dma("sync", av.sl(2, NT), c.dram[pfx + "bias_b"].ap()[:, :],
          reads=[(pfx + "bias_b",)], writes=[av.reg()], tag="a_ldbb")
    c.dma("sync", av.sl(2 + NT, NT), c.dram[pfx + "bias_t"].ap()[:, :],
          reads=[(pfx + "bias_t",)], writes=[av.reg()], tag="a_ldbt")
    c.stt(av.sl(2 + NT, NT), av.sl(2 + NT, NT), av.sl(1, 1), av.sl(2, NT),
          OP.mult, OP.add, reads=[av.reg()], writes=[av.reg()], tag="a_combo")
    def fn(eng):
        return eng.iota(rc.sl(0, T), [[1, T]], base=1, channel_multiplier=0,
                        allow_small_or_imprecise_dtypes=True)
    c.p.op("gpsimd", fn, writes=[rc.reg()], tag="a_iota")
    c.recip(rc.sl(0, T), rc.sl(0, T), reads=[rc.reg()], writes=[rc.reg()], tag="a_rc")
    for wname, vname, dst in (("Wq", "V_b", "qvb"), ("Wq", "V_t", "qvt"),
                              ("Wk", "W_b", "kwb"), ("Wk", "W_t", "kwt")):
        wd = c.dram[pfx + wname].ap().rearrange("(k p) m -> p k m", p=128)
        vd = c.dram[pfx + vname].ap().rearrange("(k p) r -> p k r", p=128)
        if not hasattr(c, "_a_vv"):
            c._a_vv = mkplane(c, "a_vv", NT * R, BF16)
        vv = c._a_vv
        c.dma("gpsimd", vv._mk(0, [[vv.ncols, 128], [R, NT], [1, R]]), vd,
              reads=[(pfx + vname,)], writes=[vv.reg()], tag="a_ldv")
        for iM in range(NT):
            wb = wst[iM % 2]
            c.dma("gpsimd", wb._mk(0, [[wb.ncols, 128], [128, NT], [1, 128]]),
                  wd[:, :, iM * 128:(iM + 1) * 128],
                  reads=[(pfx + wname,)], writes=[wb.reg()], tag="a_ldw")
            for nr in range(1):
                pt, preg = next_psum(c)
                def fn(eng, pt=pt, wb=wb, vv=vv):
                    last = None
                    for kc in range(NT):
                        last = eng.matmul(pt.ap()[:, 0:R], wb.sl(kc * 128, 128),
                                          vv._mk(0, [[vv.ncols, 128], [R, NT], [1, R]])[:, kc, :],
                                          start=(kc == 0), stop=(kc == NT - 1))
                    return last
                c.mm_group(fn, reads=[wb.reg(), vv.reg()], writes=[preg], tag="a_cwmm")
                c.copy(WQV[dst]._mk(iM * R, [[WQV[dst].ncols, 128], [1, R]]),
                       pt.ap()[:, 0:R], reads=[preg], writes=[WQV[dst].reg()],
                       tag="a_cwev", engine="scalar")
    c.dma("gpsimd", WQV["xt"]._mk(0, [[WQV["xt"].ncols, 128], [R, NT], [1, R]]),
          c.dram[pfx + "X_t"].ap().rearrange("(k p) r -> p k r", p=128),
          reads=[(pfx + "X_t",)], writes=[WQV["xt"].reg()], tag="a_ldxt")
    for nm, dst in (("U_bT", UBT), ("U_tT", UTT)):
        c.dma("gpsimd", dst._mk(0, [[dst.ncols, 128], [D, NR], [1, D]]),
              c.dram[pfx + nm].ap().rearrange("(k p) m -> p k m", p=128),
              reads=[(pfx + nm,)], writes=[dst.reg()], tag="a_ldu")

    NCH = T // 512
    for b in range(B):
        c.dma("gpsimd", A5.v3(NT, T), xin4[:, :, b, :], reads=[(XI,)],
              writes=[A5.reg()], tag="a_ldx")
        ln_cmajor(c, A5, A5.v3(NT, T), A5, A5.v3(NT, T), OP5, meanb5, rstdb5, rows5,
                  tag="a_ln")
        for dst_pl, wkey, outname in ((AB, "qvb", "ab"), (AT, "qvt", "at")):
            for rt in range(NR):
                for nch in range(NCH):
                    pt, preg = next_psum(c)
                    def fn(eng, pt=pt, wkey=wkey, rt=rt, nch=nch):
                        last = None
                        for kc in range(NT):
                            lhsT = WQV[wkey]._mk(kc * R + rt * 128,
                                                 [[WQV[wkey].ncols, 128], [1, 128]])
                            last = eng.matmul(pt.ap()[:, :], lhsT,
                                              A5.v3(NT, T)[:, kc, nch * 512:(nch + 1) * 512],
                                              start=(kc == 0), stop=(kc == NT - 1))
                        return last
                    c.mm_group(fn, reads=[WQV[wkey].reg(), A5.reg()], writes=[preg],
                               tag="a_prmm")
                    c.copy(dst_pl._mk(rt * T + nch * 512, [[dst_pl.ncols, 128], [1, 512]]),
                           pt.ap()[:, :], reads=[preg], writes=[dst_pl.reg()],
                           tag="a_prev", engine="scalar")
        for wkey, dsts in (("kwb", cB), ("kwt", cT), ("xt", zX)):
            for rt in range(NR):
                for nch in range(NCH):
                    pt, preg = next_psum(c)
                    def fn(eng, pt=pt, wkey=wkey, rt=rt, nch=nch):
                        last = None
                        for kc in range(NT):
                            lhsT = WQV[wkey]._mk(kc * R + rt * 128,
                                                 [[WQV[wkey].ncols, 128], [1, 128]])
                            last = eng.matmul(pt.ap()[:, :], lhsT,
                                              A5.v3(NT, T)[:, kc, nch * 512:(nch + 1) * 512],
                                              start=(kc == 0), stop=(kc == NT - 1))
                        return last
                    c.mm_group(fn, reads=[WQV[wkey].reg(), A5.reg()], writes=[preg],
                               tag="a_csmm")
                    c.copy(dsts[rt].sl(nch * 512, 512), pt.ap()[:, :],
                           reads=[preg], writes=[dsts[rt].reg()], tag="a_csev",
                           engine="scalar")
        zc = c.zero_col
        for dsts in (cB, cT, zX):
            for rt in range(NR):
                zb = zc._mk(0, [[1, 128], [0, T]])
                c.scan(dsts[rt].sl(0, T), dsts[rt].sl(0, T), zb, 0.0, OP.add, OP.add,
                       reads=[dsts[rt].reg(), ("zero_col",)], writes=[dsts[rt].reg()],
                       tag="a_scan")
        for rt in range(NR):
            c.tt(zX[rt].sl(0, T), zX[rt].sl(0, T), rc.sl(0, T), OP.mult,
                 reads=[zX[rt].reg(), rc.reg()], writes=[zX[rt].reg()], tag="a_zt")
        for rt in range(NR):
            ab_v = AB._mk(rt * T, [[AB.ncols, 128], [1, T]])
            at_v = AT._mk(rt * T, [[AT.ncols, 128], [1, T]])
            c.tt(ab_v, ab_v, cB[rt].sl(0, T), OP.mult,
                 reads=[AB.reg(), cB[rt].reg()], writes=[AB.reg()], tag="a_pb")
            c.tt(at_v, at_v, zX[rt].sl(0, T), OP.mult,
                 reads=[AT.reg(), zX[rt].reg()], writes=[AT.reg()], tag="a_pt1")
            c.tt(at_v, at_v, cT[rt].sl(0, T), OP.mult,
                 reads=[AT.reg(), cT[rt].reg()], writes=[AT.reg()], tag="a_pt2")
            c.ts(at_v, at_v, av.sl(1, 1), None, OP.mult, OP.bypass,
                 reads=[AT.reg(), av.reg()], writes=[AT.reg()], tag="a_pt3")
        for iM in range(NT):
            for nch in range(NCH):
                pt, preg = next_psum(c)
                def fn(eng, pt=pt, iM=iM, nch=nch):
                    last = None
                    for rt in range(NR):
                        lhsT = UBT._mk(rt * D + iM * 128, [[UBT.ncols, 128], [1, 128]])
                        last = eng.matmul(pt.ap()[:, :], lhsT,
                                          AB._mk(rt * T + nch * 512, [[AB.ncols, 128], [1, 512]]),
                                          start=(rt == 0), stop=False)
                    for rt in range(NR):
                        lhsT = UTT._mk(rt * D + iM * 128, [[UTT.ncols, 128], [1, 128]])
                        last = eng.matmul(pt.ap()[:, :], lhsT,
                                          AT._mk(rt * T + nch * 512, [[AT.ncols, 128], [1, 512]]),
                                          start=False, stop=(rt == NR - 1))
                    return last
                c.mm_group(fn, reads=[UBT.reg(), UTT.reg(), AB.reg(), AT.reg()],
                           writes=[preg], tag="a_gmm")
                sl = OP5.v3(NT, T)[:, iM, nch * 512:(nch + 1) * 512]
                c.tt(sl, pt.ap()[:, :], rc.sl(nch * 512, 512), OP.mult,
                     reads=[preg, rc.reg()], writes=[OP5.reg()], tag="a_gev1")
                c.ts(sl, sl, av.sl(2 + NT + iM, 1), None, OP.add, OP.bypass,
                     reads=[OP5.reg(), av.reg()], writes=[OP5.reg()], tag="a_gev2")
        WoT3 = c.dram[pfx + "WoT"].ap().rearrange("(k p) m -> p k m", p=128)
        for iM in range(NT):
            wb = wst[iM % 2]
            c.dma("gpsimd", wb._mk(0, [[wb.ncols, 128], [128, NT], [1, 128]]),
                  WoT3[:, :, iM * 128:(iM + 1) * 128],
                  reads=[(pfx + "WoT",)], writes=[wb.reg()], tag="a_ldwo")
            for nch in range(NCH):
                pt, preg = next_psum(c)
                def fn(eng, pt=pt, wb=wb, nch=nch):
                    last = None
                    for kc in range(NT):
                        last = eng.matmul(pt.ap()[:, :], wb.sl(kc * 128, 128),
                                          OP5.v3(NT, T)[:, kc, nch * 512:(nch + 1) * 512],
                                          start=(kc == 0), stop=(kc == NT - 1))
                    return last
                c.mm_group(fn, reads=[wb.reg(), OP5.reg()], writes=[preg], tag="a_omm")
                c.dma("sync", xc5.v3(1, 512, off=iM * 512),
                      xin4[:, iM, b, nch * 512:(nch + 1) * 512],
                      reads=[(XI,)], writes=[xc5.reg(iM)], tag="a_ldres")
                c.tt(xc5.v3(1, 512, off=iM * 512), pt.ap()[:, :],
                     xc5.v3(1, 512, off=iM * 512), OP.add,
                     reads=[preg, xc5.reg(iM)], writes=[xc5.reg(iM)], tag="a_res")
                c.dma("sync", xout4[:, iM, b, nch * 512:(nch + 1) * 512],
                      xc5.v3(1, 512, off=iM * 512),
                      reads=[xc5.reg(iM)], writes=[(XO,)], tag="a_st")


def build_prog_l1_attn(D=1024, T=2048, B=2):
    nc = bass.Bass("TRN2", target_bir_lowering=False, debug=False)
    c = Ctx(nc, D=D, T=T, B=B)
    c.dram = {}
    R = 256
    xin = _decl(nc, c, "xin", [B, D, T])
    xout = _decl(nc, c, "xout", [B, D, T], kind="ExternalOutput")
    for nm in ("Wq", "Wk"):
        _decl(nc, c, "L1_attn_" + nm, [D, D])
    _decl(nc, c, "L1_attn_WoT", [D, D])
    for nm in ("V_b", "W_b", "V_t", "W_t", "X_t"):
        _decl(nc, c, "L1_attn_" + nm, [D, R])
    _decl(nc, c, "L1_attn_U_bT", [R, D])
    _decl(nc, c, "L1_attn_U_tT", [R, D])
    _decl(nc, c, "L1_attn_bias_b", [128, D // 128])
    _decl(nc, c, "L1_attn_bias_t", [128, D // 128])
    _decl(nc, c, "L1_attn_alpha", [1, 1])
    setup_common(c)
    build_l1_attn(c, xin, xout)
    c.p.fence()
    c.p.emit()
    global _LAST_PROG
    _LAST_PROG = c.p
    return nc


def host_l1_attn(I, D=1024):
    NT = D // 128
    m = {}
    for nm in ("Wq", "Wk", "V_b", "W_b", "V_t", "W_t", "X_t"):
        m["L1_attn_" + nm] = I["L1_attn_" + nm]
    m["L1_attn_WoT"] = np.ascontiguousarray(I["L1_attn_Wo"].T)
    m["L1_attn_U_bT"] = np.ascontiguousarray(I["L1_attn_U_b"].T)
    m["L1_attn_U_tT"] = np.ascontiguousarray(I["L1_attn_U_t"].T)
    m["L1_attn_bias_b"] = _vecprep(I["L1_attn_bias_b"], NT)
    m["L1_attn_bias_t"] = _vecprep(I["L1_attn_bias_t"], NT)
    m["L1_attn_alpha"] = np.asarray(I["L1_attn_alpha"], np.float32).reshape(1, 1)
    return m



def build_l0_ffn_a(c: Ctx, xin, e_out, s_out, xn_out):
    XI = xin.ap().tensor.name; EO = e_out.ap().tensor.name; SO = s_out.ap().tensor.name; XNO = xn_out.ap().tensor.name
    NT, T, D, B = c.NT, c.T, c.D, c.B
    NP = c.NP
    NPC = NP // 512

    A2 = mkplane(c, "f_A2", NT * T, BF16)
    SQ2 = mkplane(c, "f_SQ2", NT * T, BF16)
    mb2 = mkplane(c, "f_mb2", T, F32)
    rb2 = mkplane(c, "f_rb2", T, F32)
    rows2_t = mkplane(c, "f_rows2", T, F32)
    phip = mkplane(c, "f_phi", NT * NP, BF16)
    est = mkplane(c, "f_est", NP, BF16)
    sa = mkplane(c, "f_sa", NP, BF16)
    sb2 = mkplane(c, "f_sb", NP, F32)

    rows2 = [Row(rows2_t, 0), Row(rows2_t, 32), Row(sb2, 0, 0), Row(sb2, 32, 0)]

    xin4 = xin.ap().rearrange("b (n p) t -> p n b t", p=128)
    c.dma("gpsimd", phip._mk(0, [[phip.ncols, 128], [NP, NT], [1, NP]]),
          c.dram["L0_ffn_phi"].ap().rearrange("(n p) q -> p n q", p=128),
          reads=[("L0_ffn_phi",)], writes=[phip.reg()], tag="f_ldphi")

    MT = T // 128
    for b in range(B):
        c.dma("gpsimd", A2.v3(NT, T), xin4[:, :, b, :], reads=[(XI,)],
              writes=[A2.reg()], tag="f_ldx")
        ln_cmajor(c, A2, A2.v3(NT, T), A2, A2.v3(NT, T), SQ2, mb2, rb2, rows2,
                  tag="f_ln")
        c.dma("sync", xn_out.ap()[b].rearrange("(n p) t -> p n t", p=128),
              A2.v3(NT, T), reads=[A2.reg()], writes=[(XNO,)], tag="f_stxn")
        for mt in range(MT):
            for npc in range(NPC):
                pt, preg = next_psum(c)
                def fn(eng, pt=pt, mt=mt, npc=npc):
                    last = None
                    for kc in range(NT):
                        lhsT = A2._mk(kc * T + mt * 128, [[A2.ncols, 128], [1, 128]])
                        last = eng.matmul(pt.ap()[:, :], lhsT,
                                          phip._mk(kc * NP + npc * 512, [[phip.ncols, 128], [1, 512]]),
                                          start=(kc == 0), stop=(kc == NT - 1))
                    return last
                c.mm_group(fn, reads=[A2.reg(), phip.reg()], writes=[preg], tag="f_lgmm")
                c.act(est.sl(npc * 512, 512), pt.ap()[:, :], AF.Exp,
                      reads=[preg], writes=[est.reg()], tag="f_exp")
            c.dma("sync", e_out.ap()[b, mt * 128:(mt + 1) * 128, :], est.sl(0, NP),
                  reads=[est.reg()], writes=[(EO,)], tag="f_ste")
    for mt in range(MT):
        c.dma("sync", est.sl(0, NP), e_out.ap()[0, mt * 128:(mt + 1) * 128, :],
              reads=[(EO,)], writes=[est.reg()], tag="f_lde0")
        c.dma("sync", sa.sl(0, NP), e_out.ap()[1, mt * 128:(mt + 1) * 128, :],
              reads=[(EO,)], writes=[sa.reg()], tag="f_lde1")
        c.tt(sb2.sl(0, NP), est.sl(0, NP), sa.sl(0, NP), OP.add,
             reads=[est.reg(), sa.reg()], writes=[sb2.reg()], tag="f_sadd")
        c.dma("sync", s_out.ap()[mt * 128:(mt + 1) * 128, :], sb2.sl(0, NP),
              reads=[sb2.reg()], writes=[(SO,)], tag="f_sts")


def build_l0_ffn_b(c: Ctx, xin, e_in, xn_in, s_all, xout, n_parts=8):
    XI = xin.ap().tensor.name; XO = xout.ap().tensor.name; EI = e_in.ap().tensor.name; XNI = xn_in.ap().tensor.name; SA = s_all.ap().tensor.name
    need_ident_bf(c)
    """Phase B: S = sum(s_all); D = E/S; slots; y; out; x2 = x1 + out."""
    NT, T, D, B = c.NT, c.T, c.D, c.B
    NP = c.NP
    NN = NP // 32
    NPC = NP // 512
    MT = T // 128

    xn2 = mkplane(c, "g_xn", NT * T, BF16)
    xnT = mkplane(c, "g_xnT", MT * D, BF16)
    slT = mkplane(c, "g_slT", NT * NP, BF16)
    ysb = mkplane(c, "g_y", (NP // 128) * D, BF16)
    rS = [mkplane(c, "g_rS0", NP, F32), mkplane(c, "g_rS1", NP, F32)]
    rcz = mkplane(c, "g_rcz", 256, F32)
    zr = mkplane(c, "g_zr", 256, F32, npart=1)
    wnb = [mkplane(c, f"g_wn{j}", NT * 512, BF16) for j in range(4)]
    bbc = [mkplane(c, "g_bb0", D, BF16), mkplane(c, "g_bb1", D, BF16)]
    ecb = [mkplane(c, "g_ec0", 256, BF16), mkplane(c, "g_ec1", 256, BF16)]
    etb = [mkplane(c, f"g_et{j}", 512, BF16) for j in range(4)]
    ebig = [mkplane(c, "g_eb0", NP, BF16), mkplane(c, "g_eb1", NP, BF16)]
    xch = mkplane(c, "g_xch", NT * 256, F32)
    otmp = mkplane(c, "g_otmp", 256, F32)
    pout = c.ps("g_pout", [128, 2048])
    ptb = c.ps("g_ptb", [128, 128], BF16)

    xin4 = xin.ap().rearrange("b (n p) t -> p n b t", p=128)
    xout4 = xout.ap().rearrange("b (n p) t -> p n b t", p=128)
    sred = c.dram["g_sred"]
    d_dram = c.dram["g_D"]

    for mt in range(MT):
        c.dma("sync", rS[0].sl(0, NP), s_all.ap()[0, mt * 128:(mt + 1) * 128, :],
              reads=[(SA,)], writes=[rS[0].reg()], tag="g_lds")
        for i in range(1, n_parts):
            c.dma("sync", rS[1].sl(0, NP), s_all.ap()[i, mt * 128:(mt + 1) * 128, :],
                  reads=[(SA,)], writes=[rS[1].reg()], tag="g_lds2")
            c.tt(rS[0].sl(0, NP), rS[0].sl(0, NP), rS[1].sl(0, NP), OP.add,
                 reads=[rS[0].reg(), rS[1].reg()], writes=[rS[0].reg()], tag="g_sacc")
        c.recip_pos(rS[0].sl(0, NP), rS[0].sl(0, NP), reads=[rS[0].reg()],
                    writes=[rS[0].reg()], tag="g_srec")
        for b in range(B):
            eb = ebig[b % 2]
            c.dma("sync", eb.sl(0, NP), e_in.ap()[b, mt * 128:(mt + 1) * 128, :],
                  reads=[(EI,)], writes=[eb.reg()], tag="g_dld")
            c.tt(eb.sl(0, NP), eb.sl(0, NP), rS[0].sl(0, NP), OP.mult,
                 reads=[eb.reg(), rS[0].reg()], writes=[eb.reg()], tag="g_dmul")
            c.dma("sync", d_dram.ap()[b, mt * 128:(mt + 1) * 128, :], eb.sl(0, NP),
                  reads=[eb.reg()], writes=[("g_D",)], tag="g_dst")

    precast_weights(c, ["L0_ffn_W"], slT, slT.ncols // 2)
    for b in range(B):
        c.dma("sync", xn2.v3(NT, T), xn_in.ap()[b].rearrange("(n p) t -> p n t", p=128),
              reads=[(XNI,)], writes=[xn2.reg()], tag="g_ldxn")
        for mt in range(MT):
            for kc in range(NT):
                ptv = ptb.ap()[:, :]
                src = xn2._mk(kc * T + mt * 128, [[xn2.ncols, 128], [1, 128]])
                def fn(eng, ptv=ptv, src=src):
                    return eng.transpose(ptv, src, c.ident_bf.sl(0, 128))
                c.p.op("tensor", fn, reads=[xn2.reg(), c.ident_bf.reg()],
                       writes=[("g_ptb",)], tag="g_xtr")
                c.copy(xnT._mk(mt * D + kc * 128, [[xnT.ncols, 128], [1, 128]]), ptv,
                       reads=[("g_ptb",)], writes=[xnT.reg()], tag="g_xtrev",
                       engine="vector")
        e3 = d_dram.ap()[b].rearrange("(q p) r -> p q r", p=128)
        for grp in range(2):
            for npc in range(NPC):
                for mt in range(MT):
                    eb = etb[mt % 4]
                    c.dma("sync", eb.sl(0, 512), e3[:, mt, npc * 512:(npc + 1) * 512],
                          reads=[("g_D",)], writes=[eb.reg()], tag="g_ldec2")
                    def fn(eng, eb=eb, grp=grp, mt=mt):
                        last = None
                        for dj in range(NT // 2):
                            dM = grp * (NT // 2) + dj
                            lhsT = xnT._mk(mt * D + dM * 128, [[xnT.ncols, 128], [1, 128]])
                            last = eng.matmul(pout.ap()[:, dj * 512:(dj + 1) * 512],
                                              lhsT, eb.sl(0, 512),
                                              start=(mt == 0), stop=(mt == MT - 1))
                        return last
                    c.mm_group(fn, reads=[xnT.reg(), eb.reg()], writes=[("g_pout",)],
                               tag="g_slmm")
                for dj in range(NT // 2):
                    dM = grp * (NT // 2) + dj
                    c.copy(slT._mk(dM * NP + npc * 512, [[slT.ncols, 128], [1, 512]]),
                           pout.ap()[:, dj * 512:(dj + 1) * 512],
                           reads=[("g_pout",)], writes=[slT.reg()], tag="g_slev",
                           engine="scalar")
        W3 = c.dram["bf_L0_ffn_W"].ap().rearrange("n (k p) m -> p n k m", p=128)
        for n in range(NN):
            bb = bbc[n % 2]
            j, base = n // 4, (n % 4) * 32
            c.dma("gpsimd", bb._mk(base * bb.ncols, [[bb.ncols, 32], [1, D]]),
                  c.dram["L0_ffn_bbc"].ap()[n],
                  reads=[("L0_ffn_bbc",)], writes=[bb.reg()], tag="g_ldb")
            DW = min(512, D)
            for dc in range(D // DW):
                wb = wnb[(n * (D // DW) + dc) % 4]
                c.dma("sync", wb._mk(0, [[wb.ncols, 128], [DW, NT], [1, DW]]),
                      W3[:, n, :, dc * DW:(dc + 1) * DW],
                      reads=[("bf_L0_ffn_W",)], writes=[wb.reg()], tag="g_ldW")
                pt2, preg2 = next_psum(c)
                def fn(eng, pt2=pt2, n=n, wb=wb, dc=dc, DW=DW):
                    last = None
                    for kc in range(NT):
                        lhsT = slT._mk(kc * NP + n * 32, [[slT.ncols, 128], [1, 32]])
                        last = eng.matmul(pt2.ap()[0:32, 0:DW], lhsT,
                                          wb._mk(kc * DW, [[wb.ncols, 128], [1, DW]]),
                                          start=(kc == 0), stop=(kc == NT - 1))
                    return last
                c.mm_group(fn, reads=[slT.reg(), wb.reg()], writes=[preg2], tag="g_ymm")
                dst = ysb._mk(base * ysb.ncols + j * D + dc * DW, [[ysb.ncols, 32], [1, DW]])
                c.tt(dst, pt2.ap()[0:32, 0:DW],
                     bb._mk(base * bb.ncols + dc * DW, [[bb.ncols, 32], [1, DW]]),
                     OP.add, reads=[preg2, bb.reg()], writes=[ysb.reg()], tag="g_yev")
        prow = c.psum_pool[0]
        prreg = (prow.ap().tensor.name,)
        for mc in range(T // 256):
            for npc in range(NP // 128):
                eb = ecb[npc % 2]
                def fn(eng, eb=eb, b=b, mc=mc, npc=npc):
                    return eng.dma_start(out=eb.sl(0, 256),
                                         in_=e_in.ap()[b, mc * 256:(mc + 1) * 256,
                                                       npc * 128:(npc + 1) * 128],
                                         transpose=True)
                c.p.op("sync", fn, reads=[(EI,)], writes=[eb.reg()], tag="g_ldec",
                       kind="dma")
                def fn(eng, eb=eb, npc=npc):
                    last = None
                    for dM in range(NT):
                        st = (npc == 0) and (dM % 2 == 0)
                        sp = (npc == NP // 128 - 1) and (dM % 2 == 1)
                        last = eng.matmul(pout.ap()[:, dM * 256:(dM + 1) * 256], lhsT_f(npc, dM),
                                          eb.sl(0, 256), start=st, stop=sp)
                    return last
                def lhsT_f(npc, dM):
                    return ysb._mk(npc * D + dM * 128, [[ysb.ncols, 128], [1, 128]])
                c.mm_group(fn, reads=[ysb.reg(), eb.reg()], writes=[("g_pout",)],
                           tag="g_omm")
                def fn(eng, eb=eb, npc=npc, prow=prow):
                    return eng.matmul(prow.ap()[0:1, 0:256], c.ones_col.sl(0, 1),
                                      eb.sl(0, 256), start=(npc == 0),
                                      stop=(npc == NP // 128 - 1))
                c.mm_group(fn, reads=[c.ones_col.reg(), eb.reg()], writes=[prreg],
                           tag="g_zmm")
            c.copy(zr.sl(0, 256, np_=1), prow.ap()[0:1, 0:256], reads=[prreg],
                   writes=[zr.reg()], tag="g_zev", engine="scalar")
            c.recip(zr.sl(0, 256, np_=1), zr.sl(0, 256, np_=1), reads=[zr.reg()],
                    writes=[zr.reg()], tag="g_zrec")
            pt3 = c.psum_pool[1]
            preg3 = (pt3.ap().tensor.name,)
            def fn(eng, pt3=pt3):
                return eng.matmul(pt3.ap()[:, 0:256], c.ones128._mk(0, [[128, 1], [1, 128]]),
                                  zr.sl(0, 256, np_=1), start=True, stop=True)
            c.mm_group(fn, reads=[zr.reg(), c.ones128.reg()], writes=[preg3], tag="g_zbc")
            c.copy(rcz.sl(0, 256), pt3.ap()[:, 0:256], reads=[preg3],
                   writes=[rcz.reg()], tag="g_zbcev", engine="scalar")
            c.dma("sync", xch.v3(NT, 256),
                  xin4[:, :, b, mc * 256:(mc + 1) * 256],
                  reads=[(XI,)], writes=[xch.reg()], tag="g_ldx1")
            for dM in range(NT):
                c.tt(otmp.sl(0, 256), pout.ap()[:, dM * 256:(dM + 1) * 256],
                     rcz.sl(0, 256), OP.mult,
                     reads=[("g_pout",), rcz.reg()], writes=[otmp.reg()], tag="g_oev1")
                sl = xch.v3(NT, 256)[:, dM, :]
                c.tt(sl, otmp.sl(0, 256), sl, OP.add,
                     reads=[otmp.reg(), xch.reg()], writes=[xch.reg()], tag="g_oev2")
            c.dma("sync", xout4[:, :, b, mc * 256:(mc + 1) * 256], xch.v3(NT, 256),
                  reads=[xch.reg()], writes=[(XO,)], tag="g_stx2")


def build_prog_l0_ffn_a(D=1024, T=2048, B=2, NP=2048):
    nc = bass.Bass("TRN2", target_bir_lowering=False, debug=False)
    c = Ctx(nc, D=D, T=T, B=B)
    c.dram = {}
    c.NP = NP
    xin = _decl(nc, c, "xin", [B, D, T])
    _decl(nc, c, "L0_ffn_phi", [D, NP])
    e_out = _decl(nc, c, "e_out", [B, T, NP], kind="ExternalOutput", dtype=BF16)
    s_out = _decl(nc, c, "s_out", [T, NP], kind="ExternalOutput")
    xn_out = _decl(nc, c, "xn_out", [B, D, T], kind="ExternalOutput", dtype=BF16)
    setup_common(c)
    build_l0_ffn_a(c, xin, e_out, s_out, xn_out)
    c.p.fence()
    c.p.emit()
    global _LAST_PROG
    _LAST_PROG = c.p
    return nc


def build_prog_l0_ffn_b(D=1024, T=2048, B=2, NP=2048, n_parts=8):
    nc = bass.Bass("TRN2", target_bir_lowering=False, debug=False)
    c = Ctx(nc, D=D, T=T, B=B)
    c.dram = {}
    c.NP = NP
    NN = NP // 32
    xin = _decl(nc, c, "xin", [B, D, T])
    e_in = _decl(nc, c, "e_in", [B, T, NP], dtype=BF16)
    xn_in = _decl(nc, c, "xn_in", [B, D, T], dtype=BF16)
    s_all = _decl(nc, c, "s_all", [n_parts, T, NP])
    xout = _decl(nc, c, "xout", [B, D, T], kind="ExternalOutput")
    _decl(nc, c, "L0_ffn_W", [NN, D, D])
    _decl(nc, c, "bf_L0_ffn_W", [NN, D, D], kind="Internal", dtype=BF16)
    _decl(nc, c, "L0_ffn_bbc", [NN, 32, D])
    _decl(nc, c, "g_sred", [T, NP], kind="Internal")
    _decl(nc, c, "g_D", [B, T, NP], kind="Internal", dtype=BF16)
    setup_common(c, npsum=3)
    build_l0_ffn_b(c, xin, e_in, xn_in, s_all, xout, n_parts=n_parts)
    c.p.fence()
    c.p.emit()
    global _LAST_PROG
    _LAST_PROG = c.p
    return nc


def host_l0_ffn(I, D=1024):
    phi = np.asarray(I["L0_ffn_phi"])
    NN = phi.shape[1]
    m = {
        "L0_ffn_phi": np.ascontiguousarray(phi.reshape(D, -1)),
        "L0_ffn_W": I["L0_ffn_W"],
        "L0_ffn_bbc": np.ascontiguousarray(
            np.broadcast_to(np.asarray(I["L0_ffn_b"])[:, None, :], (NN, 32, D))),
    }
    return m



_PROGS = {}


def _get_prog(name, builder):
    if name not in _PROGS:
        _PROGS[name] = builder()
    return _PROGS[name]


def kernel5(**I):
    D, T, NP = 1024, 2048, 2048
    BT, BPC = 16, 2
    x = np.asarray(I["L0_attn_x"], np.float32)

    xs = [np.ascontiguousarray(x[c * BPC:(c + 1) * BPC].transpose(0, 2, 1))
          for c in range(NCORES)]

    nc1 = _get_prog("l0a", lambda: build_prog_l0_attn(D=D, T=T, B=BPC))
    w1 = host_l0_attn(I, D=D)
    maps = [dict(w1, xin=xs[c]) for c in range(NCORES)]
    r1 = run_bass_kernel_spmd(nc1, maps, core_ids=list(range(NCORES)))
    x1 = [r["xout"] for r in r1.results]

    nc2 = _get_prog("l0fa", lambda: build_prog_l0_ffn_a(D=D, T=T, B=BPC, NP=NP))
    wf = host_l0_ffn(I, D=D)
    maps = [{"xin": x1[c], "L0_ffn_phi": wf["L0_ffn_phi"]} for c in range(NCORES)]
    r2 = run_bass_kernel_spmd(nc2, maps, core_ids=list(range(NCORES)))
    S_all = np.stack([r["s_out"] for r in r2.results])

    nc3 = _get_prog("l0fb", lambda: build_prog_l0_ffn_b(D=D, T=T, B=BPC, NP=NP,
                                                        n_parts=NCORES))
    maps = [{"xin": x1[c], "e_in": r2.results[c]["e_out"],
             "xn_in": r2.results[c]["xn_out"], "s_all": S_all,
             "L0_ffn_W": wf["L0_ffn_W"], "L0_ffn_bbc": wf["L0_ffn_bbc"]}
            for c in range(NCORES)]
    r3 = run_bass_kernel_spmd(nc3, maps, core_ids=list(range(NCORES)))
    x2 = [r["xout"] for r in r3.results]

    nc4 = _get_prog("l1a", lambda: build_prog_l1_attn(D=D, T=T, B=BPC))
    w4 = host_l1_attn(I, D=D)
    maps = [dict(w4, xin=x2[c]) for c in range(NCORES)]
    r4 = run_bass_kernel_spmd(nc4, maps, core_ids=list(range(NCORES)))
    x3 = [r["xout"] for r in r4.results]

    nc5 = _get_prog("l1f", lambda: build_prog_l1_ffn(D=D, T=T, B=BPC))
    w5 = host_l1_ffn(I, D=D)
    maps = [dict(w5, xin=x3[c]) for c in range(NCORES)]
    r5 = run_bass_kernel_spmd(nc5, maps, core_ids=list(range(NCORES)))

    out = np.concatenate([r["xout"].transpose(0, 2, 1) for r in r5.results], axis=0)
    return np.ascontiguousarray(out.astype(np.float32))



def build_prog_launchA(D=1024, T=2048, B=2, NP=2048):
    nc = bass.Bass("TRN2", target_bir_lowering=False, debug=False)
    c = Ctx(nc, D=D, T=T, B=B)
    c.dram = {}
    c.NP = NP
    NT = c.NT
    HID = 4 * D
    xin = _decl(nc, c, "xin", [B, D, T])
    x1 = _decl(nc, c, "x1", [B, D, T], kind="ExternalOutput")
    for nm in ("WkT", "WvT", "WrT", "WoT", "fWrT"):
        _decl(nc, c, "L0_attn_" + nm, [D, D])
        _decl(nc, c, "bf_L0_attn_" + nm, [D, D], kind="Internal", dtype=BF16)
    _decl(nc, c, "L0_attn_fWkT", [D, HID])
    _decl(nc, c, "bf_L0_attn_fWkT", [D, HID], kind="Internal", dtype=BF16)
    _decl(nc, c, "L0_attn_fWvT", [HID, D])
    _decl(nc, c, "bf_L0_attn_fWvT", [HID, D], kind="Internal", dtype=BF16)
    for nm in ("ln1_g", "ln1_b", "ln2_g", "ln2_b", "tmk", "tmv", "tmr",
               "f_tmk", "f_tmr", "time_decay", "time_first"):
        _decl(nc, c, "L0_attn_" + nm, [128, NT])
    _decl(nc, c, "L0_ffn_phi", [D, NP])
    e_out = _decl(nc, c, "e_out", [B, T, NP], kind="ExternalOutput", dtype=BF16)
    s_out = _decl(nc, c, "s_out", [T, NP], kind="ExternalOutput")
    xn_out = _decl(nc, c, "xn_out", [B, D, T], kind="ExternalOutput", dtype=BF16)
    setup_common(c, npsum=0)
    c.push_scope(6)
    build_l0_attn(c, xin, x1, None)
    c.pop_scope()
    c.push_scope(6)
    build_l0_ffn_a(c, x1, e_out, s_out, xn_out)
    c.pop_scope()
    c.p.fence()
    c.p.emit()
    global _LAST_PROG
    _LAST_PROG = c.p
    return nc


def build_prog_launchB(D=1024, T=2048, B=2, NP=2048, n_parts=8):
    nc = bass.Bass("TRN2", target_bir_lowering=False, debug=False)
    c = Ctx(nc, D=D, T=T, B=B)
    c.dram = {}
    c.NP = NP
    NN = NP // 32
    R = 256
    F = 4 * D
    x1 = _decl(nc, c, "x1", [B, D, T])
    e_in = _decl(nc, c, "e_in", [B, T, NP], dtype=BF16)
    xn_in = _decl(nc, c, "xn_in", [B, D, T], dtype=BF16)
    s_all = _decl(nc, c, "s_all", [n_parts, T, NP])
    x2 = _decl(nc, c, "x2", [B, D, T], kind="Internal")
    x3 = _decl(nc, c, "x3", [B, D, T], kind="Internal")
    xout = _decl(nc, c, "xout", [B, D, T], kind="ExternalOutput")
    _decl(nc, c, "L0_ffn_W", [NN, D, D])
    _decl(nc, c, "bf_L0_ffn_W", [NN, D, D], kind="Internal", dtype=BF16)
    _decl(nc, c, "L0_ffn_bbc", [NN, 32, D])
    _decl(nc, c, "g_sred", [T, NP], kind="Internal")
    _decl(nc, c, "g_D", [B, T, NP], kind="Internal", dtype=BF16)
    for nm in ("Wq", "Wk"):
        _decl(nc, c, "L1_attn_" + nm, [D, D])
    _decl(nc, c, "L1_attn_WoT", [D, D])
    for nm in ("V_b", "W_b", "V_t", "W_t", "X_t"):
        _decl(nc, c, "L1_attn_" + nm, [D, R])
    _decl(nc, c, "L1_attn_U_bT", [R, D])
    _decl(nc, c, "L1_attn_U_tT", [R, D])
    _decl(nc, c, "L1_attn_bias_b", [128, D // 128])
    _decl(nc, c, "L1_attn_bias_t", [128, D // 128])
    _decl(nc, c, "L1_attn_alpha", [1, 1])
    _decl(nc, c, "L1_ffn_w_red", [D, 16])
    _decl(nc, c, "L1_ffn_wg", [8, 16])
    _decl(nc, c, "L1_ffn_w1", [8, D, F])
    _decl(nc, c, "bf_L1_ffn_w1", [8, D, F], kind="Internal", dtype=BF16)
    _decl(nc, c, "L1_ffn_b1c", [128, 8 * (F // 128)])
    _decl(nc, c, "L1_ffn_w2", [8, F, D])
    _decl(nc, c, "bf_L1_ffn_w2", [8, F, D], kind="Internal", dtype=BF16)
    _decl(nc, c, "L1_ffn_b2", [8, D])
    _decl(nc, c, "m_scr16", [1, 16], kind="Internal")
    _decl(nc, c, "m_scr8x16", [8, 16], kind="Internal")
    setup_common(c, npsum=0)
    c.push_scope(3)
    build_l0_ffn_b(c, x1, e_in, xn_in, s_all, x2, n_parts=n_parts)
    c.pop_scope()
    c.push_scope(6)
    build_l1_attn(c, x2, x3)
    c.pop_scope()
    c.push_scope(6)
    build_l1_ffn(c, x3, xout)
    c.pop_scope()
    c.p.fence()
    c.p.emit()
    global _LAST_PROG
    _LAST_PROG = c.p
    return nc


def kernel(**I):
    D, T, NP = 1024, 2048, 2048
    BPC = 2
    x = np.asarray(I["L0_attn_x"], np.float32)
    xs = [np.ascontiguousarray(x[c * BPC:(c + 1) * BPC].transpose(0, 2, 1))
          for c in range(NCORES)]
    wa = host_l0_attn(I, D=D)
    wf = host_l0_ffn(I, D=D)
    w4 = host_l1_attn(I, D=D)
    w5 = host_l1_ffn(I, D=D)

    ncA = _get_prog("A", lambda: build_prog_launchA(D=D, T=T, B=BPC, NP=NP))
    maps = [dict(wa, xin=xs[c], L0_ffn_phi=wf["L0_ffn_phi"]) for c in range(NCORES)]
    rA = run_bass_kernel_spmd(ncA, maps, core_ids=list(range(NCORES)))
    S_all = np.stack([r["s_out"] for r in rA.results])

    ncB = _get_prog("B", lambda: build_prog_launchB(D=D, T=T, B=BPC, NP=NP,
                                                    n_parts=NCORES))
    maps = []
    for c in range(NCORES):
        m = dict(w4)
        m.update(w5)
        m.update(L0_ffn_W=wf["L0_ffn_W"], L0_ffn_bbc=wf["L0_ffn_bbc"],
                 x1=rA.results[c]["x1"], e_in=rA.results[c]["e_out"],
                 xn_in=rA.results[c]["xn_out"], s_all=S_all)
        maps.append(m)
    rB = run_bass_kernel_spmd(ncB, maps, core_ids=list(range(NCORES)))
    out = np.concatenate([r["xout"].transpose(0, 2, 1) for r in rB.results], axis=0)
    return np.ascontiguousarray(out.astype(np.float32))


# version 07
# speedup vs baseline: 1.0555x; 1.0170x over previous
import numpy as np
import concourse.bass as bass
import concourse.mybir as mybir
from concourse.bass_utils import run_bass_kernel_spmd

F32 = mybir.dt.float32
BF16 = mybir.dt.bfloat16
AF = mybir.ActivationFunctionType
OP = mybir.AluOpType
AX = mybir.AxisListType

NCORES = 8


ENGINES = ("sync", "gpsimd", "tensor", "vector", "scalar")
_W = {"sync": 16, "gpsimd": 16, "tensor": 1, "vector": 1, "scalar": 1}


class Prog:

    def __init__(self, nc):
        self.nc = nc
        self.ops = []
        self.eng_ops = {e: [] for e in ENGINES}
        self._writers = {}
        self._readers = {}
        self._keys = {}
        self.inst_tags = {}
        self._gq_count = 0

    @staticmethod
    def _base(r):
        return r[0] if isinstance(r, tuple) else r

    def _alias(self, r):
        base = self._base(r)
        seen = self._keys.setdefault(base, set())
        seen.add(r)
        if r == base:
            return list(seen)
        out = [r]
        if base in seen:
            out.append(base)
        return out

    @staticmethod
    def _norm(r):
        return r[0] if isinstance(r, tuple) and len(r) == 1 else r

    def op(self, engine, fn, reads=(), writes=(), tag="", kind="compute"):
        assert engine in ENGINES, engine
        if engine == "sync":
            kind = "dma"
        reads = [self._norm(r) for r in reads]
        writes = [self._norm(r) for r in writes]
        i = len(self.ops)
        deps = set()
        for r in reads:
            for k in self._alias(r):
                w = self._writers.get(k)
                if w is not None:
                    deps.add(w)
        wr_aliases = []
        for r in writes:
            aks = self._alias(r)
            wr_aliases.append((r, aks))
            for k in aks:
                w = self._writers.get(k)
                if w is not None:
                    deps.add(w)
                for rd in self._readers.get(k, ()):
                    deps.add(rd)
        deps.discard(i)
        for r in reads:
            self._readers.setdefault(r, []).append(i)
        for r, aks in wr_aliases:
            if r == self._base(r):
                for k in list(self._keys.get(r, ())):
                    self._writers[k] = i
                    self._readers[k] = []
            self._writers[r] = i
            self._readers[r] = []
        gq = self._gq_count if (engine == "gpsimd" and kind == "dma") else -1
        wm = -1
        for d in deps:
            o = self.ops[d]
            wm = max(wm, o[5] if o[6] < 0 else o[6])
        self.ops.append([engine, fn, deps, tag, kind, wm, gq])
        if gq >= 0:
            self._gq_count += 1
        self.eng_ops[engine].append(i)
        return i

    def barrier(self, tag="barrier"):
        lasts = [self.eng_ops[f][-1] for f in ENGINES if self.eng_ops[f]]
        for e in ENGINES:
            i = len(self.ops)
            deps = set(lasts)
            deps.discard(i)
            wm = -1
            for d in deps:
                o = self.ops[d]
                wm = max(wm, o[5] if o[6] < 0 else o[6])
            self.ops.append([e, (lambda eng: eng.nop()), deps, tag, "compute", wm, -1])
            self.eng_ops[e].append(i)

    def fence(self, engine="sync", tag="final_fence"):
        i = len(self.ops)
        deps = set()
        for f in ENGINES:
            if self.eng_ops[f]:
                deps.add(self.eng_ops[f][-1])
        deps.discard(i)
        wm = -1
        for d in deps:
            o = self.ops[d]
            wm = max(wm, o[5] if o[6] < 0 else o[6])
        self.ops.append([engine, (lambda eng: eng.nop()), deps, tag, "compute", wm, -1])
        self.eng_ops[engine].append(i)
        return i

    def emit(self, dma_depth=8):
        nc = self.nc
        n = len(self.ops)
        sem_names = [f"se_{e}" for e in ENGINES if e != "sync"]
        sem_names += [f"sq_sync{j}" for j in range(dma_depth)]
        sem_names += ["gq_mono"]
        cms, sems = [], {}
        for nm in sem_names:
            cm = nc.semaphore(nm)
            sems[nm] = cm.__enter__()
            cms.append(cm)
        comp = [None] * n
        throttle = [None] * n
        etick = {e: 0 for e in ENGINES}
        scnt = 0
        for i, (e, _, deps, _, kind, wm, gq) in enumerate(self.ops):
            if e == "sync":
                k = scnt
                scnt += 1
                snm = f"sq_sync{k % dma_depth}"
                comp[i] = (snm, 16 * (k // dma_depth + 1))
                if k >= dma_depth:
                    throttle[i] = (snm, 16 * (k // dma_depth))
            elif gq >= 0:
                comp[i] = ("gq_mono", 16 * (gq + 1))
                if gq > 0:
                    throttle[i] = ("gq_mono", 16 * gq)
            else:
                etick[e] += 1
                comp[i] = (f"se_{e}", etick[e])
        blk_cm = nc.Block()
        block = blk_cm.__enter__()
        prog = self

        def body_for(e):
            def body(eng):
                seen = {nm: 0 for nm in sem_names}
                for i in prog.eng_ops[e]:
                    _, fn, deps, tag, kind, wm, gq = prog.ops[i]
                    need = {}
                    for d in deps:
                        snm, v = comp[d]
                        if v > need.get(snm, 0):
                            need[snm] = v
                    if throttle[i] is not None:
                        snm, v = throttle[i]
                        if v > need.get(snm, 0):
                            need[snm] = v
                    for snm in sem_names:
                        v = need.get(snm, 0)
                        if v > seen[snm]:
                            eng.wait_ge(sems[snm], v)
                            seen[snm] = v
                    try:
                        inst = fn(eng)
                    except Exception as ex:
                        raise RuntimeError(f"emit failed at op tag={tag} engine={e}: {ex}") from ex
                    assert inst is not None, f"op fn returned None: {e} {tag}"
                    try:
                        prog.inst_tags[inst.ins.name] = (tag, i)
                    except Exception:
                        pass
                    snm, v = comp[i]
                    inst.then_inc(sems[snm], 16 if (e == "sync" or gq >= 0) else 1)
            return body

        for e in ENGINES:
            getattr(block, e)(body_for(e))
        blk_cm.__exit__(None, None, None)
        for cm in cms:
            cm.__exit__(None, None, None)



def ap3(t, off, dims):
    return bass.AP(t.ap().tensor, off, [list(d) for d in dims])


class Ctx:

    def __init__(self, nc, D=1024, T=2048, B=2):
        self.nc = nc
        self.p = Prog(nc)
        self.D, self.T, self.B = D, T, B
        self.NT = D // 128
        self.NN = T // 512
        self.bufs = {}
        self._uid = 0

    def uid(self):
        self._uid += 1
        return self._uid

    def sb(self, name, shape, dtype):
        if getattr(self, "_scopes", None):
            g = self.nc.sbuf_tensor(name, shape, dtype)
            t = g.__enter__()
            self._scopes[-1].append(g)
        else:
            t = self.nc.alloc_sbuf_tensor(name, shape, dtype)
        self.bufs[name] = t
        return t

    def ps(self, name, shape, dtype=F32):
        if getattr(self, "_scopes", None):
            g = self.nc.psum_tensor(name, shape, dtype)
            t = g.__enter__()
            self._scopes[-1].append(g)
        else:
            t = self.nc.alloc_psum_tensor(name, shape, dtype)
        self.bufs[name] = t
        return t

    def push_scope(self, npsum=4):
        if not hasattr(self, "_scopes"):
            self._scopes = []
        self._scopes.append([])
        u = self.uid()
        self.psum_pool = [self.ps(f"pp_s{u}_{j}", [128, 512]) for j in range(npsum)]
        self._psum_i = 0

    def pop_scope(self):
        self.p.barrier()
        for g in reversed(self._scopes.pop()):
            g.__exit__(None, None, None)

    def dma(self, q, dst, src, reads, writes, tag="dma"):
        self.p.op(q, (lambda eng, d=dst, s=src: eng.dma_start(out=d, in_=s)),
                  reads=reads, writes=writes, tag=tag, kind="dma")

    def memset(self, dst, val, writes, tag="memset", engine="vector"):
        self.p.op(engine, (lambda eng, d=dst, v=val: eng.memset(d, v)),
                  writes=writes, tag=tag)

    def tt(self, out, in0, in1, op, reads, writes, tag="tt", engine="vector"):
        self.p.op(engine,
                  (lambda eng, o=out, a=in0, b=in1, c=op: eng.tensor_tensor(out=o, in0=a, in1=b, op=c)),
                  reads=reads, writes=writes, tag=tag)

    def ts(self, out, in0, s1, s2, op0, op1, reads, writes, tag="ts", engine="vector"):
        def fn(eng, o=out, a=in0, s1=s1, s2=s2, op0=op0, op1=op1):
            return eng.tensor_scalar(out=o, in0=a, scalar1=s1, scalar2=s2, op0=op0, op1=op1)
        self.p.op(engine, fn, reads=reads, writes=writes, tag=tag)

    def stt(self, out, in0, scal, in1, op0, op1, reads, writes, tag="stt", engine="vector"):
        def fn(eng, o=out, a=in0, s=scal, b=in1, op0=op0, op1=op1):
            return eng.scalar_tensor_tensor(out=o, in0=a, scalar=s, in1=b, op0=op0, op1=op1)
        self.p.op(engine, fn, reads=reads, writes=writes, tag=tag)

    def act(self, out, in_, func, reads, writes, bias=0.0, scale=1.0, tag="act"):
        def fn(eng, o=out, i=in_, f=func, b=bias, s=scale):
            return eng.activation(o, i, f, bias=b, scale=s)
        self.p.op("scalar", fn, reads=reads, writes=writes, tag=tag)

    def copy(self, out, in_, reads, writes, tag="copy", engine="vector"):
        if engine == "scalar":
            fn = lambda eng, o=out, i=in_: eng.copy(o, i)
        else:
            fn = lambda eng, o=out, i=in_: eng.tensor_copy(o, i)
        self.p.op(engine, fn, reads=reads, writes=writes, tag=tag)

    def scan(self, out, data0, data1, initial, op0, op1, reads, writes, tag="scan"):
        def fn(eng, o=out, d0=data0, d1=data1, ini=initial, op0=op0, op1=op1):
            return eng.tensor_tensor_scan(out=o, data0=d0, data1=d1, initial=ini, op0=op0, op1=op1)
        self.p.op("vector", fn, reads=reads, writes=writes, tag=tag)

    def recip(self, out, in_, reads, writes, tag="recip"):
        self.p.op("vector", (lambda eng, o=out, i=in_: eng.reciprocal(o, i)),
                  reads=reads, writes=writes, tag=tag)

    def recip_pos(self, out, in_, reads, writes, tag="rcp"):
        self.act(out, in_, AF.Ln, reads=reads, writes=writes, tag=tag + "_ln")
        self.act(out, out, AF.Exp, scale=-1.0, reads=list(writes), writes=writes,
                 tag=tag + "_ex")

    def mm(self, out, lhsT, rhs, reads, writes, start=True, stop=True, tag="mm"):
        def fn(eng, o=out, l=lhsT, r=rhs, st=start, sp=stop):
            return eng.matmul(o, l, r, start=st, stop=sp)
        self.p.op("tensor", fn, reads=reads, writes=writes, tag=tag)

    def mm_group(self, fn, reads, writes, tag="mmg"):
        self.p.op("tensor", fn, reads=reads, writes=writes, tag=tag)

    def transpose(self, out, in_, ident, reads, writes, tag="tr"):
        def fn(eng, o=out, i=in_, d=ident):
            return eng.transpose(o, i, d)
        self.p.op("tensor", fn, reads=reads, writes=writes, tag=tag)



class Plane:

    def __init__(self, t, ncols, npart=128):
        self.t = t
        self.name = t.ap().tensor.name if hasattr(t.ap().tensor, "name") else t.name
        self.ncols = ncols
        self.npart = npart

    def _mk(self, off, dims):
        return bass.AP(self.t.ap().tensor, off, [list(d) for d in dims])

    def sl(self, c0, n, p0=0, np_=None):
        np_ = self.npart if np_ is None else np_
        return self._mk(p0 * self.ncols + c0, [[self.ncols, np_], [1, n]])

    def v3(self, ntile, tlen, stride=None, off=0):
        stride = tlen if stride is None else stride
        return self._mk(off, [[self.ncols, self.npart], [stride, ntile], [1, tlen]])

    def bc3(self, ntile, tlen, off=0):
        return self._mk(off, [[self.ncols, self.npart], [0, ntile], [1, tlen]])

    def reg(self, *key):
        return (self.name,) + key if key else (self.name,)


def mkplane(c, name, ncols, dtype, npart=128):
    return Plane(c.sb(name, [npart, ncols], dtype), ncols, npart)


class Row:

    def __init__(self, pl, base, col0=0):
        self.pl, self.base, self.col0 = pl, base, col0

    def sl(self, c0, n, np_=1):
        return self.pl._mk(self.base * self.pl.ncols + self.col0 + c0,
                           [[self.pl.ncols, np_], [1, n]])

    def reg(self, *k):
        return (self.pl.name, "row", self.base, self.col0) + k



def setup_common(c: Ctx, npsum=4):
    c.ones_col = mkplane(c, "ones_col", 1, BF16)
    c.ones128 = mkplane(c, "ones128", 128, F32)
    c.memset(c.ones_col.sl(0, 1), 1.0, writes=[c.ones_col.reg()], tag="init")
    c.memset(c.ones128.sl(0, 128), 1.0, writes=[c.ones128.reg()], tag="init")
    c.psum_pool = [c.ps(f"pp{j}", [128, 512]) for j in range(npsum)]
    c._psum_i = 0


def need_zero_col(c):
    if not hasattr(c, "zero_col"):
        c.zero_col = mkplane(c, "zero_col", 1, F32)
        c.memset(c.zero_col.sl(0, 1), 0.0, writes=[c.zero_col.reg()], tag="init")
    return c.zero_col


def need_ident32(c):
    if not hasattr(c, "ident32"):
        c.ident32 = mkplane(c, "ident32", 128, F32)
        c.memset(c.ident32.sl(0, 128), 1.0, writes=[c.ident32.reg()], tag="init")
        def fn(eng):
            return eng.affine_select(c.ident32.sl(0, 128), c.ident32.sl(0, 128),
                                     [[-1, 128]], OP.is_equal, 0.0, channel_multiplier=1)
        c.p.op("gpsimd", fn, reads=[c.ident32.reg()], writes=[c.ident32.reg()], tag="ident")
    return c.ident32


def need_ident_bf(c):
    if not hasattr(c, "ident_bf"):
        c.ident_bf = mkplane(c, "ident_bf", 128, BF16)
        c.memset(c.ident_bf.sl(0, 128), 1.0, writes=[c.ident_bf.reg()], tag="init")
        def fn(eng):
            return eng.affine_select(c.ident_bf.sl(0, 128), c.ident_bf.sl(0, 128),
                                     [[-1, 128]], OP.is_equal, 0.0, channel_multiplier=1)
        c.p.op("gpsimd", fn, reads=[c.ident_bf.reg()], writes=[c.ident_bf.reg()], tag="identb")
    return c.ident_bf


def need_e8(c):
    if not hasattr(c, "E8"):
        c.E8 = mkplane(c, "E8", 8 * 128, F32, npart=8)
        c.memset(c.E8.sl(0, 8 * 128, np_=8), 1.0, writes=[c.E8.reg()], tag="init")
        def fn(eng):
            ap = c.E8._mk(0, [[8 * 128, 8], [128, 8], [1, 128]])
            return eng.affine_select(ap, ap, [[-1, 8], [0, 128]], OP.is_equal, 0.0,
                                     channel_multiplier=1)
        c.p.op("gpsimd", fn, reads=[c.E8.reg()], writes=[c.E8.reg()], tag="e8")
    return c.E8


def next_psum(c):
    j = c._psum_i % len(c.psum_pool)
    c._psum_i += 1
    t = c.psum_pool[j]
    return t, (t.ap().tensor.name,)


def load_vec(c, name, dram_name, q="sync"):
    pl = mkplane(c, name, c.NT, F32)
    src = c.dram[dram_name]
    c.dma(q, pl.sl(0, c.NT), src.ap()[:, :], reads=[(dram_name,)], writes=[pl.reg()],
          tag=f"ld_{name}")
    return pl


def precast_weights(c, names, staging_pl, half_cols):
    i = 0
    for name in names:
        src = c.dram[name]
        dst = c.dram["bf_" + name]
        tot = 1
        for d in src.ap().shape:
            tot *= d
        hc = half_cols
        while hc > 128 and (tot % hc != 0 or (tot // hc) % 128 != 0):
            hc //= 2
        assert tot % hc == 0 and (tot // hc) % 128 == 0, (name, tot, half_cols)
        R = tot // hc
        for r0 in range(0, R, 128):
            h = i % 2
            stg = staging_pl.sl(h * half_cols, hc)
            sv = bass.AP(src.ap().tensor, r0 * hc, [[hc, 128], [1, hc]])
            dv = bass.AP(dst.ap().tensor, r0 * hc, [[hc, 128], [1, hc]])
            c.dma("gpsimd", stg, sv, reads=[(name,)], writes=[(staging_pl.name, h)],
                  tag="pc_ld")
            c.dma("sync", dv, stg, reads=[(staging_pl.name, h)],
                  writes=[("bf_" + name,)], tag="pc_st")
            i += 1



def ln_cmajor(c: Ctx, src_pl, src3, dst_pl, dst3, sq_pl, meanb, rstdb, rows,
              gcol=None, bcol=None, gb_regs=(), eps=1e-5, tag="ln", tlen=None,
              sq_parts=1, ntile=None):
    NT = c.NT if ntile is None else ntile
    T = c.T if tlen is None else tlen
    D = c.D
    r0, r1 = rows[0], rows[1]
    NCH = T // 512
    sq_parts = min(sq_parts, NCH)
    part_nch = NCH // sq_parts
    sq3 = lambda: sq_pl.v3(NT, T // sq_parts)
    for part in range(sq_parts):
        lo = part * part_nch
        c.act(sq_pl.v3(NT, part_nch * 512), src3[:, :, lo * 512:(lo + part_nch) * 512],
              AF.Square, reads=[src_pl.reg()], writes=[sq_pl.reg()], tag=tag + "_sq")
        for nch in range(lo, lo + part_nch):
            for row, rhs3, rbase in ((r0, src3[:, :, nch * 512:(nch + 1) * 512], src_pl.reg()),
                                     (r1, sq_pl.v3(NT, part_nch * 512)[:, :, (nch - lo) * 512:(nch - lo + 1) * 512],
                                      sq_pl.reg())):
                pt, preg = next_psum(c)
                def fn(eng, pt=pt, rhs3=rhs3):
                    last = None
                    for kc in range(NT):
                        last = eng.matmul(pt.ap()[0:1, :], c.ones_col.sl(0, 1),
                                          rhs3[:, kc, :],
                                          start=(kc == 0), stop=(kc == NT - 1))
                    return last
                c.mm_group(fn, reads=[rbase, c.ones_col.reg()], writes=[preg], tag=tag + "_red")
                c.copy(row.sl(nch * 512, 512, np_=1), pt.ap()[0:1, :],
                       reads=[preg], writes=[row.reg()], tag=tag + "_rowev", engine="scalar")
    a0, a32 = rows[2], rows[3]
    c.ts(r0.sl(0, T, np_=1), r0.sl(0, T, np_=1), 1.0 / D, None, OP.mult, OP.bypass,
         reads=[r0.reg()], writes=[r0.reg()], tag=tag + "_mean")
    c.ts(r1.sl(0, T, np_=1), r1.sl(0, T, np_=1), 1.0 / D, None, OP.mult, OP.bypass,
         reads=[r1.reg()], writes=[r1.reg()], tag=tag + "_msq")
    c.tt(a0.sl(0, T, np_=1), r0.sl(0, T, np_=1), r0.sl(0, T, np_=1), OP.mult,
         reads=[r0.reg()], writes=[a0.reg()], tag=tag + "_m2")
    c.dma("sync", a32.sl(0, T, np_=1), a0.sl(0, T, np_=1),
          reads=[a0.reg()], writes=[a32.reg()], tag=tag + "_m2cp")
    c.tt(r1.sl(0, T, np_=1), r1.sl(0, T, np_=1), a32.sl(0, T, np_=1), OP.subtract,
         reads=[r1.reg(), a32.reg()], writes=[r1.reg()], tag=tag + "_var")
    c.ts(r1.sl(0, T, np_=1), r1.sl(0, T, np_=1), float(eps), None, OP.add, OP.bypass,
         reads=[r1.reg()], writes=[r1.reg()], tag=tag + "_vare")
    c.act(a32.sl(0, T, np_=1), r1.sl(0, T, np_=1), AF.Ln, reads=[r1.reg()],
          writes=[a32.reg()], tag=tag + "_rv")
    c.act(a32.sl(0, T, np_=1), a32.sl(0, T, np_=1), AF.Exp, scale=-0.5,
          reads=[a32.reg()], writes=[a32.reg()], tag=tag + "_rstd")
    for row, dst in ((r0, meanb), (a32, rstdb)):
        for nch in range(NCH):
            pt, preg = next_psum(c)
            def fn(eng, pt=pt, row=row, nch=nch):
                ones_v = c.ones128._mk(row.base * 128, [[128, 1], [1, 128]])
                return eng.matmul(pt.ap()[:, :], ones_v,
                                  row.sl(nch * 512, 512, np_=1), start=True, stop=True)
            c.mm_group(fn, reads=[row.reg(), c.ones128.reg()], writes=[preg], tag=tag + "_bc")
            c.copy(dst.sl(nch * 512, 512), pt.ap()[:, :], reads=[preg],
                   writes=[dst.reg()], tag=tag + "_bcev", engine="scalar")
    mb3 = meanb.bc3(NT, T)
    rb3 = rstdb.bc3(NT, T)
    c.tt(dst3, src3, mb3, OP.subtract, reads=[src_pl.reg(), meanb.reg()],
         writes=[dst_pl.reg()], tag=tag + "_sub")
    c.tt(dst3, dst3, rb3, OP.mult, reads=[dst_pl.reg(), rstdb.reg()],
         writes=[dst_pl.reg()], tag=tag + "_app")
    if gcol is not None:
        for kc in range(NT):
            c.ts(dst3[:, kc, :], dst3[:, kc, :], gcol(kc), bcol(kc),
                 OP.mult, OP.add, reads=[dst_pl.reg()] + list(gb_regs),
                 writes=[dst_pl.reg()], tag=tag + "_gb")



def build_l0_attn(c: Ctx, xin, xout, w):
    XI = xin.ap().tensor.name; XO = xout.ap().tensor.name
    NT, T, D, B = c.NT, c.T, c.D, c.B
    NH = 4 * NT
    nc = c.nc

    A = mkplane(c, "A", NT * T, BF16)
    Bz = mkplane(c, "Bz", NT * (T + 1), BF16)
    C = mkplane(c, "C", NT * T, BF16)
    Dp = mkplane(c, "Dp", NT * T, BF16)
    E = mkplane(c, "E", max(NT * T, 4 * NT * 512), BF16)
    SC = mkplane(c, "SC", 4 * T, F32)
    wk = [mkplane(c, "wkbuf0", NT * 128, BF16), mkplane(c, "wkbuf1", NT * 128, BF16)]
    wbig = [mkplane(c, "wbig0", NT * 128, BF16), mkplane(c, "wbig1", NT * 128, BF16)]
    vecs = mkplane(c, "vecs", 16 * c.NT, F32)

    VNAMES = ["ln1_g", "ln1_b", "ln2_g", "ln2_b", "tmk", "tmv", "tmr",
              "f_tmk", "f_tmr", "time_decay", "time_first"]
    vcol = {}
    for j, nm in enumerate(VNAMES):
        vcol[nm] = j * NT
        c.dma("sync", vecs.sl(j * NT, NT), c.dram["L0_attn_" + nm].ap()[:, :],
              reads=[("L0_attn_" + nm,)], writes=[vecs.reg()], tag="ldv_" + nm)
    vcol["lam"] = 11 * NT
    vcol["eu"] = 12 * NT
    c.act(vecs.sl(11 * NT, NT), vecs.sl(vcol["time_decay"], NT), AF.Exp,
          reads=[vecs.reg()], writes=[vecs.reg()], tag="lam1")
    c.act(vecs.sl(11 * NT, NT), vecs.sl(11 * NT, NT), AF.Exp, scale=-1.0,
          reads=[vecs.reg()], writes=[vecs.reg()], tag="lam2")
    c.act(vecs.sl(12 * NT, NT), vecs.sl(vcol["time_first"], NT), AF.Exp,
          reads=[vecs.reg()], writes=[vecs.reg()], tag="eu")

    def vc(nm, kc):
        return vecs.sl(vcol[nm] + kc, 1)

    rows = [Row(SC, 0, 3 * T), Row(SC, 32, 3 * T), Row(SC, 0, 2 * T), Row(SC, 32, 2 * T)]

    meanb = Plane(SC.t, 4 * T)
    rstdb = Plane(SC.t, 4 * T)

    class SCView:
        def __init__(self, c0):
            self.c0 = c0
        def sl(self, off, n, np_=128):
            return SC._mk(self.c0 + off, [[4 * T, np_], [1, n]])
        def bc3(self, ntile, tlen, off=0):
            return SC._mk(self.c0 + off, [[4 * T, 128], [0, ntile], [1, tlen]])
        def reg(self, *k):
            return ("SC",) + k
    meanb, rstdb = SCView(0), SCView(T)
    ek, vz = SCView(0), SCView(T)
    asc, bsc = SCView(2 * T), SCView(3 * T)
    x1c = SCView(0)
    x0c = SCView(2 * T)

    xin3 = [xin.ap()[b].rearrange("(n p) t -> p n t", p=128) for b in range(B)]
    xout3 = [xout.ap()[b].rearrange("(n p) t -> p n t", p=128) for b in range(B)]

    Bz_xn = Bz.v3(NT, T, stride=T + 1, off=1)
    Bz_xx = Bz.v3(NT, T, stride=T + 1, off=0)
    NCH = T // 512
    c.dma("gpsimd", A.v3(NT, T), xin3[0], reads=[(XI,)], writes=[A.reg()],
          tag="ld_x0_pre")
    precast_weights(c, ["L0_attn_WkT", "L0_attn_WvT", "L0_attn_WrT", "L0_attn_WoT",
                        "L0_attn_fWrT", "L0_attn_fWkT", "L0_attn_fWvT"],
                    E, E.ncols // 2)

    def stream_wcols(dst_pl, src_name, i, nk, tag):
        src = c.dram["bf_" + src_name].ap().rearrange("(n p) m -> p n m", p=128)
        c.dma("sync", dst_pl._mk(0, [[dst_pl.ncols, 128], [128, nk], [1, 128]]),
              src[:, :, i * 128:(i + 1) * 128],
              reads=[("bf_" + src_name,)], writes=[dst_pl.reg()], tag=tag)

    def proj_mm(dst_pl, dst_off_fn, wname, rhs_pl, rhs3, act_fn, act_dtype_note, b, tagp,
                wbuf=None, nk=None, rhs_reg=None):
        nk = NT if nk is None else nk
        for i in range(NT):
            wbuf = wk[i % 2]
            stream_wcols(wbuf, wname, i, nk, tagp + "_w%d" % i)
            for nch in range(NCH):
                pt, preg = next_psum(c)
                def fn(eng, pt=pt, i=i, nch=nch, wbuf=wbuf):
                    last = None
                    for kc in range(nk):
                        last = eng.matmul(pt.ap()[:, :], wbuf.sl(kc * 128, 128),
                                          rhs3[:, kc, nch * 512:(nch + 1) * 512],
                                          start=(kc == 0), stop=(kc == nk - 1))
                    return last
                c.mm_group(fn, reads=[wbuf.reg(), rhs_reg or rhs_pl.reg()],
                           writes=[preg], tag=tagp + "_mm")
                act_fn(i, nch, pt, preg)

    for b in range(B):
        if b > 0:
            c.dma("gpsimd", A.v3(NT, T), xin3[b], reads=[(XI,)], writes=[A.reg()],
                  tag="ld_x0")
        ln_cmajor(c, A, A.v3(NT, T), A, A.v3(NT, T), C, meanb, rstdb, rows, tag="ln0")
        c.memset(Bz._mk(0, [[Bz.ncols, 128], [T + 1, NT], [1, 1]]), 0.0,
                 writes=[Bz.reg()], tag="bz_zero")
        ln_cmajor(c, A, A.v3(NT, T), Bz, Bz_xn, C, meanb, rstdb, rows, tag="ln1",
                  gcol=lambda kc: vc("ln1_g", kc), bcol=lambda kc: vc("ln1_b", kc),
                  gb_regs=[vecs.reg()])

        c.tt(C.v3(NT, T), Bz_xn, Bz_xx, OP.subtract, reads=[Bz.reg()], writes=[C.reg()],
             tag="mix_d")
        for kc in range(NT):
            c.stt(Dp.v3(NT, T)[:, kc, :], C.v3(NT, T)[:, kc, :], vc("tmk", kc),
                  Bz_xx[:, kc, :], OP.mult, OP.add,
                  reads=[C.reg(), Bz.reg(), vecs.reg()], writes=[Dp.reg()], tag="mix_xk")
            c.stt(E.v3(NT, T)[:, kc, :], C.v3(NT, T)[:, kc, :], vc("tmv", kc),
                  Bz_xx[:, kc, :], OP.mult, OP.add,
                  reads=[C.reg(), Bz.reg(), vecs.reg()], writes=[E.reg()], tag="mix_xv")

        for i in range(NT):
            wki = wk[i % 2]
            stream_wcols(wki, "L0_attn_WkT", i, NT, "wk%d" % i)
            for nch in range(NCH):
                pt, preg = next_psum(c)
                def fn(eng, pt=pt, i=i, nch=nch, wki=wki):
                    last = None
                    for kc in range(NT):
                        last = eng.matmul(pt.ap()[:, :], wki.sl(kc * 128, 128),
                                          Dp.v3(NT, T)[:, kc, nch * 512:(nch + 1) * 512],
                                          start=(kc == 0), stop=(kc == NT - 1))
                    return last
                c.mm_group(fn, reads=[wki.reg(), Dp.reg()], writes=[preg], tag="kmm")
                c.act(ek.sl(nch * 512, 512), pt.ap()[:, :], AF.Exp,
                      reads=[preg], writes=[ek.reg()], tag="ek")
            stream_wcols(wbig[0], "L0_attn_WvT", i, NT, "wv%d" % i)
            for nch in range(NCH):
                pt, preg = next_psum(c)
                def fn(eng, pt=pt, i=i, nch=nch):
                    last = None
                    for kc in range(NT):
                        last = eng.matmul(pt.ap()[:, :], wbig[0].sl(kc * 128, 128),
                                          E.v3(NT, T)[:, kc, nch * 512:(nch + 1) * 512],
                                          start=(kc == 0), stop=(kc == NT - 1))
                    return last
                c.mm_group(fn, reads=[wbig[0].reg(), E.reg()], writes=[preg], tag="vmm")
                c.copy(vz.sl(nch * 512, 512), pt.ap()[:, :], reads=[preg],
                       writes=[vz.reg()], tag="vev", engine="scalar")
            c.tt(vz.sl(0, T), ek.sl(0, T), vz.sl(0, T), OP.mult,
                 reads=[ek.reg(), vz.reg()], writes=[vz.reg()], tag="z")
            lam_b = vecs._mk(vcol["lam"] + i, [[vecs.ncols, 128], [0, T]])
            c.scan(bsc.sl(0, T), lam_b, ek.sl(0, T), 0.0, OP.mult, OP.add,
                   reads=[vecs.reg(), ek.reg()], writes=[bsc.reg()], tag="bscan")
            c.scan(asc.sl(0, T), lam_b, vz.sl(0, T), 0.0, OP.mult, OP.add,
                   reads=[vecs.reg(), vz.reg()], writes=[asc.reg()], tag="ascan")
            c.stt(ek.sl(1, T - 1), ek.sl(1, T - 1), vc("eu", i), bsc.sl(0, T - 1),
                  OP.mult, OP.add, reads=[ek.reg(), bsc.reg(), vecs.reg()],
                  writes=[ek.reg()], tag="den")
            c.ts(ek.sl(0, 1), ek.sl(0, 1), vc("eu", i), None, OP.mult, OP.bypass,
                 reads=[ek.reg(), vecs.reg()], writes=[ek.reg()], tag="den0")
            c.recip_pos(ek.sl(0, T), ek.sl(0, T), reads=[ek.reg()], writes=[ek.reg()],
                        tag="wkvrcp")
            c.stt(vz.sl(1, T - 1), vz.sl(1, T - 1), vc("eu", i), asc.sl(0, T - 1),
                  OP.mult, OP.add, reads=[vz.reg(), asc.reg(), vecs.reg()],
                  writes=[vz.reg()], tag="num")
            c.ts(vz.sl(0, 1), vz.sl(0, 1), vc("eu", i), None, OP.mult, OP.bypass,
                 reads=[vz.reg(), vecs.reg()], writes=[vz.reg()], tag="num0")
            c.tt(C.v3(NT, T)[:, i, :], vz.sl(0, T), ek.sl(0, T), OP.mult,
                 reads=[vz.reg(), ek.reg()], writes=[C.reg()], tag="y")

        c.tt(Dp.v3(NT, T), Bz_xn, Bz_xx, OP.subtract, reads=[Bz.reg()], writes=[Dp.reg()],
             tag="mix_d2")
        for kc in range(NT):
            c.stt(E.v3(NT, T)[:, kc, :], Dp.v3(NT, T)[:, kc, :], vc("tmr", kc),
                  Bz_xx[:, kc, :], OP.mult, OP.add,
                  reads=[Dp.reg(), Bz.reg(), vecs.reg()], writes=[E.reg()], tag="mix_xr")
        def sr_evac(i, nch, pt, preg):
            c.act(Dp.v3(NT, T)[:, i, nch * 512:(nch + 1) * 512], pt.ap()[:, :],
                  AF.Sigmoid, reads=[preg], writes=[Dp.reg()], tag="sr")
        proj_mm(Dp, None, "L0_attn_WrT", E, E.v3(NT, T), sr_evac, None, b, "rproj")
        c.tt(C.v3(NT, T), C.v3(NT, T), Dp.v3(NT, T), OP.mult,
             reads=[C.reg(), Dp.reg()], writes=[C.reg()], tag="sy")

        def wo_evac(i, nch, pt, preg):
            sl = A.v3(NT, T)[:, i, nch * 512:(nch + 1) * 512]
            c.tt(sl, pt.ap()[:, :], sl, OP.add, reads=[preg, A.reg()], writes=[A.reg()],
                 tag="xmid")
        proj_mm(A, None, "L0_attn_WoT", C, C.v3(NT, T), wo_evac, None, b, "oproj")

        c.memset(Bz._mk(0, [[Bz.ncols, 128], [T + 1, NT], [1, 1]]), 0.0,
                 writes=[Bz.reg()], tag="bz_zero2")
        ln_cmajor(c, A, A.v3(NT, T), Bz, Bz_xn, C, meanb, rstdb, rows, tag="ln2",
                  gcol=lambda kc: vc("ln2_g", kc), bcol=lambda kc: vc("ln2_b", kc),
                  gb_regs=[vecs.reg()])

        c.tt(C.v3(NT, T), Bz_xn, Bz_xx, OP.subtract, reads=[Bz.reg()], writes=[C.reg()],
             tag="mix_d3")
        for kc in range(NT):
            c.stt(Dp.v3(NT, T)[:, kc, :], C.v3(NT, T)[:, kc, :], vc("f_tmk", kc),
                  Bz_xx[:, kc, :], OP.mult, OP.add,
                  reads=[C.reg(), Bz.reg(), vecs.reg()], writes=[Dp.reg()], tag="mix_xk2")
            c.stt(E.v3(NT, T)[:, kc, :], C.v3(NT, T)[:, kc, :], vc("f_tmr", kc),
                  Bz_xx[:, kc, :], OP.mult, OP.add,
                  reads=[C.reg(), Bz.reg(), vecs.reg()], writes=[E.reg()], tag="mix_xr2")

        def sp_evac(i, nch, pt, preg):
            c.act(C.v3(NT, T)[:, i, nch * 512:(nch + 1) * 512], pt.ap()[:, :],
                  AF.Sigmoid, reads=[preg], writes=[C.reg()], tag="sp")
        proj_mm(C, None, "L0_attn_fWrT", E, E.v3(NT, T), sp_evac, None, b, "fr")

        fWvT3 = c.dram["bf_L0_attn_fWvT"].ap().rearrange("(n p) m -> p n m", p=128)
        E_kk = E._mk(0, [[E.ncols, 128], [512, NH], [1, 512]])
        for n in range(NCH):
            c.dma("sync", SC._mk(2 * T, [[4 * T, 128], [512, NT], [1, 512]]),
                  xin3[b][:, :, n * 512:(n + 1) * 512],
                  reads=[(XI,)], writes=[("SC",)], tag="ld_x0c")
            for fM in range(NH):
                wkf = wk[fM % 2]
                stream_wcols(wkf, "L0_attn_fWkT", fM, NT, "fwk")
                pt, preg = next_psum(c)
                def fn(eng, pt=pt, fM=fM, n=n, wkf=wkf):
                    last = None
                    for kc in range(NT):
                        last = eng.matmul(pt.ap()[:, :], wkf.sl(kc * 128, 128),
                                          Dp.v3(NT, T)[:, kc, n * 512:(n + 1) * 512],
                                          start=(kc == 0), stop=(kc == NT - 1))
                    return last
                c.mm_group(fn, reads=[wkf.reg(), Dp.reg()], writes=[preg], tag="kkmm")
                sl = E_kk[:, fM, :]
                c.ts(sl, pt.ap()[:, :], 0.0, None, OP.max, OP.bypass,
                     reads=[preg], writes=[E.reg()], tag="kkrelu")
                c.act(sl, sl, AF.Square, reads=[E.reg()], writes=[E.reg()], tag="kksq")
            for dM in range(NT):
                pt, preg = next_psum(c)
                for slab in range(4):
                    wb = wbig[slab % 2]
                    c.dma("sync",
                          wb._mk(0, [[wb.ncols, 128], [128, NT], [1, 128]]),
                          fWvT3[:, slab * NT:(slab + 1) * NT, dM * 128:(dM + 1) * 128],
                          reads=[("bf_L0_attn_fWvT",)], writes=[wb.reg()], tag="fwv_w")
                    def fn(eng, pt=pt, wb=wb, slab=slab, n=n):
                        last = None
                        for j in range(NT):
                            kc = slab * NT + j
                            last = eng.matmul(pt.ap()[:, :], wb.sl(j * 128, 128),
                                              E_kk[:, kc, :],
                                              start=(slab == 0 and j == 0),
                                              stop=(slab == 3 and j == NT - 1))
                        return last
                    c.mm_group(fn, reads=[wb.reg(), E.reg()], writes=[preg], tag="fvmm")
                t1 = x1c.sl(dM * 512, 512)
                c.tt(t1, pt.ap()[:, :], C.v3(NT, T)[:, dM, n * 512:(n + 1) * 512],
                     OP.mult, reads=[preg, C.reg()], writes=[ek.reg(), vz.reg()],
                     tag="rkv")
                c.tt(t1, t1, A.v3(NT, T)[:, dM, n * 512:(n + 1) * 512], OP.add,
                     reads=[ek.reg(), vz.reg(), A.reg()], writes=[ek.reg(), vz.reg()],
                     tag="addmid")
                c.tt(t1, t1, x0c.sl(dM * 512, 512), OP.add,
                     reads=[ek.reg(), vz.reg(), asc.reg(), bsc.reg()],
                     writes=[ek.reg(), vz.reg()], tag="addx0")
            c.dma("sync", xout3[b][:, :, n * 512:(n + 1) * 512],
                  SC._mk(0, [[4 * T, 128], [512, NT], [1, 512]]),
                  reads=[ek.reg(), vz.reg()], writes=[(XO,)], tag="st_x1")



def _decl(nc, c, name, shape, kind="ExternalInput", dtype=F32):
    h = nc.dram_tensor(name, list(shape), dtype, kind=kind)
    c.dram[name] = h
    return h


def build_prog_l0_attn(D=1024, T=2048, B=2):
    nc = bass.Bass("TRN2", target_bir_lowering=False, debug=False)
    c = Ctx(nc, D=D, T=T, B=B)
    c.dram = {}
    NT = c.NT
    HID = 4 * D
    xin = _decl(nc, c, "xin", [B, D, T])
    xout = _decl(nc, c, "xout", [B, D, T], kind="ExternalOutput")
    for nm in ("WkT", "WvT", "WrT", "WoT", "fWrT"):
        _decl(nc, c, "L0_attn_" + nm, [D, D])
        _decl(nc, c, "bf_L0_attn_" + nm, [D, D], kind="Internal", dtype=BF16)
    _decl(nc, c, "L0_attn_fWkT", [D, HID])
    _decl(nc, c, "bf_L0_attn_fWkT", [D, HID], kind="Internal", dtype=BF16)
    _decl(nc, c, "L0_attn_fWvT", [HID, D])
    _decl(nc, c, "bf_L0_attn_fWvT", [HID, D], kind="Internal", dtype=BF16)
    for nm in ("ln1_g", "ln1_b", "ln2_g", "ln2_b", "tmk", "tmv", "tmr",
               "f_tmk", "f_tmr", "time_decay", "time_first"):
        _decl(nc, c, "L0_attn_" + nm, [128, NT])
    setup_common(c)
    build_l0_attn(c, xin, xout, None)
    c.p.fence()
    c.p.emit()
    global _LAST_PROG
    _LAST_PROG = c.p
    return nc


def _vecprep(v, NT):
    v = np.asarray(v, dtype=np.float32).reshape(-1)
    return np.ascontiguousarray(v.reshape(NT, 128).T)


def host_l0_attn(I, D=1024):
    NT = D // 128
    m = {}
    for nm in ("Wk", "Wv", "Wr", "Wo"):
        m["L0_attn_" + nm + "T"] = np.ascontiguousarray(I["L0_attn_" + nm].T)
    m["L0_attn_fWrT"] = np.ascontiguousarray(I["L0_attn_fWr"].T)
    m["L0_attn_fWkT"] = np.ascontiguousarray(I["L0_attn_fWk"].T)
    m["L0_attn_fWvT"] = np.ascontiguousarray(I["L0_attn_fWv"].T)
    for nm in ("ln1_g", "ln1_b", "ln2_g", "ln2_b", "tmk", "tmv", "tmr",
               "f_tmk", "f_tmr", "time_decay", "time_first"):
        m["L0_attn_" + nm] = _vecprep(I["L0_attn_" + nm], NT)
    return m



def build_l1_ffn(c: Ctx, xin, xout, pfx="L1_ffn_"):
    XI = xin.ap().tensor.name; XO = xout.ap().tensor.name
    need_ident32(c)
    need_e8(c)
    NT, T, D, B = c.NT, c.T, c.D, c.B
    TOK = B * T
    NH = 4 * NT
    F = 4 * D
    NE = 8
    NCHU = TOK // 512

    A4 = mkplane(c, "m_A4", NT * TOK, BF16)
    SQP = min(4, TOK // 512)
    SQ4 = mkplane(c, "m_SQ4", NT * TOK // SQP, BF16)
    mb4 = mkplane(c, "m_mb4", TOK, BF16)
    rb4 = mkplane(c, "m_rb4", TOK, BF16)

    hpl = mkplane(c, "m_h", NH * 512, BF16)
    yac = mkplane(c, "m_yac", NT * 512, F32)
    x3c = mkplane(c, "m_x3c", NT * 512, F32)
    rows4 = [Row(yac, 0, 0), Row(yac, 32, 0), Row(x3c, 0, 0), Row(x3c, 32, 0)]
    w1b = [mkplane(c, f"m_w1b{j}", NT * 128, BF16) for j in range(4)]
    w2b = [mkplane(c, f"m_w2b{j}", 2 * NT * 128, BF16) for j in range(4)]
    wbc = mkplane(c, "m_wbc", 512, F32)
    redp = mkplane(c, "m_red", 512, F32, npart=16)
    wCp = mkplane(c, "m_wC", 512, F32, npart=8)
    wredp = mkplane(c, "m_wred", NT * 16, F32)
    wgp = mkplane(c, "m_wg", 16, F32, npart=8)
    wgnT = mkplane(c, "m_wgnT", 8, F32, npart=16)
    colsum = mkplane(c, "m_colsum", 1, F32, npart=16)
    gmisc = mkplane(c, "m_gmisc", 64, F32)
    b1c = mkplane(c, "m_b1c", NE * NH, F32)
    b2sb = mkplane(c, "m_b2sb", D, F32, npart=8)

    xin4 = xin.ap().rearrange("b (n p) t -> p n b t", p=128)
    xout4 = xout.ap().rearrange("b (n p) t -> p n b t", p=128)
    NPB = T // 512
    def xin_chunk(n):
        return xin4[:, :, n // NPB, (n % NPB) * 512:(n % NPB) * 512 + 512]
    def xout_chunk(n):
        return xout4[:, :, n // NPB, (n % NPB) * 512:(n % NPB) * 512 + 512]

    for b in range(B):
        c.dma("gpsimd", A4._mk(b * T, [[A4.ncols, 128], [TOK, NT], [1, T]]),
              xin4[:, :, b, :], reads=[(XI,)], writes=[A4.reg()], tag="m_ldx")
    c.dma("sync", wredp._mk(0, [[wredp.ncols, 128], [16, NT], [1, 16]]),
          c.dram[pfx + "w_red"].ap().rearrange("(n p) g -> p n g", p=128),
          reads=[(pfx + "w_red",)], writes=[wredp.reg()], tag="m_ldwred")
    c.dma("sync", wgp.sl(0, 16, np_=8), c.dram[pfx + "wg"].ap()[:, :],
          reads=[(pfx + "wg",)], writes=[wgp.reg()], tag="m_ldwg")
    c.dma("sync", b1c.sl(0, NE * NH), c.dram[pfx + "b1c"].ap()[:, :],
          reads=[(pfx + "b1c",)], writes=[b1c.reg()], tag="m_ldb1")
    c.dma("sync", b2sb.sl(0, D, np_=8), c.dram[pfx + "b2"].ap()[:, :],
          reads=[(pfx + "b2",)], writes=[b2sb.reg()], tag="m_ldb2")

    precast_weights(c, [pfx + "w1", pfx + "w2"], hpl, hpl.ncols // 2)
    ln_cmajor(c, A4, A4.v3(NT, TOK), A4, A4.v3(NT, TOK), SQ4, mb4, rb4, rows4,
              tag="m_ln", tlen=TOK, sq_parts=SQP)

    pt, preg = next_psum(c)
    def fn(eng, pt=pt):
        last = None
        for kc in range(NT):
            last = eng.matmul(pt.ap()[0:1, 0:16], c.ones128.sl(0, 1),
                              wredp._mk(0, [[wredp.ncols, 128], [16, NT], [1, 16]])[:, kc, :],
                              start=(kc == 0), stop=(kc == NT - 1))
        return last
    c.mm_group(fn, reads=[wredp.reg(), c.ones128.reg()], writes=[preg], tag="m_colsum_mm")
    c.copy(gmisc.sl(0, 16, np_=1), pt.ap()[0:1, 0:16], reads=[preg], writes=[gmisc.reg()],
           tag="m_colsum_ev", engine="scalar")
    cs_dram = c.dram["m_scr16"]
    c.dma("sync", cs_dram.ap()[:, :], gmisc.sl(0, 16, np_=1), reads=[gmisc.reg()],
          writes=[("m_scr16",)], tag="m_cs_out")
    def fn(eng):
        return eng.dma_start(out=colsum.sl(0, 1, np_=16),
                             in_=cs_dram.ap().rearrange("a g -> (g a)")[:, None],
                             allow_slow_non_contiguous=True)
    c.p.op("sync", fn, reads=[("m_scr16",)], writes=[colsum.reg()], tag="m_cs_in")
    c.act(gmisc.sl(32, 16, np_=8), wgp.sl(0, 16, np_=8), AF.Square,
          reads=[wgp.reg()], writes=[gmisc.reg()], tag="m_wgsq")
    c.p.op("vector", (lambda eng: eng.reduce_sum(gmisc.sl(16, 1, np_=8), gmisc.sl(32, 16, np_=8),
                                                 mybir.AxisListType.X)),
           reads=[gmisc.reg()], writes=[gmisc.reg()], tag="m_wgss2")
    c.act(gmisc.sl(17, 1, np_=8), gmisc.sl(16, 1, np_=8), AF.Sqrt,
          reads=[gmisc.reg()], writes=[gmisc.reg()], tag="m_wgn1")
    c.ts(gmisc.sl(17, 1, np_=8), gmisc.sl(17, 1, np_=8), 1e-4, None, OP.max, OP.bypass,
         reads=[gmisc.reg()], writes=[gmisc.reg()], tag="m_wgn2")
    c.recip(gmisc.sl(18, 1, np_=8), gmisc.sl(17, 1, np_=8), reads=[gmisc.reg()],
            writes=[gmisc.reg()], tag="m_wgn3")
    c.ts(wgp.sl(0, 16, np_=8), wgp.sl(0, 16, np_=8), gmisc.sl(18, 1, np_=8), None,
         OP.mult, OP.bypass, reads=[wgp.reg(), gmisc.reg()], writes=[wgp.reg()], tag="m_wgn4")
    wgn_dram = c.dram["m_scr8x16"]
    c.dma("sync", wgn_dram.ap()[:, :], wgp.sl(0, 16, np_=8), reads=[wgp.reg()],
          writes=[("m_scr8x16",)], tag="m_wgn_out")
    def fn(eng):
        return eng.dma_start(out=wgnT.sl(0, 8, np_=16),
                             in_=wgn_dram.ap().rearrange("e g -> g e"),
                             allow_slow_non_contiguous=True)
    c.p.op("sync", fn, reads=[("m_scr8x16",)], writes=[wgnT.reg()], tag="m_wgn_in")

    for n in range(NCHU):
        c.dma("sync", x3c.v3(NT, 512), xin_chunk(n),
              reads=[(XI,)], writes=[x3c.reg()], tag="m_ldx3c")
        pt, preg = next_psum(c)
        def fn(eng, pt=pt, n=n):
            last = None
            for kc in range(NT):
                last = eng.matmul(pt.ap()[0:16, :],
                                  wredp._mk(0, [[wredp.ncols, 128], [16, NT], [1, 16]])[:, kc, :],
                                  x3c.v3(NT, 512)[:, kc, :],
                                  start=(kc == 0), stop=(kc == NT - 1))
            return last
        c.mm_group(fn, reads=[wredp.reg(), x3c.reg()], writes=[preg], tag="m_redmm")
        m16 = mb4._mk(n * 512, [[mb4.ncols, 16], [1, 512]])
        r16 = rb4._mk(n * 512, [[rb4.ncols, 16], [1, 512]])
        c.ts(redp.sl(0, 512, np_=16), m16, colsum.sl(0, 1, np_=16), None, OP.mult,
             OP.bypass, reads=[mb4.reg(), colsum.reg()], writes=[redp.reg()], tag="m_red1")
        c.tt(redp.sl(0, 512, np_=16), pt.ap()[0:16, :], redp.sl(0, 512, np_=16),
             OP.subtract, reads=[preg, redp.reg()], writes=[redp.reg()], tag="m_red2")
        c.tt(redp.sl(0, 512, np_=16), redp.sl(0, 512, np_=16), r16, OP.mult,
             reads=[redp.reg(), rb4.reg()], writes=[redp.reg()], tag="m_red3")
        for j in range(4):
            pt2, preg2 = next_psum(c)
            def fn(eng, pt2=pt2, j=j):
                return eng.matmul(pt2.ap()[:, 0:8], redp.sl(j * 128, 128, np_=16),
                                  wgnT.sl(0, 8, np_=16), start=True, stop=True)
            c.mm_group(fn, reads=[redp.reg(), wgnT.reg()], writes=[preg2], tag="m_lgmm")
            l8 = gmisc.sl(24, 8)
            mx = gmisc.sl(33, 1)
            sm = gmisc.sl(34, 1)
            c.copy(l8, pt2.ap()[:, 0:8], reads=[preg2], writes=[gmisc.reg()],
                   tag="m_l8", engine="vector")
            c.p.op("vector", (lambda eng, l8=l8, mx=mx: eng.reduce_max(mx, l8, mybir.AxisListType.X)),
                   reads=[gmisc.reg()], writes=[gmisc.reg()], tag="m_gmax")
            c.ts(l8, l8, mx, None, OP.subtract, OP.bypass,
                 reads=[gmisc.reg()], writes=[gmisc.reg()], tag="m_gsub")
            el8 = gmisc.sl(40, 8)
            c.act(el8, l8, AF.Exp, reads=[gmisc.reg()], writes=[gmisc.reg()], tag="m_gexp")
            c.p.op("vector", (lambda eng, el8=el8, sm=sm: eng.reduce_sum(sm, el8, mybir.AxisListType.X)),
                   reads=[gmisc.reg()], writes=[gmisc.reg()], tag="m_gsum")
            c.recip(sm, sm, reads=[gmisc.reg()], writes=[gmisc.reg()], tag="m_gscore")
            c.ts(l8, l8, 0.0, None, OP.is_equal, OP.bypass,
                 reads=[gmisc.reg()], writes=[gmisc.reg()], tag="m_gmask")
            c.ts(l8, l8, sm, None, OP.mult, OP.bypass,
                 reads=[gmisc.reg()], writes=[gmisc.reg()], tag="m_gsel")
            pt3, preg3 = next_psum(c)
            def fn(eng, pt3=pt3, l8=l8):
                return eng.transpose(pt3.ap()[0:8, 0:128], l8, c.ident32.sl(0, 128))
            c.p.op("tensor", fn, reads=[gmisc.reg(), ("ident32",)], writes=[preg3],
                   tag="m_gtr")
            c.copy(wCp._mk(j * 128, [[wCp.ncols, 8], [1, 128]]), pt3.ap()[0:8, 0:128],
                   reads=[preg3], writes=[wCp.reg()], tag="m_gtev", engine="vector")

        for e in range(NE):
            pt, preg = next_psum(c)
            def fn(eng, pt=pt, e=e):
                return eng.matmul(pt.ap()[:, :], c.E8._mk(e * 128, [[8 * 128, 8], [1, 128]]),
                                  wCp._mk(0, [[wCp.ncols, 8], [1, 512]]),
                                  start=True, stop=True)
            c.mm_group(fn, reads=[wCp.reg(), c.E8.reg()], writes=[preg], tag="m_bc")
            c.copy(wbc.sl(0, 512), pt.ap()[:, :], reads=[preg], writes=[wbc.reg()],
                   tag="m_bcev", engine="scalar")
            w1d = c.dram["bf_" + pfx + "w1"].ap()[e].rearrange("(k p) m -> p k m", p=128)
            for fM in range(NH):
                wb = w1b[fM % 4]
                c.dma("sync", wb._mk(0, [[wb.ncols, 128], [128, NT], [1, 128]]),
                      w1d[:, :, fM * 128:(fM + 1) * 128],
                      reads=[("bf_" + pfx + "w1",)], writes=[wb.reg()], tag="m_w1w")
                pt, preg = next_psum(c)
                def fn(eng, pt=pt, wb=wb, n=n):
                    last = None
                    for kc in range(NT):
                        last = eng.matmul(pt.ap()[:, :], wb.sl(kc * 128, 128),
                                          A4.v3(NT, TOK)[:, kc, n * 512:(n + 1) * 512],
                                          start=(kc == 0), stop=(kc == NT - 1))
                    return last
                c.mm_group(fn, reads=[wb.reg(), A4.reg()], writes=[preg], tag="m_w1mm")
                c.act(hpl.v3(NH, 512)[:, fM, :], pt.ap()[:, :], AF.Gelu,
                      bias=b1c.sl(e * NH + fM, 1), reads=[preg, b1c.reg()],
                      writes=[hpl.reg()], tag="m_gelu")
                c.tt(hpl.v3(NH, 512)[:, fM, :], hpl.v3(NH, 512)[:, fM, :], wbc.sl(0, 512),
                     OP.mult, reads=[hpl.reg(), wbc.reg()], writes=[hpl.reg()], tag="m_hsc")
            w2d = c.dram["bf_" + pfx + "w2"].ap()[e].rearrange("(k p) m -> p k m", p=128)
            for dM in range(NT):
                pt, preg = next_psum(c)
                for slab in range(2):
                    wb = w2b[(dM * 2 + slab) % 4]
                    c.dma("sync", wb._mk(0, [[wb.ncols, 128], [128, 2 * NT], [1, 128]]),
                          w2d[:, slab * 2 * NT:(slab + 1) * 2 * NT, dM * 128:(dM + 1) * 128],
                          reads=[("bf_" + pfx + "w2",)], writes=[wb.reg()], tag="m_w2w")
                    def fn(eng, pt=pt, wb=wb, slab=slab):
                        last = None
                        for j2 in range(2 * NT):
                            kc = slab * 2 * NT + j2
                            last = eng.matmul(pt.ap()[:, :], wb.sl(j2 * 128, 128),
                                              hpl.v3(NH, 512)[:, kc, :],
                                              start=(slab == 0 and j2 == 0),
                                              stop=(slab == 1 and j2 == 2 * NT - 1))
                        return last
                    c.mm_group(fn, reads=[wb.reg(), hpl.reg()], writes=[preg], tag="m_w2mm")
                if e == 0:
                    c.copy(yac.sl(dM * 512, 512), pt.ap()[:, :], reads=[preg],
                           writes=[yac.reg()], tag="m_yc0", engine="vector")
                else:
                    c.tt(yac.sl(dM * 512, 512), pt.ap()[:, :], yac.sl(dM * 512, 512),
                         OP.add, reads=[preg, yac.reg()], writes=[yac.reg()], tag="m_yacc")
        for dM in range(NT):
            pt, preg = next_psum(c)
            def fn(eng, pt=pt, dM=dM):
                return eng.matmul(pt.ap()[:, :], b2sb._mk(dM * 128, [[b2sb.ncols, 8], [1, 128]]),
                                  wCp._mk(0, [[wCp.ncols, 8], [1, 512]]), start=True, stop=True)
            c.mm_group(fn, reads=[b2sb.reg(), wCp.reg()], writes=[preg], tag="m_b2mm")
            c.tt(yac.sl(dM * 512, 512), pt.ap()[:, :], yac.sl(dM * 512, 512), OP.add,
                 reads=[preg, yac.reg()], writes=[yac.reg()], tag="m_b2acc")
        c.tt(x3c.v3(NT, 512), yac.v3(NT, 512), x3c.v3(NT, 512), OP.add,
             reads=[yac.reg(), x3c.reg()], writes=[x3c.reg()], tag="m_x4")
        c.dma("sync", xout_chunk(n), x3c.v3(NT, 512),
              reads=[x3c.reg()], writes=[(XO,)], tag="m_stx4")


def build_prog_l1_ffn(D=1024, T=2048, B=2):
    nc = bass.Bass("TRN2", target_bir_lowering=False, debug=False)
    c = Ctx(nc, D=D, T=T, B=B)
    c.dram = {}
    F = 4 * D
    xin = _decl(nc, c, "xin", [B, D, T])
    xout = _decl(nc, c, "xout", [B, D, T], kind="ExternalOutput")
    _decl(nc, c, "L1_ffn_w_red", [D, 16])
    _decl(nc, c, "L1_ffn_wg", [8, 16])
    _decl(nc, c, "L1_ffn_w1", [8, D, F])
    _decl(nc, c, "bf_L1_ffn_w1", [8, D, F], kind="Internal", dtype=BF16)
    _decl(nc, c, "L1_ffn_b1c", [128, 8 * (F // 128)])
    _decl(nc, c, "L1_ffn_w2", [8, F, D])
    _decl(nc, c, "bf_L1_ffn_w2", [8, F, D], kind="Internal", dtype=BF16)
    _decl(nc, c, "L1_ffn_b2", [8, D])
    _decl(nc, c, "m_scr16", [1, 16], kind="Internal")
    _decl(nc, c, "m_scr8x16", [8, 16], kind="Internal")
    setup_common(c, npsum=6)
    build_l1_ffn(c, xin, xout)
    c.p.fence()
    c.p.emit()
    global _LAST_PROG
    _LAST_PROG = c.p
    return nc


def host_l1_ffn(I, D=1024):
    F = 4 * D
    m = {
        "L1_ffn_w_red": I["L1_ffn_w_red"],
        "L1_ffn_wg": I["L1_ffn_wg"],
        "L1_ffn_w1": I["L1_ffn_w1"],
        "L1_ffn_w2": I["L1_ffn_w2"],
        "L1_ffn_b2": I["L1_ffn_b2"],
    }
    b1 = np.asarray(I["L1_ffn_b1"])
    cols = np.concatenate([b1[e].reshape(F // 128, 128).T for e in range(8)], axis=1)
    m["L1_ffn_b1c"] = np.ascontiguousarray(cols)
    return m



def build_l1_attn(c: Ctx, xin, xout, pfx="L1_attn_"):
    XI = xin.ap().tensor.name; XO = xout.ap().tensor.name
    need_zero_col(c)
    NT, T, D, B = c.NT, c.T, c.D, c.B
    R = 256
    NR = R // 128

    A5 = mkplane(c, "a_A5", NT * T, BF16)
    OP5 = mkplane(c, "a_OP5", NT * T, BF16)
    RK = mkplane(c, "a_RK", 6 * T, F32)
    AB = mkplane(c, "a_ABr", NR * T, BF16)
    AT = mkplane(c, "a_ATr", NR * T, BF16)
    rc = mkplane(c, "a_rc", T, F32)
    rows5_t = mkplane(c, "a_rows5", T, F32)
    xc5 = mkplane(c, "a_xc5", NT * 512, F32)
    wst = [mkplane(c, "a_wst0", NT * 128, BF16), mkplane(c, "a_wst1", NT * 128, BF16)]
    WQV = {}
    for nm in ("qvb", "qvt", "kwb", "kwt", "xt"):
        WQV[nm] = mkplane(c, "a_W" + nm, NT * R, BF16)
    UBT = mkplane(c, "a_UBT", NR * D, BF16)
    UTT = mkplane(c, "a_UTT", NR * D, BF16)
    av = mkplane(c, "a_av", 32, F32)

    rows5 = [Row(rows5_t, 0), Row(rows5_t, 32), Row(RK, 0, 2 * T), Row(RK, 32, 2 * T)]

    class SCV:
        def __init__(self, c0):
            self.c0 = c0
        def sl(self, off, n, np_=128):
            return RK._mk(self.c0 + off, [[RK.ncols, np_], [1, n]])
        def bc3(self, ntile, tlen, off=0):
            return RK._mk(self.c0 + off, [[RK.ncols, 128], [0, ntile], [1, tlen]])
        def reg(self, *k):
            return ("a_RK",) + k
    meanb5, rstdb5 = SCV(0), SCV(T)
    cB = [SCV(0), SCV(T)]
    cT = [SCV(2 * T), SCV(3 * T)]
    zX = [SCV(4 * T), SCV(5 * T)]

    xin4 = xin.ap().rearrange("b (n p) t -> p n b t", p=128)
    xout4 = xout.ap().rearrange("b (n p) t -> p n b t", p=128)

    c.dma("sync", av.sl(0, 1, np_=1), c.dram[pfx + "alpha"].ap()[:, :],
          reads=[(pfx + "alpha",)], writes=[av.reg()], tag="a_lda")
    pt, preg = next_psum(c)
    def fn(eng, pt=pt):
        return eng.matmul(pt.ap()[:, 0:1], c.ones128._mk(0, [[128, 1], [1, 128]]),
                          av.sl(0, 1, np_=1), start=True, stop=True)
    c.mm_group(fn, reads=[av.reg(), c.ones128.reg()], writes=[preg], tag="a_bca")
    c.copy(av.sl(1, 1), pt.ap()[:, 0:1], reads=[preg], writes=[av.reg()],
           tag="a_bcaev", engine="vector")
    c.dma("sync", av.sl(2, NT), c.dram[pfx + "bias_b"].ap()[:, :],
          reads=[(pfx + "bias_b",)], writes=[av.reg()], tag="a_ldbb")
    c.dma("sync", av.sl(2 + NT, NT), c.dram[pfx + "bias_t"].ap()[:, :],
          reads=[(pfx + "bias_t",)], writes=[av.reg()], tag="a_ldbt")
    c.stt(av.sl(2 + NT, NT), av.sl(2 + NT, NT), av.sl(1, 1), av.sl(2, NT),
          OP.mult, OP.add, reads=[av.reg()], writes=[av.reg()], tag="a_combo")
    def fn(eng):
        return eng.iota(rc.sl(0, T), [[1, T]], base=1, channel_multiplier=0,
                        allow_small_or_imprecise_dtypes=True)
    c.p.op("gpsimd", fn, writes=[rc.reg()], tag="a_iota")
    c.recip(rc.sl(0, T), rc.sl(0, T), reads=[rc.reg()], writes=[rc.reg()], tag="a_rc")
    for wname, vname, dst in (("Wq", "V_b", "qvb"), ("Wq", "V_t", "qvt"),
                              ("Wk", "W_b", "kwb"), ("Wk", "W_t", "kwt")):
        wd = c.dram[pfx + wname].ap().rearrange("(k p) m -> p k m", p=128)
        vd = c.dram[pfx + vname].ap().rearrange("(k p) r -> p k r", p=128)
        if not hasattr(c, "_a_vv"):
            c._a_vv = mkplane(c, "a_vv", NT * R, BF16)
        vv = c._a_vv
        c.dma("gpsimd", vv._mk(0, [[vv.ncols, 128], [R, NT], [1, R]]), vd,
              reads=[(pfx + vname,)], writes=[vv.reg()], tag="a_ldv")
        for iM in range(NT):
            wb = wst[iM % 2]
            c.dma("gpsimd", wb._mk(0, [[wb.ncols, 128], [128, NT], [1, 128]]),
                  wd[:, :, iM * 128:(iM + 1) * 128],
                  reads=[(pfx + wname,)], writes=[wb.reg()], tag="a_ldw")
            for nr in range(1):
                pt, preg = next_psum(c)
                def fn(eng, pt=pt, wb=wb, vv=vv):
                    last = None
                    for kc in range(NT):
                        last = eng.matmul(pt.ap()[:, 0:R], wb.sl(kc * 128, 128),
                                          vv._mk(0, [[vv.ncols, 128], [R, NT], [1, R]])[:, kc, :],
                                          start=(kc == 0), stop=(kc == NT - 1))
                    return last
                c.mm_group(fn, reads=[wb.reg(), vv.reg()], writes=[preg], tag="a_cwmm")
                c.copy(WQV[dst]._mk(iM * R, [[WQV[dst].ncols, 128], [1, R]]),
                       pt.ap()[:, 0:R], reads=[preg], writes=[WQV[dst].reg()],
                       tag="a_cwev", engine="scalar")
    c.dma("gpsimd", WQV["xt"]._mk(0, [[WQV["xt"].ncols, 128], [R, NT], [1, R]]),
          c.dram[pfx + "X_t"].ap().rearrange("(k p) r -> p k r", p=128),
          reads=[(pfx + "X_t",)], writes=[WQV["xt"].reg()], tag="a_ldxt")
    for nm, dst in (("U_bT", UBT), ("U_tT", UTT)):
        c.dma("gpsimd", dst._mk(0, [[dst.ncols, 128], [D, NR], [1, D]]),
              c.dram[pfx + nm].ap().rearrange("(k p) m -> p k m", p=128),
              reads=[(pfx + nm,)], writes=[dst.reg()], tag="a_ldu")

    NCH = T // 512
    for b in range(B):
        c.dma("gpsimd", A5.v3(NT, T), xin4[:, :, b, :], reads=[(XI,)],
              writes=[A5.reg()], tag="a_ldx")
        ln_cmajor(c, A5, A5.v3(NT, T), A5, A5.v3(NT, T), OP5, meanb5, rstdb5, rows5,
                  tag="a_ln")
        for dst_pl, wkey, outname in ((AB, "qvb", "ab"), (AT, "qvt", "at")):
            for rt in range(NR):
                for nch in range(NCH):
                    pt, preg = next_psum(c)
                    def fn(eng, pt=pt, wkey=wkey, rt=rt, nch=nch):
                        last = None
                        for kc in range(NT):
                            lhsT = WQV[wkey]._mk(kc * R + rt * 128,
                                                 [[WQV[wkey].ncols, 128], [1, 128]])
                            last = eng.matmul(pt.ap()[:, :], lhsT,
                                              A5.v3(NT, T)[:, kc, nch * 512:(nch + 1) * 512],
                                              start=(kc == 0), stop=(kc == NT - 1))
                        return last
                    c.mm_group(fn, reads=[WQV[wkey].reg(), A5.reg()], writes=[preg],
                               tag="a_prmm")
                    c.copy(dst_pl._mk(rt * T + nch * 512, [[dst_pl.ncols, 128], [1, 512]]),
                           pt.ap()[:, :], reads=[preg], writes=[dst_pl.reg()],
                           tag="a_prev", engine="scalar")
        for wkey, dsts in (("kwb", cB), ("kwt", cT), ("xt", zX)):
            for rt in range(NR):
                for nch in range(NCH):
                    pt, preg = next_psum(c)
                    def fn(eng, pt=pt, wkey=wkey, rt=rt, nch=nch):
                        last = None
                        for kc in range(NT):
                            lhsT = WQV[wkey]._mk(kc * R + rt * 128,
                                                 [[WQV[wkey].ncols, 128], [1, 128]])
                            last = eng.matmul(pt.ap()[:, :], lhsT,
                                              A5.v3(NT, T)[:, kc, nch * 512:(nch + 1) * 512],
                                              start=(kc == 0), stop=(kc == NT - 1))
                        return last
                    c.mm_group(fn, reads=[WQV[wkey].reg(), A5.reg()], writes=[preg],
                               tag="a_csmm")
                    c.copy(dsts[rt].sl(nch * 512, 512), pt.ap()[:, :],
                           reads=[preg], writes=[dsts[rt].reg()], tag="a_csev",
                           engine="scalar")
        zc = c.zero_col
        for dsts in (cB, cT, zX):
            for rt in range(NR):
                zb = zc._mk(0, [[1, 128], [0, T]])
                c.scan(dsts[rt].sl(0, T), dsts[rt].sl(0, T), zb, 0.0, OP.add, OP.add,
                       reads=[dsts[rt].reg(), ("zero_col",)], writes=[dsts[rt].reg()],
                       tag="a_scan")
        for rt in range(NR):
            c.tt(zX[rt].sl(0, T), zX[rt].sl(0, T), rc.sl(0, T), OP.mult,
                 reads=[zX[rt].reg(), rc.reg()], writes=[zX[rt].reg()], tag="a_zt")
        for rt in range(NR):
            ab_v = AB._mk(rt * T, [[AB.ncols, 128], [1, T]])
            at_v = AT._mk(rt * T, [[AT.ncols, 128], [1, T]])
            c.tt(ab_v, ab_v, cB[rt].sl(0, T), OP.mult,
                 reads=[AB.reg(), cB[rt].reg()], writes=[AB.reg()], tag="a_pb")
            c.tt(at_v, at_v, zX[rt].sl(0, T), OP.mult,
                 reads=[AT.reg(), zX[rt].reg()], writes=[AT.reg()], tag="a_pt1")
            c.tt(at_v, at_v, cT[rt].sl(0, T), OP.mult,
                 reads=[AT.reg(), cT[rt].reg()], writes=[AT.reg()], tag="a_pt2")
            c.ts(at_v, at_v, av.sl(1, 1), None, OP.mult, OP.bypass,
                 reads=[AT.reg(), av.reg()], writes=[AT.reg()], tag="a_pt3")
        for iM in range(NT):
            for nch in range(NCH):
                pt, preg = next_psum(c)
                def fn(eng, pt=pt, iM=iM, nch=nch):
                    last = None
                    for rt in range(NR):
                        lhsT = UBT._mk(rt * D + iM * 128, [[UBT.ncols, 128], [1, 128]])
                        last = eng.matmul(pt.ap()[:, :], lhsT,
                                          AB._mk(rt * T + nch * 512, [[AB.ncols, 128], [1, 512]]),
                                          start=(rt == 0), stop=False)
                    for rt in range(NR):
                        lhsT = UTT._mk(rt * D + iM * 128, [[UTT.ncols, 128], [1, 128]])
                        last = eng.matmul(pt.ap()[:, :], lhsT,
                                          AT._mk(rt * T + nch * 512, [[AT.ncols, 128], [1, 512]]),
                                          start=False, stop=(rt == NR - 1))
                    return last
                c.mm_group(fn, reads=[UBT.reg(), UTT.reg(), AB.reg(), AT.reg()],
                           writes=[preg], tag="a_gmm")
                sl = OP5.v3(NT, T)[:, iM, nch * 512:(nch + 1) * 512]
                c.tt(sl, pt.ap()[:, :], rc.sl(nch * 512, 512), OP.mult,
                     reads=[preg, rc.reg()], writes=[OP5.reg()], tag="a_gev1")
                c.ts(sl, sl, av.sl(2 + NT + iM, 1), None, OP.add, OP.bypass,
                     reads=[OP5.reg(), av.reg()], writes=[OP5.reg()], tag="a_gev2")
        WoT3 = c.dram[pfx + "WoT"].ap().rearrange("(k p) m -> p k m", p=128)
        for iM in range(NT):
            wb = wst[iM % 2]
            c.dma("gpsimd", wb._mk(0, [[wb.ncols, 128], [128, NT], [1, 128]]),
                  WoT3[:, :, iM * 128:(iM + 1) * 128],
                  reads=[(pfx + "WoT",)], writes=[wb.reg()], tag="a_ldwo")
            for nch in range(NCH):
                pt, preg = next_psum(c)
                def fn(eng, pt=pt, wb=wb, nch=nch):
                    last = None
                    for kc in range(NT):
                        last = eng.matmul(pt.ap()[:, :], wb.sl(kc * 128, 128),
                                          OP5.v3(NT, T)[:, kc, nch * 512:(nch + 1) * 512],
                                          start=(kc == 0), stop=(kc == NT - 1))
                    return last
                c.mm_group(fn, reads=[wb.reg(), OP5.reg()], writes=[preg], tag="a_omm")
                c.dma("sync", xc5.v3(1, 512, off=iM * 512),
                      xin4[:, iM, b, nch * 512:(nch + 1) * 512],
                      reads=[(XI,)], writes=[xc5.reg(iM)], tag="a_ldres")
                c.tt(xc5.v3(1, 512, off=iM * 512), pt.ap()[:, :],
                     xc5.v3(1, 512, off=iM * 512), OP.add,
                     reads=[preg, xc5.reg(iM)], writes=[xc5.reg(iM)], tag="a_res")
                c.dma("sync", xout4[:, iM, b, nch * 512:(nch + 1) * 512],
                      xc5.v3(1, 512, off=iM * 512),
                      reads=[xc5.reg(iM)], writes=[(XO,)], tag="a_st")


def build_prog_l1_attn(D=1024, T=2048, B=2):
    nc = bass.Bass("TRN2", target_bir_lowering=False, debug=False)
    c = Ctx(nc, D=D, T=T, B=B)
    c.dram = {}
    R = 256
    xin = _decl(nc, c, "xin", [B, D, T])
    xout = _decl(nc, c, "xout", [B, D, T], kind="ExternalOutput")
    for nm in ("Wq", "Wk"):
        _decl(nc, c, "L1_attn_" + nm, [D, D])
    _decl(nc, c, "L1_attn_WoT", [D, D])
    for nm in ("V_b", "W_b", "V_t", "W_t", "X_t"):
        _decl(nc, c, "L1_attn_" + nm, [D, R])
    _decl(nc, c, "L1_attn_U_bT", [R, D])
    _decl(nc, c, "L1_attn_U_tT", [R, D])
    _decl(nc, c, "L1_attn_bias_b", [128, D // 128])
    _decl(nc, c, "L1_attn_bias_t", [128, D // 128])
    _decl(nc, c, "L1_attn_alpha", [1, 1])
    setup_common(c)
    build_l1_attn(c, xin, xout)
    c.p.fence()
    c.p.emit()
    global _LAST_PROG
    _LAST_PROG = c.p
    return nc


def host_l1_attn(I, D=1024):
    NT = D // 128
    m = {}
    for nm in ("Wq", "Wk", "V_b", "W_b", "V_t", "W_t", "X_t"):
        m["L1_attn_" + nm] = I["L1_attn_" + nm]
    m["L1_attn_WoT"] = np.ascontiguousarray(I["L1_attn_Wo"].T)
    m["L1_attn_U_bT"] = np.ascontiguousarray(I["L1_attn_U_b"].T)
    m["L1_attn_U_tT"] = np.ascontiguousarray(I["L1_attn_U_t"].T)
    m["L1_attn_bias_b"] = _vecprep(I["L1_attn_bias_b"], NT)
    m["L1_attn_bias_t"] = _vecprep(I["L1_attn_bias_t"], NT)
    m["L1_attn_alpha"] = np.asarray(I["L1_attn_alpha"], np.float32).reshape(1, 1)
    return m



def build_l0_ffn_a(c: Ctx, xin, e_out, s_out, xn_out):
    XI = xin.ap().tensor.name; EO = e_out.ap().tensor.name; SO = s_out.ap().tensor.name; XNO = xn_out.ap().tensor.name
    NT, T, D, B = c.NT, c.T, c.D, c.B
    NP = c.NP
    NPC = NP // 512

    A2 = mkplane(c, "f_A2", NT * T, BF16)
    SQ2 = mkplane(c, "f_SQ2", NT * T, BF16)
    mb2 = mkplane(c, "f_mb2", T, F32)
    rb2 = mkplane(c, "f_rb2", T, F32)
    rows2_t = mkplane(c, "f_rows2", T, F32)
    phip = mkplane(c, "f_phi", NT * NP, BF16)
    est = mkplane(c, "f_est", NP, BF16)
    sa = mkplane(c, "f_sa", NP, BF16)
    sb2 = mkplane(c, "f_sb", NP, F32)

    rows2 = [Row(rows2_t, 0), Row(rows2_t, 32), Row(sb2, 0, 0), Row(sb2, 32, 0)]

    xin4 = xin.ap().rearrange("b (n p) t -> p n b t", p=128)
    c.dma("gpsimd", phip._mk(0, [[phip.ncols, 128], [NP, NT], [1, NP]]),
          c.dram["L0_ffn_phi"].ap().rearrange("(n p) q -> p n q", p=128),
          reads=[("L0_ffn_phi",)], writes=[phip.reg()], tag="f_ldphi")

    MT = T // 128
    for b in range(B):
        c.dma("gpsimd", A2.v3(NT, T), xin4[:, :, b, :], reads=[(XI,)],
              writes=[A2.reg()], tag="f_ldx")
        ln_cmajor(c, A2, A2.v3(NT, T), A2, A2.v3(NT, T), SQ2, mb2, rb2, rows2,
                  tag="f_ln")
        c.dma("sync", xn_out.ap()[b].rearrange("(n p) t -> p n t", p=128),
              A2.v3(NT, T), reads=[A2.reg()], writes=[(XNO,)], tag="f_stxn")
        for mt in range(MT):
            for npc in range(NPC):
                pt, preg = next_psum(c)
                def fn(eng, pt=pt, mt=mt, npc=npc):
                    last = None
                    for kc in range(NT):
                        lhsT = A2._mk(kc * T + mt * 128, [[A2.ncols, 128], [1, 128]])
                        last = eng.matmul(pt.ap()[:, :], lhsT,
                                          phip._mk(kc * NP + npc * 512, [[phip.ncols, 128], [1, 512]]),
                                          start=(kc == 0), stop=(kc == NT - 1))
                    return last
                c.mm_group(fn, reads=[A2.reg(), phip.reg()], writes=[preg], tag="f_lgmm")
                c.act(est.sl(npc * 512, 512), pt.ap()[:, :], AF.Exp,
                      reads=[preg], writes=[est.reg()], tag="f_exp")
            c.dma("sync", e_out.ap()[b, mt * 128:(mt + 1) * 128, :], est.sl(0, NP),
                  reads=[est.reg()], writes=[(EO,)], tag="f_ste")
    for mt in range(MT):
        c.dma("sync", est.sl(0, NP), e_out.ap()[0, mt * 128:(mt + 1) * 128, :],
              reads=[(EO,)], writes=[est.reg()], tag="f_lde0")
        c.dma("sync", sa.sl(0, NP), e_out.ap()[1, mt * 128:(mt + 1) * 128, :],
              reads=[(EO,)], writes=[sa.reg()], tag="f_lde1")
        c.tt(sb2.sl(0, NP), est.sl(0, NP), sa.sl(0, NP), OP.add,
             reads=[est.reg(), sa.reg()], writes=[sb2.reg()], tag="f_sadd")
        c.dma("sync", s_out.ap()[mt * 128:(mt + 1) * 128, :], sb2.sl(0, NP),
              reads=[sb2.reg()], writes=[(SO,)], tag="f_sts")


def build_l0_ffn_b(c: Ctx, xin, e_in, xn_in, s_all, xout, n_parts=8):
    XI = xin.ap().tensor.name; XO = xout.ap().tensor.name; EI = e_in.ap().tensor.name; XNI = xn_in.ap().tensor.name; SA = s_all.ap().tensor.name
    need_ident_bf(c)
    """Phase B: S = sum(s_all); D = E/S; slots; y; out; x2 = x1 + out."""
    NT, T, D, B = c.NT, c.T, c.D, c.B
    NP = c.NP
    NN = NP // 32
    NPC = NP // 512
    MT = T // 128

    xn2 = mkplane(c, "g_xn", NT * T, BF16)
    xnT = mkplane(c, "g_xnT", MT * D, BF16)
    slT = mkplane(c, "g_slT", NT * NP, BF16)
    ysb = mkplane(c, "g_y", (NP // 128) * D, BF16)
    rS = [mkplane(c, "g_rS0", NP, F32), mkplane(c, "g_rS1", NP, F32)]
    rcz = mkplane(c, "g_rcz", 256, F32)
    zr = mkplane(c, "g_zr", 256, F32, npart=1)
    wnb = [mkplane(c, f"g_wn{j}", NT * 512, BF16) for j in range(4)]
    bbc = [mkplane(c, "g_bb0", D, BF16), mkplane(c, "g_bb1", D, BF16)]
    ecb = [mkplane(c, "g_ec0", 256, BF16), mkplane(c, "g_ec1", 256, BF16)]
    etb = [mkplane(c, f"g_et{j}", 512, BF16) for j in range(4)]
    ebig = [mkplane(c, "g_eb0", NP, BF16), mkplane(c, "g_eb1", NP, BF16)]
    xch = mkplane(c, "g_xch", NT * 256, F32)
    otmp = mkplane(c, "g_otmp", 256, F32)
    pout = c.ps("g_pout", [128, 2048])
    ptb = c.ps("g_ptb", [128, 128], BF16)

    xin4 = xin.ap().rearrange("b (n p) t -> p n b t", p=128)
    xout4 = xout.ap().rearrange("b (n p) t -> p n b t", p=128)
    sred = c.dram["g_sred"]
    d_dram = c.dram["g_D"]

    for mt in range(MT):
        c.dma("sync", rS[0].sl(0, NP), s_all.ap()[0, mt * 128:(mt + 1) * 128, :],
              reads=[(SA,)], writes=[rS[0].reg()], tag="g_lds")
        for i in range(1, n_parts):
            c.dma("sync", rS[1].sl(0, NP), s_all.ap()[i, mt * 128:(mt + 1) * 128, :],
                  reads=[(SA,)], writes=[rS[1].reg()], tag="g_lds2")
            c.tt(rS[0].sl(0, NP), rS[0].sl(0, NP), rS[1].sl(0, NP), OP.add,
                 reads=[rS[0].reg(), rS[1].reg()], writes=[rS[0].reg()], tag="g_sacc")
        c.recip_pos(rS[0].sl(0, NP), rS[0].sl(0, NP), reads=[rS[0].reg()],
                    writes=[rS[0].reg()], tag="g_srec")
        for b in range(B):
            eb = ebig[b % 2]
            c.dma("sync", eb.sl(0, NP), e_in.ap()[b, mt * 128:(mt + 1) * 128, :],
                  reads=[(EI,)], writes=[eb.reg()], tag="g_dld")
            c.tt(eb.sl(0, NP), eb.sl(0, NP), rS[0].sl(0, NP), OP.mult,
                 reads=[eb.reg(), rS[0].reg()], writes=[eb.reg()], tag="g_dmul")
            c.dma("sync", d_dram.ap()[b, mt * 128:(mt + 1) * 128, :], eb.sl(0, NP),
                  reads=[eb.reg()], writes=[("g_D",)], tag="g_dst")

    precast_weights(c, ["L0_ffn_W"], slT, slT.ncols // 2)
    for b in range(B):
        c.dma("sync", xn2.v3(NT, T), xn_in.ap()[b].rearrange("(n p) t -> p n t", p=128),
              reads=[(XNI,)], writes=[xn2.reg()], tag="g_ldxn")
        for mt in range(MT):
            for kc in range(NT):
                ptv = ptb.ap()[:, :]
                src = xn2._mk(kc * T + mt * 128, [[xn2.ncols, 128], [1, 128]])
                def fn(eng, ptv=ptv, src=src):
                    return eng.transpose(ptv, src, c.ident_bf.sl(0, 128))
                c.p.op("tensor", fn, reads=[xn2.reg(), c.ident_bf.reg()],
                       writes=[("g_ptb",)], tag="g_xtr")
                c.copy(xnT._mk(mt * D + kc * 128, [[xnT.ncols, 128], [1, 128]]), ptv,
                       reads=[("g_ptb",)], writes=[xnT.reg()], tag="g_xtrev",
                       engine="vector")
        e3 = d_dram.ap()[b].rearrange("(q p) r -> p q r", p=128)
        for grp in range(2):
            for npc in range(NPC):
                for mt in range(MT):
                    eb = etb[mt % 4]
                    c.dma("sync", eb.sl(0, 512), e3[:, mt, npc * 512:(npc + 1) * 512],
                          reads=[("g_D",)], writes=[eb.reg()], tag="g_ldec2")
                    def fn(eng, eb=eb, grp=grp, mt=mt):
                        last = None
                        for dj in range(NT // 2):
                            dM = grp * (NT // 2) + dj
                            lhsT = xnT._mk(mt * D + dM * 128, [[xnT.ncols, 128], [1, 128]])
                            last = eng.matmul(pout.ap()[:, dj * 512:(dj + 1) * 512],
                                              lhsT, eb.sl(0, 512),
                                              start=(mt == 0), stop=(mt == MT - 1))
                        return last
                    c.mm_group(fn, reads=[xnT.reg(), eb.reg()], writes=[("g_pout",)],
                               tag="g_slmm")
                for dj in range(NT // 2):
                    dM = grp * (NT // 2) + dj
                    c.copy(slT._mk(dM * NP + npc * 512, [[slT.ncols, 128], [1, 512]]),
                           pout.ap()[:, dj * 512:(dj + 1) * 512],
                           reads=[("g_pout",)], writes=[slT.reg()], tag="g_slev",
                           engine="scalar")
        W3 = c.dram["bf_L0_ffn_W"].ap().rearrange("n (k p) m -> p n k m", p=128)
        for n in range(NN):
            bb = bbc[n % 2]
            j, base = n // 4, (n % 4) * 32
            c.dma("gpsimd", bb._mk(base * bb.ncols, [[bb.ncols, 32], [1, D]]),
                  c.dram["L0_ffn_bbc"].ap()[n],
                  reads=[("L0_ffn_bbc",)], writes=[bb.reg()], tag="g_ldb")
            DW = min(512, D)
            for dc in range(D // DW):
                wb = wnb[(n * (D // DW) + dc) % 4]
                c.dma("sync", wb._mk(0, [[wb.ncols, 128], [DW, NT], [1, DW]]),
                      W3[:, n, :, dc * DW:(dc + 1) * DW],
                      reads=[("bf_L0_ffn_W",)], writes=[wb.reg()], tag="g_ldW")
                pt2, preg2 = next_psum(c)
                def fn(eng, pt2=pt2, n=n, wb=wb, dc=dc, DW=DW):
                    last = None
                    for kc in range(NT):
                        lhsT = slT._mk(kc * NP + n * 32, [[slT.ncols, 128], [1, 32]])
                        last = eng.matmul(pt2.ap()[0:32, 0:DW], lhsT,
                                          wb._mk(kc * DW, [[wb.ncols, 128], [1, DW]]),
                                          start=(kc == 0), stop=(kc == NT - 1))
                    return last
                c.mm_group(fn, reads=[slT.reg(), wb.reg()], writes=[preg2], tag="g_ymm")
                dst = ysb._mk(base * ysb.ncols + j * D + dc * DW, [[ysb.ncols, 32], [1, DW]])
                c.tt(dst, pt2.ap()[0:32, 0:DW],
                     bb._mk(base * bb.ncols + dc * DW, [[bb.ncols, 32], [1, DW]]),
                     OP.add, reads=[preg2, bb.reg()], writes=[ysb.reg()], tag="g_yev")
        prow = c.psum_pool[0]
        prreg = (prow.ap().tensor.name,)
        for mc in range(T // 256):
            for npc in range(NP // 128):
                eb = ecb[npc % 2]
                def fn(eng, eb=eb, b=b, mc=mc, npc=npc):
                    return eng.dma_start(out=eb.sl(0, 256),
                                         in_=e_in.ap()[b, mc * 256:(mc + 1) * 256,
                                                       npc * 128:(npc + 1) * 128],
                                         transpose=True)
                c.p.op("sync", fn, reads=[(EI,)], writes=[eb.reg()], tag="g_ldec",
                       kind="dma")
                def fn(eng, eb=eb, npc=npc):
                    last = None
                    for dM in range(NT):
                        st = (npc == 0) and (dM % 2 == 0)
                        sp = (npc == NP // 128 - 1) and (dM % 2 == 1)
                        last = eng.matmul(pout.ap()[:, dM * 256:(dM + 1) * 256], lhsT_f(npc, dM),
                                          eb.sl(0, 256), start=st, stop=sp)
                    return last
                def lhsT_f(npc, dM):
                    return ysb._mk(npc * D + dM * 128, [[ysb.ncols, 128], [1, 128]])
                c.mm_group(fn, reads=[ysb.reg(), eb.reg()], writes=[("g_pout",)],
                           tag="g_omm")
                def fn(eng, eb=eb, npc=npc, prow=prow):
                    return eng.matmul(prow.ap()[0:1, 0:256], c.ones_col.sl(0, 1),
                                      eb.sl(0, 256), start=(npc == 0),
                                      stop=(npc == NP // 128 - 1))
                c.mm_group(fn, reads=[c.ones_col.reg(), eb.reg()], writes=[prreg],
                           tag="g_zmm")
            c.copy(zr.sl(0, 256, np_=1), prow.ap()[0:1, 0:256], reads=[prreg],
                   writes=[zr.reg()], tag="g_zev", engine="scalar")
            c.recip(zr.sl(0, 256, np_=1), zr.sl(0, 256, np_=1), reads=[zr.reg()],
                    writes=[zr.reg()], tag="g_zrec")
            pt3 = c.psum_pool[1]
            preg3 = (pt3.ap().tensor.name,)
            def fn(eng, pt3=pt3):
                return eng.matmul(pt3.ap()[:, 0:256], c.ones128._mk(0, [[128, 1], [1, 128]]),
                                  zr.sl(0, 256, np_=1), start=True, stop=True)
            c.mm_group(fn, reads=[zr.reg(), c.ones128.reg()], writes=[preg3], tag="g_zbc")
            c.copy(rcz.sl(0, 256), pt3.ap()[:, 0:256], reads=[preg3],
                   writes=[rcz.reg()], tag="g_zbcev", engine="scalar")
            c.dma("sync", xch.v3(NT, 256),
                  xin4[:, :, b, mc * 256:(mc + 1) * 256],
                  reads=[(XI,)], writes=[xch.reg()], tag="g_ldx1")
            for dM in range(NT):
                c.tt(otmp.sl(0, 256), pout.ap()[:, dM * 256:(dM + 1) * 256],
                     rcz.sl(0, 256), OP.mult,
                     reads=[("g_pout",), rcz.reg()], writes=[otmp.reg()], tag="g_oev1")
                sl = xch.v3(NT, 256)[:, dM, :]
                c.tt(sl, otmp.sl(0, 256), sl, OP.add,
                     reads=[otmp.reg(), xch.reg()], writes=[xch.reg()], tag="g_oev2")
            c.dma("sync", xout4[:, :, b, mc * 256:(mc + 1) * 256], xch.v3(NT, 256),
                  reads=[xch.reg()], writes=[(XO,)], tag="g_stx2")


def build_prog_l0_ffn_a(D=1024, T=2048, B=2, NP=2048):
    nc = bass.Bass("TRN2", target_bir_lowering=False, debug=False)
    c = Ctx(nc, D=D, T=T, B=B)
    c.dram = {}
    c.NP = NP
    xin = _decl(nc, c, "xin", [B, D, T])
    _decl(nc, c, "L0_ffn_phi", [D, NP])
    e_out = _decl(nc, c, "e_out", [B, T, NP], kind="ExternalOutput", dtype=BF16)
    s_out = _decl(nc, c, "s_out", [T, NP], kind="ExternalOutput")
    xn_out = _decl(nc, c, "xn_out", [B, D, T], kind="ExternalOutput", dtype=BF16)
    setup_common(c)
    build_l0_ffn_a(c, xin, e_out, s_out, xn_out)
    c.p.fence()
    c.p.emit()
    global _LAST_PROG
    _LAST_PROG = c.p
    return nc


def build_prog_l0_ffn_b(D=1024, T=2048, B=2, NP=2048, n_parts=8):
    nc = bass.Bass("TRN2", target_bir_lowering=False, debug=False)
    c = Ctx(nc, D=D, T=T, B=B)
    c.dram = {}
    c.NP = NP
    NN = NP // 32
    xin = _decl(nc, c, "xin", [B, D, T])
    e_in = _decl(nc, c, "e_in", [B, T, NP], dtype=BF16)
    xn_in = _decl(nc, c, "xn_in", [B, D, T], dtype=BF16)
    s_all = _decl(nc, c, "s_all", [n_parts, T, NP])
    xout = _decl(nc, c, "xout", [B, D, T], kind="ExternalOutput")
    _decl(nc, c, "L0_ffn_W", [NN, D, D])
    _decl(nc, c, "bf_L0_ffn_W", [NN, D, D], kind="Internal", dtype=BF16)
    _decl(nc, c, "L0_ffn_bbc", [NN, 32, D])
    _decl(nc, c, "g_sred", [T, NP], kind="Internal")
    _decl(nc, c, "g_D", [B, T, NP], kind="Internal", dtype=BF16)
    setup_common(c, npsum=3)
    build_l0_ffn_b(c, xin, e_in, xn_in, s_all, xout, n_parts=n_parts)
    c.p.fence()
    c.p.emit()
    global _LAST_PROG
    _LAST_PROG = c.p
    return nc


def host_l0_ffn(I, D=1024):
    phi = np.asarray(I["L0_ffn_phi"])
    NN = phi.shape[1]
    m = {
        "L0_ffn_phi": np.ascontiguousarray(phi.reshape(D, -1)),
        "L0_ffn_W": I["L0_ffn_W"],
        "L0_ffn_bbc": np.ascontiguousarray(
            np.broadcast_to(np.asarray(I["L0_ffn_b"])[:, None, :], (NN, 32, D))),
    }
    return m



_PROGS = {}


def _get_prog(name, builder):
    if name not in _PROGS:
        _PROGS[name] = builder()
    return _PROGS[name]


def kernel5(**I):
    D, T, NP = 1024, 2048, 2048
    BT, BPC = 16, 2
    x = np.asarray(I["L0_attn_x"], np.float32)

    xs = [np.ascontiguousarray(x[c * BPC:(c + 1) * BPC].transpose(0, 2, 1))
          for c in range(NCORES)]

    nc1 = _get_prog("l0a", lambda: build_prog_l0_attn(D=D, T=T, B=BPC))
    w1 = host_l0_attn(I, D=D)
    maps = [dict(w1, xin=xs[c]) for c in range(NCORES)]
    r1 = run_bass_kernel_spmd(nc1, maps, core_ids=list(range(NCORES)))
    x1 = [r["xout"] for r in r1.results]

    nc2 = _get_prog("l0fa", lambda: build_prog_l0_ffn_a(D=D, T=T, B=BPC, NP=NP))
    wf = host_l0_ffn(I, D=D)
    maps = [{"xin": x1[c], "L0_ffn_phi": wf["L0_ffn_phi"]} for c in range(NCORES)]
    r2 = run_bass_kernel_spmd(nc2, maps, core_ids=list(range(NCORES)))
    S_all = np.stack([r["s_out"] for r in r2.results])

    nc3 = _get_prog("l0fb", lambda: build_prog_l0_ffn_b(D=D, T=T, B=BPC, NP=NP,
                                                        n_parts=NCORES))
    maps = [{"xin": x1[c], "e_in": r2.results[c]["e_out"],
             "xn_in": r2.results[c]["xn_out"], "s_all": S_all,
             "L0_ffn_W": wf["L0_ffn_W"], "L0_ffn_bbc": wf["L0_ffn_bbc"]}
            for c in range(NCORES)]
    r3 = run_bass_kernel_spmd(nc3, maps, core_ids=list(range(NCORES)))
    x2 = [r["xout"] for r in r3.results]

    nc4 = _get_prog("l1a", lambda: build_prog_l1_attn(D=D, T=T, B=BPC))
    w4 = host_l1_attn(I, D=D)
    maps = [dict(w4, xin=x2[c]) for c in range(NCORES)]
    r4 = run_bass_kernel_spmd(nc4, maps, core_ids=list(range(NCORES)))
    x3 = [r["xout"] for r in r4.results]

    nc5 = _get_prog("l1f", lambda: build_prog_l1_ffn(D=D, T=T, B=BPC))
    w5 = host_l1_ffn(I, D=D)
    maps = [dict(w5, xin=x3[c]) for c in range(NCORES)]
    r5 = run_bass_kernel_spmd(nc5, maps, core_ids=list(range(NCORES)))

    out = np.concatenate([r["xout"].transpose(0, 2, 1) for r in r5.results], axis=0)
    return np.ascontiguousarray(out.astype(np.float32))



def build_prog_launchA(D=1024, T=2048, B=2, NP=2048):
    nc = bass.Bass("TRN2", target_bir_lowering=False, debug=False)
    c = Ctx(nc, D=D, T=T, B=B)
    c.dram = {}
    c.NP = NP
    NT = c.NT
    HID = 4 * D
    xin = _decl(nc, c, "xin", [B, D, T])
    x1 = _decl(nc, c, "x1", [B, D, T], kind="ExternalOutput")
    for nm in ("WkT", "WvT", "WrT", "WoT", "fWrT"):
        _decl(nc, c, "L0_attn_" + nm, [D, D])
        _decl(nc, c, "bf_L0_attn_" + nm, [D, D], kind="Internal", dtype=BF16)
    _decl(nc, c, "L0_attn_fWkT", [D, HID])
    _decl(nc, c, "bf_L0_attn_fWkT", [D, HID], kind="Internal", dtype=BF16)
    _decl(nc, c, "L0_attn_fWvT", [HID, D])
    _decl(nc, c, "bf_L0_attn_fWvT", [HID, D], kind="Internal", dtype=BF16)
    for nm in ("ln1_g", "ln1_b", "ln2_g", "ln2_b", "tmk", "tmv", "tmr",
               "f_tmk", "f_tmr", "time_decay", "time_first"):
        _decl(nc, c, "L0_attn_" + nm, [128, NT])
    _decl(nc, c, "L0_ffn_phi", [D, NP])
    e_out = _decl(nc, c, "e_out", [B, T, NP], kind="ExternalOutput", dtype=BF16)
    s_out = _decl(nc, c, "s_out", [T, NP], kind="ExternalOutput")
    xn_out = _decl(nc, c, "xn_out", [B, D, T], kind="ExternalOutput", dtype=BF16)
    setup_common(c, npsum=0)
    c.push_scope(6)
    build_l0_attn(c, xin, x1, None)
    c.pop_scope()
    c.push_scope(6)
    build_l0_ffn_a(c, x1, e_out, s_out, xn_out)
    c.pop_scope()
    c.p.fence()
    c.p.emit()
    global _LAST_PROG
    _LAST_PROG = c.p
    return nc


def build_prog_launchB(D=1024, T=2048, B=2, NP=2048, n_parts=8):
    nc = bass.Bass("TRN2", target_bir_lowering=False, debug=False)
    c = Ctx(nc, D=D, T=T, B=B)
    c.dram = {}
    c.NP = NP
    NN = NP // 32
    R = 256
    F = 4 * D
    x1 = _decl(nc, c, "x1", [B, D, T])
    e_in = _decl(nc, c, "e_in", [B, T, NP], dtype=BF16)
    xn_in = _decl(nc, c, "xn_in", [B, D, T], dtype=BF16)
    s_all = _decl(nc, c, "s_all", [n_parts, T, NP])
    x2 = _decl(nc, c, "x2", [B, D, T], kind="Internal")
    x3 = _decl(nc, c, "x3", [B, D, T], kind="Internal")
    xout = _decl(nc, c, "xout", [B, D, T], kind="ExternalOutput")
    _decl(nc, c, "L0_ffn_W", [NN, D, D])
    _decl(nc, c, "bf_L0_ffn_W", [NN, D, D], kind="Internal", dtype=BF16)
    _decl(nc, c, "L0_ffn_bbc", [NN, 32, D])
    _decl(nc, c, "g_sred", [T, NP], kind="Internal")
    _decl(nc, c, "g_D", [B, T, NP], kind="Internal", dtype=BF16)
    for nm in ("Wq", "Wk"):
        _decl(nc, c, "L1_attn_" + nm, [D, D])
    _decl(nc, c, "L1_attn_WoT", [D, D])
    for nm in ("V_b", "W_b", "V_t", "W_t", "X_t"):
        _decl(nc, c, "L1_attn_" + nm, [D, R])
    _decl(nc, c, "L1_attn_U_bT", [R, D])
    _decl(nc, c, "L1_attn_U_tT", [R, D])
    _decl(nc, c, "L1_attn_bias_b", [128, D // 128])
    _decl(nc, c, "L1_attn_bias_t", [128, D // 128])
    _decl(nc, c, "L1_attn_alpha", [1, 1])
    _decl(nc, c, "L1_ffn_w_red", [D, 16])
    _decl(nc, c, "L1_ffn_wg", [8, 16])
    _decl(nc, c, "L1_ffn_w1", [8, D, F])
    _decl(nc, c, "bf_L1_ffn_w1", [8, D, F], kind="Internal", dtype=BF16)
    _decl(nc, c, "L1_ffn_b1c", [128, 8 * (F // 128)])
    _decl(nc, c, "L1_ffn_w2", [8, F, D])
    _decl(nc, c, "bf_L1_ffn_w2", [8, F, D], kind="Internal", dtype=BF16)
    _decl(nc, c, "L1_ffn_b2", [8, D])
    _decl(nc, c, "m_scr16", [1, 16], kind="Internal")
    _decl(nc, c, "m_scr8x16", [8, 16], kind="Internal")
    setup_common(c, npsum=0)
    c.push_scope(3)
    build_l0_ffn_b(c, x1, e_in, xn_in, s_all, x2, n_parts=n_parts)
    c.pop_scope()
    c.push_scope(6)
    build_l1_attn(c, x2, x3)
    c.pop_scope()
    c.push_scope(6)
    build_l1_ffn(c, x3, xout)
    c.pop_scope()
    c.p.fence()
    c.p.emit()
    global _LAST_PROG
    _LAST_PROG = c.p
    return nc


def kernel(**I):
    D, T, NP = 1024, 2048, 2048
    BPC = 2
    x = np.asarray(I["L0_attn_x"], np.float32)
    xs = [np.ascontiguousarray(x[c * BPC:(c + 1) * BPC].transpose(0, 2, 1))
          for c in range(NCORES)]
    wa = host_l0_attn(I, D=D)
    wf = host_l0_ffn(I, D=D)
    w4 = host_l1_attn(I, D=D)
    w5 = host_l1_ffn(I, D=D)

    ncA = _get_prog("A", lambda: build_prog_launchA(D=D, T=T, B=BPC, NP=NP))
    maps = [dict(wa, xin=xs[c], L0_ffn_phi=wf["L0_ffn_phi"]) for c in range(NCORES)]
    rA = run_bass_kernel_spmd(ncA, maps, core_ids=list(range(NCORES)))
    S_all = np.stack([r["s_out"] for r in rA.results])

    ncB = _get_prog("B", lambda: build_prog_launchB(D=D, T=T, B=BPC, NP=NP,
                                                    n_parts=NCORES))
    maps = []
    for c in range(NCORES):
        m = dict(w4)
        m.update(w5)
        m.update(L0_ffn_W=wf["L0_ffn_W"], L0_ffn_bbc=wf["L0_ffn_bbc"],
                 x1=rA.results[c]["x1"], e_in=rA.results[c]["e_out"],
                 xn_in=rA.results[c]["xn_out"], s_all=S_all)
        maps.append(m)
    rB = run_bass_kernel_spmd(ncB, maps, core_ids=list(range(NCORES)))
    out = np.concatenate([r["xout"].transpose(0, 2, 1) for r in rB.results], axis=0)
    return np.ascontiguousarray(out.astype(np.float32))
